# version 05
# baseline (speedup 1.0000x reference)
import math
import numpy as np
from contextlib import ExitStack

import concourse.bass as bass
import concourse.mybir as mybir

F32 = mybir.dt.float32
BF16 = mybir.dt.bfloat16
I32 = mybir.dt.int32
AF = mybir.ActivationFunctionType
ALU = mybir.AluOpType
AX = mybir.AxisListType

ENGines = ("tensor", "vector", "scalar", "gpsimd", "sync")


class Sem:
    __slots__ = ("h", "n", "name")

    def __init__(self, h, name):
        self.h = h
        self.n = 0
        self.name = name


class Buf:
    __slots__ = ("name", "t", "writer", "readers")

    def __init__(self, name, t=None):
        self.name = name
        self.t = t
        self.writer = None
        self.readers = []

    def ap(self):
        return self.t[:]


class Eng:
    __slots__ = ("name", "ctx", "sem", "waits", "thunks", "is_dma")

    def __init__(self, name, ctx, sem, is_dma=False):
        self.name = name
        self.ctx = ctx
        self.sem = sem
        self.waits = {}
        self.thunks = []
        self.is_dma = is_dma


class Ctx:
    def __init__(self, nc, stack):
        self.nc = nc
        self.stack = stack
        self.sems = []
        self.engs = {}
        for name in ENGines:
            s = self._sem(f"sem_{name}")
            self.engs[name] = Eng(name, self, s, is_dma=(name == "sync"))
        self.NDS = 8
        self.dma_sems = {q: [self._sem(f"dma_{q}{i}") for i in range(self.NDS)]
                         for q in ("sync", "gpsimd", "scalar")}
        self.dma_rot = {q: 0 for q in self.dma_sems}
        self._buf_id = 0
        self._cvals = self.sbuf("ctx_cvals", (128, 16), F32)
        self._cmap = {}

    def constcol(self, val, lo=0, hi=128):
        if val not in self._cmap:
            assert len(self._cmap) < 16
            col = len(self._cmap)
            self._cmap[val] = col
            self.step("gpsimd", lambda e: e.memset(self._cvals.t[:, col:col + 1], float(val)),
                      writes=[self._cvals])
        col = self._cmap[val]
        return self._cvals.t[lo:hi, col:col + 1]

    def _sem(self, name):
        h = self.stack.enter_context(self.nc.semaphore(name))
        s = Sem(h, name)
        self.sems.append(s)
        return s

    def sbuf(self, name, shape, dtype):
        t = self.stack.enter_context(self.nc.sbuf_tensor(name, list(shape), dtype))
        return Buf(name, t)

    def psum(self, name, shape, dtype):
        t = self.stack.enter_context(self.nc.psum_tensor(name, list(shape), dtype))
        return Buf(name, t)

    def dram(self, name, shape, dtype, kind="Internal"):
        t = self.nc.dram_tensor(name, list(shape), dtype, kind=kind)
        return Buf(name, t)

    def alias(self, name):
        return Buf(name, None)

    SAFE_SELF = True

    def _wait(self, eng, sem, val):
        if val <= 0:
            return
        if sem is eng.sem and not self.SAFE_SELF:
            return
        if eng.waits.get(sem.name, 0) >= val:
            return
        eng.waits[sem.name] = val
        e = self.engs
        eng.thunks.append(("wait", sem.h, val))

    def step(self, eng_name, emit, reads=(), writes=(), sem=None, inc=1, tag=""):
        eng = self.engs[eng_name]
        csem = sem if sem is not None else eng.sem
        for b in reads:
            if b.writer is not None:
                self._wait(eng, b.writer[0], b.writer[1])
        for b in writes:
            if b.writer is not None:
                self._wait(eng, b.writer[0], b.writer[1])
            for (s, v) in b.readers:
                self._wait(eng, s, v)
        eng.thunks.append(("emit", emit, csem.h, inc, tag))
        csem.n += inc
        wtoken = (csem, csem.n)
        for b in writes:
            b.writer = wtoken
            b.readers = []
        for b in reads:
            b.readers.append(wtoken)
        return wtoken

    def dma(self, queue, out, in_, reads=(), writes=(), tag="", emit=None, **kw):
        sems = self.dma_sems[queue]
        i = self.dma_rot[queue]
        self.dma_rot[queue] = (i + 1) % self.NDS
        sem = sems[i]
        if sem.n > 0:
            self._wait(self.engs[queue], sem, sem.n)
        if emit is None:
            emit = lambda e: e.dma_start(out=out, in_=in_, **kw)
        return self.step(queue, emit, reads=reads, writes=writes, sem=sem, inc=16, tag=tag)

    def dma_final_waits(self):
        return [(s, s.n) for q in self.dma_sems for s in self.dma_sems[q] if s.n > 0]

    def barrier_all(self, eng_names=ENGines):
        snap = [(s, s.n) for s in self.sems]
        for en in eng_names:
            eng = self.engs[en]
            for s, v in snap:
                self._wait(eng, s, v)

    def finalize(self, final_waits=()):
        nc = self.nc
        block = self.stack.enter_context(nc.Block())
        for name in ENGines:
            eng = self.engs[name]
            thunks = eng.thunks

            def body(e, thunks=thunks, name=name):
                for th in thunks:
                    if th[0] == "wait":
                        e.wait_ge(th[1], th[2])
                    else:
                        _, emit, semh, inc, tag = th
                        try:
                            instr = emit(e)
                        except Exception as ex:
                            raise RuntimeError(f"emit failed at tag={tag!r} eng={name}: {ex}") from ex
                        if inc:
                            instr.then_inc(semh, inc)
            getattr(block, name)(body)

        if final_waits:
            def tail(e):
                for s, v in final_waits:
                    e.wait_ge(s.h, v)
            block.sync(tail)


def ceil_div(a, b):
    return (a + b - 1) // b



def emit_identity(ctx, ident_buf, diff_buf, n=128):
    ctx.step("gpsimd", lambda e: e.iota(
        diff_buf.t[:], pattern=[[1, n]], base=0, channel_multiplier=-1,
        allow_small_or_imprecise_dtypes=True), writes=[diff_buf])
    ctx.step("gpsimd", lambda e: e.tensor_scalar(
        ident_buf.t[:], diff_buf.t[:], 0.0, None, op0=ALU.is_equal),
        reads=[diff_buf], writes=[ident_buf])

"""GLA block emitter (one sequence b at a time), shared by probe and kernel.

Layouts (per b, T tokens, H=16 heads, dk=dv=64, D=1024):
  h input: DRAM FM tiled [8, 128, T] fp32 (d-chunk, partition, token)
  weights Wq,Wk,Wv,Wg: SBUF bf16, shape (128, 8, 1024): [d-part, d-chunk, feature]
  output o: FM strips (128, 8, STRIP) bf16 handed to a callback
            (partition = v-dim within 128-feature chunk, block j = feature chunk)
State S: SBUF (128, 8, 64) fp32; pair tile j: partitions 0-63 = head 2j k-dims,
64-127 = head 2j+1; free = v-dims.
"""
import numpy as np

STAGE = 99

H = 16
DK = 64
D = 1024
CH = 64
STRIP = 512
NT = STRIP // 128


def make_gla_consts(ctx):
    c = {}
    c["diff"] = ctx.sbuf("gla_diff", (128, 128), F32)
    c["triu128s"] = ctx.sbuf("gla_triu128s", (128, 128), BF16)
    c["ident128"] = ctx.sbuf("gla_ident128", (128, 128), BF16)
    c["triu2"] = ctx.sbuf("gla_triu2", (128, CH), F32)
    c["ident2"] = ctx.sbuf("gla_ident2", (128, CH), BF16)
    c["mask2"] = ctx.sbuf("gla_mask2", (128, CH), BF16)
    c["ones2"] = ctx.sbuf("gla_ones2", (128, 1), F32)
    def emit_iota(e):
        return e.iota(c["diff"].t[:, :], pattern=[[1, 128]], base=0,
                      channel_multiplier=-1, allow_small_or_imprecise_dtypes=True)
    ctx.step("gpsimd", emit_iota, writes=[c["diff"]])
    ctx.step("gpsimd", lambda e: e.tensor_scalar(
        c["triu128s"].t[:], c["diff"].t[:, :], 0.5, None, op0=ALU.is_gt),
        reads=[c["diff"]], writes=[c["triu128s"]])
    ctx.step("gpsimd", lambda e: e.tensor_scalar(
        c["ident128"].t[:], c["diff"].t[:, :], 0.0, None, op0=ALU.is_equal),
        reads=[c["diff"]], writes=[c["ident128"]])
    ctx.step("gpsimd", lambda e: e.tensor_scalar_add(
        c["diff"].t[64:128, 0:CH], c["diff"].t[64:128, 0:CH], 64.0),
        reads=[c["diff"]], writes=[c["diff"]])
    ctx.step("gpsimd", lambda e: e.tensor_scalar(
        c["triu2"].t[:], c["diff"].t[:, 0:CH], -0.5, None, op0=ALU.is_gt),
        reads=[c["diff"]], writes=[c["triu2"]])
    ctx.step("gpsimd", lambda e: e.tensor_scalar(
        c["mask2"].t[:], c["diff"].t[:, 0:CH], -0.5, None, op0=ALU.is_gt),
        reads=[c["diff"]], writes=[c["mask2"]])
    ctx.step("gpsimd", lambda e: e.tensor_scalar(
        c["ident2"].t[:], c["diff"].t[:, 0:CH], 0.0, None, op0=ALU.is_equal),
        reads=[c["diff"]], writes=[c["ident2"]])
    ctx.step("gpsimd", lambda e: e.memset(c["ones2"].t[:], 1.0), writes=[c["ones2"]])
    return c


def alloc_gla_bufs(ctx, nstrip_resident=1):
    b = {}
    b["hbf"] = ctx.sbuf("gla_hbf", (128, 8, STRIP), BF16)
    b["qt"] = ctx.sbuf("gla_qt", (128, NT, D), BF16)
    b["kt"] = ctx.sbuf("gla_kt", (128, NT, D), BF16)
    b["vt"] = ctx.sbuf("gla_vt", (128, NT, D), BF16)
    b["lsz"] = ctx.sbuf("gla_lsz", (128, NT, D), F32)
    b["sq"] = ctx.sbuf("gla_sq", (128, D), BF16)
    b["nrm"] = ctx.sbuf("gla_nrm", (128, 2 * H), F32)
    b["qtld"] = ctx.sbuf("gla_qtld", (128, D), BF16)
    b["ktld"] = ctx.sbuf("gla_ktld", (128, D), BF16)
    b["Ep"] = ctx.sbuf("gla_Ep", (128, D), BF16)
    b["Em"] = ctx.sbuf("gla_Em", (128, D), BF16)
    b["qkf"] = ctx.sbuf("gla_qkf", (128, 16, CH), BF16)
    b["dec"] = ctx.sbuf("gla_dec", (128, 8), F32)
    b["S"] = ctx.sbuf("gla_S", (128, 8, DK), F32)
    b["Sb"] = ctx.sbuf("gla_Sb", (128, 8, DK), BF16)
    b["t1"] = ctx.sbuf("gla_t1", (128, 8, DK), F32)
    b["pp"] = ctx.sbuf("gla_pp", (128, H * CH), BF16)
    b["ofm"] = ctx.sbuf("gla_ofm", (128, 8, STRIP), BF16)
    b["oext"] = ctx.sbuf("gla_oext", (128, 512), BF16)
    b["pA1"] = ctx.psum("gla_pA1", (128, 1024), F32)
    b["pA2"] = ctx.psum("gla_pA2", (128, 1024), F32)
    b["pT"] = ctx.psum("gla_pT", (128, 1024), BF16)
    b["pDec"] = ctx.psum("gla_pDec", (128, 512), F32)
    b["pO"] = ctx.psum("gla_pO", (128, 512), F32)
    b["pU"] = ctx.psum("gla_pU", (128, 512), F32)
    return b


def emit_gla_b(ctx, consts, b, W, T, load_h_strip, store_o_strip):
    nstrip = T // STRIP
    ctx.step("vector", lambda e: e.memset(b["S"].t[:], 0.0), writes=[b["S"]])
    ctx.step("vector", lambda e: e.memset(b["Sb"].t[:], 0.0), writes=[b["Sb"]])

    for si in range(nstrip):
        load_h_strip(si)
        for tt in range(NT):
            for pas, names in enumerate((("q", "k"), ("v", "g"))):
                def emit_mm(e, tt=tt, names=names):
                    last = None
                    for dc in range(8):
                        lhsT = b["hbf"].t[:, dc, tt * 128:(tt + 1) * 128]
                        for wi, wn in enumerate(names):
                            dst = b["pA1"] if wi == 0 else b["pA2"]
                            for half in range(2):
                                last = e.matmul(
                                    dst.t[:, half * 512:(half + 1) * 512],
                                    lhsT,
                                    W[wn].t[:, dc, half * 512:(half + 1) * 512],
                                    start=(dc == 0), stop=(dc == 7))
                    return last
                ctx.step("tensor", emit_mm,
                         reads=[b["hbf"], W[names[0]], W[names[1]]],
                         writes=[b["pA1"], b["pA2"]],
                         tag=f"proj{si}.{tt}.{pas}")
                if pas == 0:
                    for wi, (src, dst) in enumerate((("q", "qt"), ("k", "kt"))):
                        ps = b["pA1"] if wi == 0 else b["pA2"]
                        ctx.step("scalar", lambda e, ps=ps: e.activation(
                            out=b["sq"].t[:], in_=ps.t[:, 0:1024],
                            func=AF.Square),
                            reads=[ps], writes=[b["sq"]])
                        ctx.step("vector", lambda e, wi=wi: e.tensor_reduce(
                            b["nrm"].t[:, wi * H:(wi + 1) * H].rearrange("p h -> p h ()"),
                            b["sq"].t[:].rearrange("p (h d) -> p h d", h=H),
                            axis=AX.X, op=ALU.add),
                            reads=[b["sq"]], writes=[b["nrm"]])
                    ctx.step("vector", lambda e: e.tensor_scalar_max(
                        b["nrm"].t[:], b["nrm"].t[:], 1e-24),
                        reads=[b["nrm"]], writes=[b["nrm"]])
                    ctx.step("scalar", lambda e: e.activation(
                        out=b["nrm"].t[:], in_=b["nrm"].t[:], func=AF.Ln),
                        reads=[b["nrm"]], writes=[b["nrm"]])
                    ctx.step("scalar", lambda e: e.activation(
                        out=b["nrm"].t[:], in_=b["nrm"].t[:], func=AF.Exp, scale=-0.5),
                        reads=[b["nrm"]], writes=[b["nrm"]])
                    for wi, dst in enumerate(("qt", "kt")):
                        ps = b["pA1"] if wi == 0 else b["pA2"]
                        ctx.step("vector", lambda e, ps=ps, wi=wi, dst=dst, tt=tt: e.tensor_tensor(
                            b[dst].t[:, tt, :].rearrange("p (h d) -> p h d", h=H),
                            ps.t[:, 0:1024].rearrange("p (h d) -> p h d", h=H),
                            b["nrm"].t[:, wi * H:(wi + 1) * H].rearrange("p h -> p h ()").broadcast_to([128, H, DK]),
                            op=ALU.mult),
                            reads=[ps, b["nrm"]], writes=[b[dst]])
                else:
                    ctx.step("scalar", lambda e, tt=tt: e.activation(
                        out=b["vt"].t[:, tt, :], in_=b["pA1"].t[:, 0:1024], func=AF.Copy),
                        reads=[b["pA1"]], writes=[b["vt"]])
                    ctx.step("scalar", lambda e, tt=tt: e.activation(
                        out=b["lsz"].t[:, tt, :], in_=b["pA2"].t[:, 0:1024], func=AF.Sigmoid),
                        reads=[b["pA2"]], writes=[b["lsz"]])
                    ctx.step("scalar", lambda e, tt=tt: e.activation(
                        out=b["lsz"].t[:, tt, :], in_=b["lsz"].t[:, tt, :], func=AF.Ln),
                        reads=[b["lsz"]], writes=[b["lsz"]])

        for ci in range(STRIP // CH if STAGE >= 1 else 0):
            tt = (ci * CH) // 128
            hf = (ci * CH) % 128
            emit_gla_chunk(ctx, consts, b, si, ci, tt, hf)

        store_o_strip(si)


def emit_gla_chunk(ctx, consts, b, si, ci, tt, hf):
    tslice = slice(hf, hf + CH)
    def mm_la(e):
        last = None
        for half in range(2):
            last = e.matmul(b["pA1"].t[tslice, half * 512:(half + 1) * 512],
                            consts["triu2"].t[tslice, :],
                            b["lsz"].t[tslice, tt, half * 512:(half + 1) * 512],
                            start=True, stop=True,
                            tile_position=(hf, hf))
        return last
    ctx.step("tensor", mm_la, reads=[consts["triu2"], b["lsz"]], writes=[b["pA1"]],
             tag=f"la{si}.{ci}")
    ctx.step("scalar", lambda e: e.activation(
        out=b["Ep"].t[tslice, :], in_=b["pA1"].t[tslice, 0:1024], func=AF.Exp),
        reads=[b["pA1"]], writes=[b["Ep"]])
    ctx.step("scalar", lambda e: e.activation(
        out=b["Em"].t[tslice, :], in_=b["pA1"].t[tslice, 0:1024], func=AF.Exp, scale=-1.0),
        reads=[b["pA1"]], writes=[b["Em"]])
    def mm_dec(e):
        last = None
        for j in range(8):
            last = e.matmul(b["pDec"].t[:, j:j + 1],
                            b["lsz"].t[tslice, tt, j * 128:(j + 1) * 128],
                            consts["ones2"].t[tslice, :],
                            start=True, stop=True,
                            tile_position=(hf, 0))
        return last
    ctx.step("tensor", mm_dec, reads=[b["lsz"], consts["ones2"]], writes=[b["pDec"]],
             tag=f"dec{si}.{ci}")
    ctx.step("scalar", lambda e: e.activation(
        out=b["dec"].t[:], in_=b["pDec"].t[:, 0:8], func=AF.Exp),
        reads=[b["pDec"]], writes=[b["dec"]])
    ctx.step("vector", lambda e: e.tensor_tensor(
        b["qtld"].t[tslice, :], b["qt"].t[tslice, tt, :], b["Ep"].t[tslice, :], op=ALU.mult),
        reads=[b["qt"], b["Ep"]], writes=[b["qtld"]])
    ctx.step("vector", lambda e: e.tensor_tensor(
        b["ktld"].t[tslice, :], b["kt"].t[tslice, tt, :], b["Em"].t[tslice, :], op=ALU.mult),
        reads=[b["kt"], b["Em"]], writes=[b["ktld"]])
    if STAGE < 2:
        return
    def mm_tr(e):
        last = None
        for src, base in ((b["qtld"], 0), (b["ktld"], 8)):
            for j in range(8):
                last = e.transpose(
                    b["pT"].t[:, (base + j) * CH:(base + j + 1) * CH],
                    src.t[tslice, j * 128:(j + 1) * 128],
                    consts["ident2"].t[tslice, :])
        return last
    ctx.step("tensor", mm_tr, reads=[b["qtld"], b["ktld"], consts["ident2"]],
             writes=[b["pT"]], tag=f"tr{si}.{ci}")
    ctx.step("vector", lambda e: e.tensor_copy(b["qkf"].t[:], b["pT"].t[:]),
             reads=[b["pT"]], writes=[b["qkf"]])
    if STAGE < 3:
        return
    def ppoff(h):
        return (h % 2) * 512 + (h // 2) * 64
    def mm_pp(e):
        last = None
        for h in range(H):
            j, hb = h // 2, (h % 2) * 64
            last = e.matmul(
                b["pA1"].t[tslice, ppoff(h):ppoff(h) + 64],
                b["qkf"].t[hb:hb + 64, 8 + j, :],
                b["qkf"].t[hb:hb + 64, j, :],
                start=True, stop=True, tile_position=(hb, hf))
        return last
    ctx.step("tensor", mm_pp, reads=[b["qkf"]], writes=[b["pA1"]], tag=f"pp{si}.{ci}")
    if STAGE < 3.7:
        return
    ctx.step("vector", lambda e: e.tensor_tensor(
        b["pp"].t[tslice, :].rearrange("p (g t) -> p g t", g=H),
        b["pA1"].t[tslice, 0:H * CH].rearrange("p (g t) -> p g t", g=H),
        consts["mask2"].t[tslice, :].rearrange("p t -> p () t").broadcast_to([CH, H, CH]),
        op=ALU.mult),
        reads=[b["pA1"], consts["mask2"]], writes=[b["pp"]])
    if STAGE < 4:
        return
    def mm_o(e):
        last = None
        for h in range(H):
            j, hb = h // 2, (h % 2) * 64
            last = e.matmul(
                b["pA2"].t[hb:hb + 64, j * 64:(j + 1) * 64],
                b["Sb"].t[hb:hb + 64, j, :],
                b["qkf"].t[hb:hb + 64, j, :],
                start=True, stop=True, tile_position=(hb, hb))
            last = e.matmul(
                b["pO"].t[hb:hb + 64, j * 64:(j + 1) * 64],
                b["vt"].t[tslice, tt, h * 64:(h + 1) * 64],
                b["pp"].t[tslice, ppoff(h):ppoff(h) + 64],
                start=True, stop=True, tile_position=(hf, hb))
        return last
    ctx.step("tensor", mm_o, reads=[b["Sb"], b["qkf"], b["vt"], b["pp"]],
             writes=[b["pO"], b["pA2"]], tag=f"o{si}.{ci}")
    if STAGE < 5:
        return
    def mm_u(e):
        last = None
        for h in range(H):
            j, hb = h // 2, (h % 2) * 64
            last = e.matmul(
                b["pU"].t[hb:hb + 64, j * 64:(j + 1) * 64],
                b["ktld"].t[tslice, h * 64:(h + 1) * 64],
                b["vt"].t[tslice, tt, h * 64:(h + 1) * 64],
                start=True, stop=True, tile_position=(hf, hb))
        return last
    ctx.step("tensor", mm_u, reads=[b["ktld"], b["vt"]], writes=[b["pU"]],
             tag=f"u{si}.{ci}")
    ctx.step("scalar", lambda e: e.activation(
        out=b["oext"].t[:], in_=b["pA2"].t[:, 0:512], func=AF.Copy),
        reads=[b["pA2"]], writes=[b["oext"]])
    ctx.step("vector", lambda e: e.tensor_tensor(
        b["ofm"].t[:, :, ci * CH:(ci + 1) * CH],
        b["pO"].t[:, 0:512].rearrange("p (j t) -> p j t", j=8),
        b["oext"].t[:].rearrange("p (j t) -> p j t", j=8),
        op=ALU.add),
        reads=[b["pO"], b["oext"]], writes=[b["ofm"]])
    ctx.step("vector", lambda e: e.tensor_tensor(
        b["t1"].t[:], b["S"].t[:],
        b["pU"].t[:, 0:512].rearrange("p (j v) -> p j v", j=8), op=ALU.add),
        reads=[b["S"], b["pU"]], writes=[b["t1"]])
    ctx.step("vector", lambda e: e.tensor_tensor(
        b["S"].t[:], b["t1"].t[:],
        b["dec"].t[:].rearrange("p j -> p j ()").broadcast_to([128, 8, DK]),
        op=ALU.mult),
        reads=[b["t1"], b["dec"]], writes=[b["S"]])
    ctx.step("vector", lambda e: e.tensor_copy(b["Sb"].t[:], b["S"].t[:]),
             reads=[b["S"]], writes=[b["Sb"]])


def gla_reference(h, Wq, Wk, Wv, Wg):
    T = h.shape[0]
    q = (h @ Wq).reshape(T, H, DK)
    k = (h @ Wk).reshape(T, H, DK)
    v = (h @ Wv).reshape(T, H, DK)
    alpha = 1.0 / (1.0 + np.exp(-(h @ Wg))).reshape(T, H, DK)
    def l2n(x):
        n = np.sqrt((x * x).sum(-1, keepdims=True))
        return x / np.maximum(n, 1e-12)
    q = l2n(q); k = l2n(k)
    S = np.zeros((H, DK, DK))
    o = np.zeros((T, H, DK))
    for t in range(T):
        S = alpha[t][:, :, None] * S + k[t][:, :, None] * v[t][:, None, :]
        o[t] = np.einsum("hkv,hk->hv", S, q[t])
    return o.reshape(T, H * DK)

"""L1 program: attn_a block (LN -> rmsnorm -> GLA -> rmsnorm -> gated FFN)
plus MoE routing (logits, top-8, weights, local cumsum positions, counts).

Per-core DRAM I/O (NB sequences of T=2048):
 in:  x_fm [8,128,NB*T] f32 (FM tiled); Wq,Wk,Wv,Wg,Wo [1024,1024];
      w1,w2 [1024,2736]; w3 [2736,1024]; g_attn,g_ffn [1,1024]; gw_t [1024,64]
 out: x1_fm [8,128,NB*T] f32; s1_out [2,NB*T] f32 (A1=rstd1, B1=-mu1*rstd1);
      w_out [128, NB*16, 64] f32 (TM; 0 for unselected);
      pos_out [128, NB*16, 64] f32 (local exclusive prefix count);
      cnt_out [1, 64] f32
 internal: ofm_d bf16; xl_d, xpl_d f32; xlg2_d bf16; rows_d [2,NB*T];
           logits_d [128, NB*16, 64] f32
"""
import numpy as np
from contextlib import ExitStack

NB = 4
T = 2048
NSTR = T // STRIP
DFF = 2736
NFC = (DFF + 127) // 128
NE = 64
CAP = 256
BIG = float(2 ** 40)


def fc_rows(fc):
    return 128 if fc < NFC - 1 else DFF - 128 * (NFC - 1)


def make_long(ctx):
    long = {}
    long["ones_row"] = ctx.sbuf("ones_row", (1, 128), BF16)
    long["ones_rowf"] = ctx.sbuf("ones_rowf", (1, 128), F32)
    long["ones_col"] = ctx.sbuf("ones_col", (128, 1), BF16)
    for n in ("ones_row", "ones_rowf", "ones_col"):
        ctx.step("gpsimd", lambda e, n=n: e.memset(long[n].t[:], 1.0), writes=[long[n]])
    return long


UPTO = 99


def emit_l1(ctx, dio, long=None, consts=None):
    if consts is None:
        consts = make_gla_consts(ctx)
    if long is None:
        long = make_long(ctx)
    emit_p1(ctx, consts, long, dio)
    if UPTO < 1:
        return long
    ctx.barrier_all()
    emit_p2(ctx, long, dio)
    if UPTO < 2:
        return long
    ctx.barrier_all()
    emit_p3(ctx, long, dio)
    return long


def rowmath_ln(ctx, sx, sx2, murow, rstd):
    (sxb, sxa), (sx2b, sx2a) = sx, sx2
    (mb, ma), (rb, ra) = murow, rstd
    ctx.step("vector", lambda e: e.tensor_scalar(ma, sxa, -1.0 / D, None, op0=ALU.mult),
             reads=[sxb], writes=[mb])
    ctx.step("vector", lambda e: e.tensor_tensor(ra, ma, ma, op=ALU.mult),
             reads=[mb], writes=[rb])
    ctx.step("vector", lambda e: e.scalar_tensor_tensor(ra, sx2a, 1.0 / D, ra,
                                                        op0=ALU.mult, op1=ALU.subtract),
             reads=[sx2b, rb], writes=[rb])
    bias5 = ctx.constcol(1e-5, 0, 1)
    ctx.step("scalar", lambda e: e.activation(out=ra, in_=ra, func=AF.Ln, bias=bias5),
             reads=[rb, ctx._cvals], writes=[rb])
    ctx.step("scalar", lambda e: e.activation(out=ra, in_=ra, func=AF.Exp, scale=-0.5),
             reads=[rb], writes=[rb])



def emit_p1(ctx, consts, long, dio):
    with ExitStack() as ph:
        sb = lambda n, s, d: Buf(n, ph.enter_context(ctx.nc.sbuf_tensor(n, list(s), d)))
        ps = lambda n, s, d: Buf(n, ph.enter_context(ctx.nc.psum_tensor(n, list(s), d)))
        ga = sb("p1_ga", (1, 1024), F32)
        ctx.dma("sync", ga.t[:], dio["g_attn"].t.ap()[:, :], writes=[ga])
        W = {}
        for wn, dn in (("q", "Wq"), ("k", "Wk"), ("v", "Wv"), ("g", "Wg")):
            W[wn] = sb(f"W{wn}s", (128, 8, 1024), BF16)
            for dc in range(8):
                ctx.dma("gpsimd", W[wn].t[:, dc, :], dio[dn].t.ap()[dc * 128:(dc + 1) * 128, :],
                        writes=[W[wn]], tag=f"ldW{wn}{dc}")
        b = {}
        b["hbf"] = sb("p1_xb", (128, 8, STRIP), BF16)
        b["sq"] = sb("p1_sq", (128, 8, STRIP), BF16)
        b["srows"] = sb("p1_srows", (1, 2 * T), F32)
        b["crow"] = sb("p1_crow", (1, T), F32)
        b["murow"] = sb("p1_murow", (1, T), F32)
        b["murb"] = sb("p1_murb", (1, T), BF16)
        b["rstd"] = sb("p1_rstd", (1, T), F32)
        b["ccol"] = sb("p1_ccol", (128, 16), F32)
        b["qt"] = sb("p1_qt", (128, NT, D), BF16)
        b["kt"] = sb("p1_kt", (128, NT, D), BF16)
        b["vt"] = sb("p1_vt", (128, NT, D), BF16)
        b["lsz"] = sb("p1_lsz", (128, NT, D), F32)
        b["nrm"] = sb("p1_nrm", (128, 2 * H), F32)
        b["sqn"] = sb("p1_sqn", (128, D), BF16)
        b["qtld"] = sb("p1_qtld", (128, D), BF16)
        b["ktld"] = sb("p1_ktld", (128, D), BF16)
        b["Ep"] = sb("p1_Ep", (128, D), BF16)
        b["Em"] = sb("p1_Em", (128, D), BF16)
        b["qkf"] = sb("p1_qkf", (128, 16, CH), BF16)
        b["dec"] = sb("p1_dec", (128, 8), F32)
        b["S"] = sb("p1_S", (128, 8, DK), F32)
        b["Sb"] = sb("p1_Sb", (128, 8, DK), BF16)
        b["t1"] = sb("p1_t1", (128, 8, DK), F32)
        b["pp"] = sb("p1_pp", (128, H * CH), BF16)
        b["oext"] = sb("p1_oext", (128, 512), BF16)
        b["ofm"] = sb("p1_ofm", (128, 8, STRIP), BF16)
        b["gcol"] = sb("p1_gcol", (128, 8), F32)
        b["pA1"] = ps("p1_pA1", (128, 1024), F32)
        b["pA2"] = ps("p1_pA2", (128, 1024), F32)
        b["pT"] = ps("p1_pT", (128, 1024), BF16)
        b["pDec"] = ps("p1_pDec", (128, 512), F32)
        b["pO"] = ps("p1_pO", (128, 512), F32)
        b["pU"] = ps("p1_pU", (128, 512), F32)

        def mmgc(e):
            last = None
            for dc in range(8):
                last = e.matmul(b["pDec"].t[:, dc:dc + 1],
                                ga.t[:, dc * 128:(dc + 1) * 128],
                                long["ones_rowf"].t[:, 0:1], start=True, stop=True)
            return last
        ctx.step("tensor", mmgc, reads=[ga, long["ones_rowf"]], writes=[b["pDec"]], tag="gacol")
        ctx.step("vector", lambda e: e.tensor_copy(b["gcol"].t[:], b["pDec"].t[:, 0:8]),
                 reads=[b["pDec"]], writes=[b["gcol"]])
        for wn in ("q", "k", "v", "g"):
            for dc in range(8):
                ctx.step("vector", lambda e, wn=wn, dc=dc: e.tensor_scalar(
                    W[wn].t[:, dc, :], W[wn].t[:, dc, :], b["gcol"].t[:, dc:dc + 1],
                    None, op0=ALU.mult),
                    reads=[W[wn], b["gcol"]], writes=[W[wn]])
        gwr = {}
        for wn in ("q", "k", "v", "g"):
            gwr[wn] = sb(f"gWr_{wn}", (1, 1024), BF16)
            gtmp = b["pA1"]
            def mmgw(e, wn=wn):
                last = None
                for half in range(2):
                    for dc in range(8):
                        last = e.matmul(gtmp.t[0:1, half * 512:(half + 1) * 512],
                                        long["ones_col"].t[:, 0:1],
                                        W[wn].t[:, dc, half * 512:(half + 1) * 512],
                                        start=(dc == 0), stop=(dc == 7))
                return last
            ctx.step("tensor", mmgw, reads=[long["ones_col"], W[wn]], writes=[gtmp], tag=f"gW{wn}")
            ctx.step("vector", lambda e, wn=wn: e.tensor_copy(gwr[wn].t[:], gtmp.t[0:1, 0:1024]),
                     reads=[gtmp], writes=[gwr[wn]])

        for bi in range(NB):
            emit_p1_b(ctx, consts, long, dio, W, gwr, b, bi)


def emit_p1_b(ctx, consts, long, dio, W, gwr, b, bi):
    for si in range(NSTR):
        col0 = bi * T + si * STRIP
        for dc in range(8):
            ctx.dma("gpsimd", b["hbf"].t[:, dc, :],
                    dio["x_fm"].t.ap()[dc, :, col0:col0 + STRIP],
                    reads=[dio["x_fm"]], writes=[b["hbf"]], tag=f"p1a_x{bi}.{si}.{dc}")
        ctx.step("scalar", lambda e: e.activation(
            out=b["sq"].t[:], in_=b["hbf"].t[:], func=AF.Square),
            reads=[b["hbf"]], writes=[b["sq"]])
        for which, src in (("x", b["hbf"]), ("x2", b["sq"])):
            def mmst(e, src=src):
                last = None
                for dc in range(8):
                    last = e.matmul(b["pO"].t[0:1, 0:STRIP], long["ones_col"].t[:, 0:1],
                                    src.t[:, dc, :], start=(dc == 0), stop=(dc == 7))
                return last
            ctx.step("tensor", mmst, reads=[src, long["ones_col"]], writes=[b["pO"]],
                     tag=f"p1a_st{bi}.{si}.{which}")
            off = 0 if which == "x" else T
            ctx.step("vector", lambda e, si=si, off=off: e.tensor_copy(
                b["srows"].t[0:1, off + si * STRIP:off + (si + 1) * STRIP], b["pO"].t[0:1, 0:STRIP]),
                reads=[b["pO"]], writes=[b["srows"]])
    rowmath_ln(ctx, (b["srows"], b["srows"].t[0:1, 0:T]), (b["srows"], b["srows"].t[0:1, T:2 * T]),
               (b["murow"], b["murow"].t[:]), (b["rstd"], b["rstd"].t[:]))
    ctx.step("vector", lambda e: e.tensor_copy(b["murb"].t[:], b["murow"].t[:]),
             reads=[b["murow"]], writes=[b["murb"]])
    ctx.step("vector", lambda e: e.tensor_tensor(b["crow"].t[:], b["rstd"].t[:], b["rstd"].t[:], op=ALU.mult),
             reads=[b["rstd"]], writes=[b["crow"]])
    ctx.step("vector", lambda e: e.tensor_scalar(b["crow"].t[:], b["crow"].t[:], -1e-5, 1.0,
                                                 op0=ALU.mult, op1=ALU.add),
             reads=[b["crow"]], writes=[b["crow"]])
    bias6 = ctx.constcol(1e-6, 0, 1)
    ctx.step("scalar", lambda e: e.activation(out=b["crow"].t[:], in_=b["crow"].t[:],
                                              func=AF.Ln, bias=bias6),
             reads=[b["crow"], ctx._cvals], writes=[b["crow"]])
    ctx.step("scalar", lambda e: e.activation(out=b["crow"].t[:], in_=b["crow"].t[:],
                                              func=AF.Exp, scale=-0.5),
             reads=[b["crow"]], writes=[b["crow"]])
    ctx.step("vector", lambda e: e.tensor_tensor(b["crow"].t[:], b["crow"].t[:], b["rstd"].t[:], op=ALU.mult),
             reads=[b["crow"], b["rstd"]], writes=[b["crow"]])
    ctx.dma("sync", dio["rows_d"].t.ap()[0:1, bi * T:(bi + 1) * T], b["rstd"].t[0:1, :],
            reads=[b["rstd"]], writes=[dio["rows_d"]], tag=f"strowA{bi}")
    ctx.step("vector", lambda e: e.tensor_tensor(b["srows"].t[0:1, 0:T], b["murow"].t[:],
                                                 b["rstd"].t[:], op=ALU.mult),
             reads=[b["murow"], b["rstd"]], writes=[b["srows"]])
    ctx.dma("sync", dio["rows_d"].t.ap()[1:2, bi * T:(bi + 1) * T], b["srows"].t[0:1, 0:T],
            reads=[b["srows"]], writes=[dio["rows_d"]], tag=f"strowB{bi}")
    def mmcc(e):
        last = None
        for ti in range(16):
            last = e.matmul(b["pDec"].t[:, ti:ti + 1],
                            b["crow"].t[0:1, ti * 128:(ti + 1) * 128],
                            long["ones_rowf"].t[:, 0:1], start=True, stop=True)
        return last
    ctx.step("tensor", mmcc, reads=[b["crow"], long["ones_rowf"]], writes=[b["pDec"]], tag=f"ccol{bi}")
    ctx.step("vector", lambda e: e.tensor_copy(b["ccol"].t[:], b["pDec"].t[:, 0:16]),
             reads=[b["pDec"]], writes=[b["ccol"]])

    ctx.step("vector", lambda e: e.memset(b["S"].t[:], 0.0), writes=[b["S"]])
    ctx.step("vector", lambda e: e.memset(b["Sb"].t[:], 0.0), writes=[b["Sb"]])
    for si in range(NSTR):
        col0 = bi * T + si * STRIP
        for dc in range(8):
            ctx.dma("gpsimd", b["hbf"].t[:, dc, :],
                    dio["x_fm"].t.ap()[dc, :, col0:col0 + STRIP],
                    reads=[dio["x_fm"]], writes=[b["hbf"]], tag=f"p1b_x{bi}.{si}.{dc}")
        for tt in range(NT):
            tglob = si * NT + tt
            for pas, names in enumerate((("q", "k"), ("v", "g"))):
                def emit_mm(e, tt=tt, names=names, tglob=tglob):
                    last = None
                    for dc in range(8):
                        lhsT = b["hbf"].t[:, dc, tt * 128:(tt + 1) * 128]
                        for wi, wn in enumerate(names):
                            dst = b["pA1"] if wi == 0 else b["pA2"]
                            for half in range(2):
                                last = e.matmul(
                                    dst.t[:, half * 512:(half + 1) * 512], lhsT,
                                    W[wn].t[:, dc, half * 512:(half + 1) * 512],
                                    start=(dc == 0), stop=False)
                    for wi, wn in enumerate(names):
                        dst = b["pA1"] if wi == 0 else b["pA2"]
                        for half in range(2):
                            last = e.matmul(
                                dst.t[:, half * 512:(half + 1) * 512],
                                b["murb"].t[0:1, tglob * 128:(tglob + 1) * 128],
                                gwr[wn].t[0:1, half * 512:(half + 1) * 512],
                                start=False, stop=True)
                    return last
                ctx.step("tensor", emit_mm,
                         reads=[b["hbf"], W[names[0]], W[names[1]], b["murb"],
                                gwr[names[0]], gwr[names[1]]],
                         writes=[b["pA1"], b["pA2"]], tag=f"proj{bi}.{si}.{tt}.{pas}")
                ccol_ap = b["ccol"].t[:, tglob:tglob + 1]
                if pas == 0:
                    for wi, dst in enumerate(("qt", "kt")):
                        psrc = b["pA1"] if wi == 0 else b["pA2"]
                        ctx.step("scalar", lambda e, psrc=psrc: e.activation(
                            out=b["sqn"].t[:], in_=psrc.t[:, 0:1024], func=AF.Square),
                            reads=[psrc], writes=[b["sqn"]])
                        ctx.step("vector", lambda e, wi=wi: e.tensor_reduce(
                            b["nrm"].t[:, wi * H:(wi + 1) * H].rearrange("p h -> p h ()"),
                            b["sqn"].t[:].rearrange("p (h d) -> p h d", h=H),
                            axis=AX.X, op=ALU.add),
                            reads=[b["sqn"]], writes=[b["nrm"]])
                    ctx.step("vector", lambda e: e.tensor_scalar_max(
                        b["nrm"].t[:], b["nrm"].t[:], 1e-24),
                        reads=[b["nrm"]], writes=[b["nrm"]])
                    ctx.step("scalar", lambda e: e.activation(
                        out=b["nrm"].t[:], in_=b["nrm"].t[:], func=AF.Ln),
                        reads=[b["nrm"]], writes=[b["nrm"]])
                    ctx.step("scalar", lambda e: e.activation(
                        out=b["nrm"].t[:], in_=b["nrm"].t[:], func=AF.Exp, scale=-0.5),
                        reads=[b["nrm"]], writes=[b["nrm"]])
                    for wi, dst in enumerate(("qt", "kt")):
                        psrc = b["pA1"] if wi == 0 else b["pA2"]
                        ctx.step("vector", lambda e, psrc=psrc, wi=wi, dst=dst, tt=tt: e.tensor_tensor(
                            b[dst].t[:, tt, :].rearrange("p (h d) -> p h d", h=H),
                            psrc.t[:, 0:1024].rearrange("p (h d) -> p h d", h=H),
                            b["nrm"].t[:, wi * H:(wi + 1) * H].rearrange("p h -> p h ()").broadcast_to([128, H, DK]),
                            op=ALU.mult),
                            reads=[psrc, b["nrm"]], writes=[b[dst]])
                else:
                    ctx.step("vector", lambda e, tt=tt, ccol_ap=ccol_ap: e.tensor_scalar(
                        b["vt"].t[:, tt, :], b["pA1"].t[:, 0:1024], ccol_ap, None, op0=ALU.mult),
                        reads=[b["pA1"], b["ccol"]], writes=[b["vt"]])
                    ctx.step("scalar", lambda e, tt=tt, ccol_ap=ccol_ap: e.activation(
                        out=b["lsz"].t[:, tt, :], in_=b["pA2"].t[:, 0:1024], func=AF.Sigmoid,
                        scale=ccol_ap),
                        reads=[b["pA2"], b["ccol"]], writes=[b["lsz"]])
                    ctx.step("scalar", lambda e, tt=tt: e.activation(
                        out=b["lsz"].t[:, tt, :], in_=b["lsz"].t[:, tt, :], func=AF.Ln),
                        reads=[b["lsz"]], writes=[b["lsz"]])
        for ci in range(STRIP // CH):
            tt = (ci * CH) // 128
            hf = (ci * CH) % 128
            emit_gla_chunk(ctx, consts, b, si, ci, tt, hf)
        for j in range(8):
            ctx.dma("gpsimd", dio["ofm_d"].t.ap()[j, :, col0:col0 + STRIP],
                    b["ofm"].t[:, j, :], reads=[b["ofm"]], writes=[dio["ofm_d"]],
                    tag=f"st_ofm{bi}.{si}.{j}")



def emit_p2(ctx, long, dio):
    with ExitStack() as ph:
        sb = lambda n, s, d: Buf(n, ph.enter_context(ctx.nc.sbuf_tensor(n, list(s), d)))
        ps = lambda n, s, d: Buf(n, ph.enter_context(ctx.nc.psum_tensor(n, list(s), d)))
        gf = sb("p2_gf", (1, 1024), F32)
        ctx.dma("sync", gf.t[:], dio["g_ffn"].t.ap()[:, :], writes=[gf])
        Wo = sb("Wos", (128, 8, 1024), BF16)
        for dc in range(8):
            ctx.dma("gpsimd", Wo.t[:, dc, :], dio["Wo"].t.ap()[dc * 128:(dc + 1) * 128, :],
                    writes=[Wo], tag=f"ldWo{dc}")
        b = {}
        b["x"] = [sb(f"p2_x{i}", (128, 8, STRIP), F32) for i in range(2)]
        b["ofm"] = [sb(f"p2_ofm{i}", (128, 8, STRIP), BF16) for i in range(2)]
        b["xlB"] = sb("p2_xlB", (128, 8, T), F32)
        b["xlb"] = [sb(f"p2_xlb{i}", (128, 8, STRIP), BF16) for i in range(2)]
        b["rowA"] = [sb(f"p2_rowA{i}", (1, STRIP), F32) for i in range(2)]
        b["rowB"] = [sb(f"p2_rowB{i}", (1, STRIP), F32) for i in range(2)]
        b["rows2"] = sb("p2_rows2", (1, T), F32)
        b["rq2"] = sb("p2_rq2", (1, T), F32)
        b["pAB"] = [ps(f"p2_pAB{i}", (128, 1024), F32) for i in range(2)]
        b["pG"] = [ps(f"p2_pG{i}", (128, 512), F32) for i in range(2)]
        b["pR"] = ps("p2_pR", (1, 512), F32)
        b["pGA"] = ps("p2_pGA", (128, 512), F32)
        for bi in range(NB):
            for si in range(NSTR):
                par = si % 2
                col0 = bi * T + si * STRIP
                x, ofm, xlb = b["x"][par], b["ofm"][par], b["xlb"][par]
                rowA, rowB, pAB = b["rowA"][par], b["rowB"][par], b["pAB"][par]
                for hc in range(2):
                    ctx.dma("gpsimd" if hc else "sync", x.t[:, hc * 4:(hc + 1) * 4, :],
                            dio["x_fm"].t.ap()[hc * 4:(hc + 1) * 4, :, col0:col0 + STRIP]
                            .rearrange("c p k -> p c k"),
                            reads=[dio["x_fm"]], writes=[x], tag=f"p2x{bi}.{si}.{hc}")
                    ctx.dma("gpsimd" if hc else "sync", ofm.t[:, hc * 4:(hc + 1) * 4, :],
                            dio["ofm_d"].t.ap()[hc * 4:(hc + 1) * 4, :, col0:col0 + STRIP]
                            .rearrange("c p k -> p c k"),
                            reads=[dio["ofm_d"]], writes=[ofm], tag=f"p2o{bi}.{si}.{hc}")
                ctx.dma("sync", rowA.t[:], dio["rows_d"].t.ap()[0:1, col0:col0 + STRIP],
                        reads=[dio["rows_d"]], writes=[rowA], tag=f"p2rA{bi}.{si}")
                ctx.dma("sync", rowB.t[:], dio["rows_d"].t.ap()[1:2, col0:col0 + STRIP],
                        reads=[dio["rows_d"]], writes=[rowB], tag=f"p2rB{bi}.{si}")
                def mmab(e, rowA=rowA, rowB=rowB, pAB=pAB):
                    e.matmul(pAB.t[:, 0:512], long["ones_rowf"].t[0:1, :],
                             rowA.t[0:1, :], start=True, stop=True)
                    return e.matmul(pAB.t[:, 512:1024], long["ones_rowf"].t[0:1, :],
                                    rowB.t[0:1, :], start=True, stop=True)
                ctx.step("tensor", mmab, reads=[rowA, rowB, long["ones_rowf"]],
                         writes=[pAB], tag=f"p2ab{bi}.{si}")
                for dc in range(8):
                    pG = b["pG"][dc % 2]
                    def mmg(e, dc=dc, pG=pG, ofm=ofm):
                        last = None
                        for vc in range(8):
                            last = e.matmul(pG.t[:, 0:STRIP],
                                            Wo.t[:, vc, dc * 128:(dc + 1) * 128],
                                            ofm.t[:, vc, :],
                                            start=(vc == 0), stop=(vc == 7))
                        return last
                    ctx.step("tensor", mmg, reads=[Wo, ofm], writes=[pG],
                             tag=f"p2g{bi}.{si}.{dc}")
                    xlc = b["xlB"].t[:, dc, si * STRIP:(si + 1) * STRIP]
                    ctx.step("vector", lambda e, dc=dc, x=x, pAB=pAB, xlc=xlc: e.tensor_tensor(
                        xlc, x.t[:, dc, :], pAB.t[:, 0:512], op=ALU.mult),
                        reads=[x, pAB], writes=[b["xlB"]])
                    ctx.step("vector", lambda e, dc=dc, pAB=pAB, xlc=xlc: e.tensor_tensor(
                        xlc, xlc, pAB.t[:, 512:1024], op=ALU.add),
                        reads=[b["xlB"], pAB], writes=[b["xlB"]])
                    ctx.step("vector", lambda e, dc=dc, pG=pG, xlc=xlc: e.tensor_tensor(
                        xlc, xlc, pG.t[:, 0:STRIP], op=ALU.add),
                        reads=[b["xlB"], pG], writes=[b["xlB"]])
                    ctx.step("vector", lambda e, dc=dc, x=x, xlc=xlc: e.tensor_tensor(
                        x.t[:, dc, :], x.t[:, dc, :], xlc, op=ALU.add),
                        reads=[x, b["xlB"]], writes=[x])
                ctx.dma("scalar", dio["xpl_d"].t.ap()[:, :, col0:col0 + STRIP]
                        .rearrange("c p k -> p c k"), x.t[:, :, :],
                        reads=[x], writes=[dio["xpl_d"]], tag=f"p2sxp{bi}.{si}")
                ctx.step("vector", lambda e, xlb=xlb, si=si: e.tensor_copy(
                    xlb.t[:], b["xlB"].t[:, :, si * STRIP:(si + 1) * STRIP]),
                    reads=[b["xlB"]], writes=[xlb])
                ctx.step("scalar", lambda e, xlb=xlb: e.activation(
                    out=xlb.t[:], in_=xlb.t[:], func=AF.Square),
                    reads=[xlb], writes=[xlb])
                def mmst(e, xlb=xlb):
                    last = None
                    for dc in range(8):
                        last = e.matmul(b["pR"].t[0:1, 0:STRIP], long["ones_col"].t[:, 0:1],
                                        xlb.t[:, dc, :], start=(dc == 0), stop=(dc == 7))
                    return last
                ctx.step("tensor", mmst, reads=[xlb, long["ones_col"]], writes=[b["pR"]],
                         tag=f"p2st{bi}.{si}")
                ctx.step("vector", lambda e, si=si: e.tensor_copy(
                    b["rows2"].t[0:1, si * STRIP:(si + 1) * STRIP], b["pR"].t[0:1, 0:STRIP]),
                    reads=[b["pR"]], writes=[b["rows2"]])
            bias6b = ctx.constcol(1e-6, 0, 1)
            ctx.step("scalar", lambda e: e.activation(out=b["rq2"].t[:], in_=b["rows2"].t[:],
                                                      func=AF.Ln, bias=bias6b, scale=1.0 / D),
                     reads=[b["rows2"], ctx._cvals], writes=[b["rq2"]])
            ctx.step("scalar", lambda e: e.activation(out=b["rq2"].t[:], in_=b["rq2"].t[:],
                                                      func=AF.Exp, scale=-0.5),
                     reads=[b["rq2"]], writes=[b["rq2"]])
            for si in range(NSTR):
                par = si % 2
                col0 = bi * T + si * STRIP
                xlb = b["xlb"][par]
                for dc in range(8):
                    def mmga(e, dc=dc, si=si):
                        return e.matmul(b["pGA"].t[:, 0:STRIP],
                                        gf.t[0:1, dc * 128:(dc + 1) * 128],
                                        b["rq2"].t[0:1, si * STRIP:(si + 1) * STRIP],
                                        start=True, stop=True)
                    ctx.step("tensor", mmga, reads=[gf, b["rq2"]], writes=[b["pGA"]],
                             tag=f"p2ga{bi}.{si}.{dc}")
                    ctx.step("vector", lambda e, dc=dc, si=si, xlb=xlb: e.tensor_tensor(
                        xlb.t[:, dc, :], b["xlB"].t[:, dc, si * STRIP:(si + 1) * STRIP],
                        b["pGA"].t[:, 0:STRIP], op=ALU.mult),
                        reads=[b["xlB"], b["pGA"]], writes=[xlb])
                ctx.dma("scalar", dio["xlg2_d"].t.ap()[:, :, col0:col0 + STRIP]
                        .rearrange("c p k -> p c k"), xlb.t[:, :, :],
                        reads=[xlb], writes=[dio["xlg2_d"]], tag=f"p2sxg{bi}.{si}")



def emit_p3(ctx, long, dio):
    with ExitStack() as ph:
        sb = lambda n, s, d: Buf(n, ph.enter_context(ctx.nc.sbuf_tensor(n, list(s), d)))
        ps = lambda n, s, d: Buf(n, ph.enter_context(ctx.nc.psum_tensor(n, list(s), d)))
        W1 = sb("w1s", (128, 8, DFF), BF16)
        W2 = sb("w2s", (128, 8, DFF), BF16)
        W3 = sb("w3s", (128, NFC, 1024), BF16)
        for dc in range(8):
            ctx.dma("gpsimd", W1.t[:, dc, :], dio["w1"].t.ap()[dc * 128:(dc + 1) * 128, :],
                    writes=[W1], tag=f"ldw1{dc}")
            ctx.dma("gpsimd", W2.t[:, dc, :], dio["w2"].t.ap()[dc * 128:(dc + 1) * 128, :],
                    writes=[W2], tag=f"ldw2{dc}")
        for fc in range(NFC):
            r = fc_rows(fc)
            ctx.dma("gpsimd", W3.t[0:r, fc, :], dio["w3"].t.ap()[fc * 128:fc * 128 + r, :],
                    writes=[W3], tag=f"ldw3{fc}")
        b = {}
        b["xlg"] = [sb(f"p3_xlg{i}", (128, 8, STRIP), BF16) for i in range(2)]
        b["ffin"] = sb("p3_ffin", (128, NFC, STRIP), BF16)
        b["sil"] = sb("p3_sil", (128, STRIP), BF16)
        b["xp"] = [sb(f"p3_xp{i}", (128, 8, STRIP), F32) for i in range(2)]
        b["pF"] = [ps(f"p3_pF{i}", (128, 1024), F32) for i in range(2)]
        b["pW3"] = [ps(f"p3_pW3{i}", (128, STRIP), F32) for i in range(2)]
        for bi in range(NB):
            for si in range(NSTR):
                emit_p3_strip(ctx, long, dio, b, W1, W2, W3, bi, si)


def emit_p3_strip(ctx, long, dio, b, W1, W2, W3, bi, si):
    col0 = bi * T + si * STRIP
    par = (bi * NSTR + si) % 2
    xlg, xp = b["xlg"][par], b["xp"][par]
    for hc in range(2):
        ctx.dma("gpsimd" if hc else "sync", xlg.t[:, hc * 4:(hc + 1) * 4, :],
                dio["xlg2_d"].t.ap()[hc * 4:(hc + 1) * 4, :, col0:col0 + STRIP]
                .rearrange("c p k -> p c k"),
                reads=[dio["xlg2_d"]], writes=[xlg], tag=f"p3xlg{bi}.{si}.{hc}")
    ctx.dma("scalar", xp.t[:, :, :],
            dio["xpl_d"].t.ap()[:, :, col0:col0 + STRIP].rearrange("c p k -> p c k"),
            reads=[dio["xpl_d"]], writes=[xp], tag=f"p3xp{bi}.{si}")
    for fc in range(NFC):
        r = fc_rows(fc)
        pF = b["pF"][fc % 2]
        def mmf(e, fc=fc, r=r, pF=pF, xlg=xlg):
            last = None
            for dc in range(8):
                last = e.matmul(pF.t[0:r, 0:STRIP],
                                W1.t[:, dc, fc * 128:fc * 128 + r],
                                xlg.t[:, dc, :], start=(dc == 0), stop=(dc == 7))
            for dc in range(8):
                last = e.matmul(pF.t[0:r, 512:512 + STRIP],
                                W2.t[:, dc, fc * 128:fc * 128 + r],
                                xlg.t[:, dc, :], start=(dc == 0), stop=(dc == 7))
            return last
        ctx.step("tensor", mmf, reads=[W1, W2, xlg], writes=[pF],
                 tag=f"p3f{bi}.{si}.{fc}")
        ctx.step("scalar", lambda e, r=r, pF=pF: e.activation(
            out=b["sil"].t[0:r, :], in_=pF.t[0:r, 0:STRIP], func=AF.Silu),
            reads=[pF], writes=[b["sil"]])
        ctx.step("vector", lambda e, fc=fc, r=r, pF=pF: e.tensor_tensor(
            b["ffin"].t[0:r, fc, :], pF.t[0:r, 512:512 + STRIP], b["sil"].t[0:r, :],
            op=ALU.mult),
            reads=[pF, b["sil"]], writes=[b["ffin"]])
    for dc in range(8):
        pW3 = b["pW3"][dc % 2]
        def mmw3(e, dc=dc, pW3=pW3):
            last = None
            for fc in range(NFC):
                r = fc_rows(fc)
                last = e.matmul(pW3.t[:, 0:STRIP],
                                W3.t[0:r, fc, dc * 128:(dc + 1) * 128],
                                b["ffin"].t[0:r, fc, :],
                                start=(fc == 0), stop=(fc == NFC - 1))
            return last
        ctx.step("tensor", mmw3, reads=[W3, b["ffin"]], writes=[pW3],
                 tag=f"p3w3{bi}.{si}.{dc}")
        ctx.step("vector", lambda e, dc=dc, pW3=pW3, xp=xp: e.tensor_tensor(
            xp.t[:, dc, :], xp.t[:, dc, :], pW3.t[:, 0:STRIP], op=ALU.add),
            reads=[xp, pW3], writes=[xp])
    ctx.dma("scalar", dio["x1_fm"].t.ap()[:, :, col0:col0 + STRIP]
            .rearrange("c p k -> p c k"), xp.t[:, :, :],
            reads=[xp], writes=[dio["x1_fm"]], tag=f"p3sx1{bi}.{si}")



def emit_p4(ctx, consts, long, dio):
    NTI = NB * 16
    with ExitStack() as ph:
        sb = lambda n, s, d: Buf(n, ph.enter_context(ctx.nc.sbuf_tensor(n, list(s), d)))
        ps = lambda n, s, d: Buf(n, ph.enter_context(ctx.nc.psum_tensor(n, list(s), d)))
        b = {}
        b["lg"] = sb("p4_lg", (128, NTI, NE), F32)
        b["wrk"] = sb("p4_wrk", (128, NTI, NE), F32)
        b["mx"] = sb("p4_mx", (128, NTI), F32)
        b["eq"] = sb("p4_eq", (128, NTI, NE), F32)
        b["m"] = sb("p4_m", (128, NTI, NE), F32)
        b["mb"] = sb("p4_mb", (128, NTI, NE), BF16)
        b["w"] = sb("p4_w", (128, NTI, NE), F32)
        b["psum"] = sb("p4_psum", (128, NTI), F32)
        b["base"] = sb("p4_base", (1, NE), F32)
        b["totrow"] = sb("p4_totrow", (1, NTI * NE), F32)
        b["brow"] = sb("p4_brow", (1, NTI * NE), F32)
        b["pP"] = ps("p4_pP", (128, 512), F32)
        b["pB"] = ps("p4_pB", (1, 512), F32)
        ctx.dma("sync", b["lg"].t[:], dio["logits_d"].t.ap()[:, :, :], reads=[dio["logits_d"]],
                writes=[b["lg"]], tag="p4lg")
        ctx.step("vector", lambda e: e.tensor_copy(b["wrk"].t[:], b["lg"].t[:]),
                 reads=[b["lg"]], writes=[b["wrk"]])
        ctx.step("vector", lambda e: e.memset(b["m"].t[:], 0.0), writes=[b["m"]])
        for it in range(8):
            ctx.step("vector", lambda e: e.tensor_reduce(
                b["mx"].t[:].rearrange("p t -> p t ()"), b["wrk"].t[:], axis=AX.X, op=ALU.max),
                reads=[b["wrk"]], writes=[b["mx"]])
            ctx.step("vector", lambda e: e.tensor_tensor(
                b["eq"].t[:], b["wrk"].t[:],
                b["mx"].t[:].rearrange("p t -> p t ()").broadcast_to([128, NTI, NE]),
                op=ALU.is_ge),
                reads=[b["wrk"], b["mx"]], writes=[b["eq"]])
            ctx.step("vector", lambda e: e.tensor_tensor(
                b["m"].t[:], b["m"].t[:], b["eq"].t[:], op=ALU.add),
                reads=[b["m"], b["eq"]], writes=[b["m"]])
            if it < 7:
                ctx.step("vector", lambda e: e.scalar_tensor_tensor(
                    b["wrk"].t[:], b["eq"].t[:], -BIG, b["wrk"].t[:], op0=ALU.mult, op1=ALU.add),
                    reads=[b["eq"], b["wrk"]], writes=[b["wrk"]])
        ctx.step("vector", lambda e: e.tensor_scalar_min(b["m"].t[:], b["m"].t[:], 1.0),
                 reads=[b["m"]], writes=[b["m"]])
        ctx.step("vector", lambda e: e.tensor_reduce(
            b["mx"].t[:].rearrange("p t -> p t ()"), b["lg"].t[:], axis=AX.X, op=ALU.max),
            reads=[b["lg"]], writes=[b["mx"]])
        ctx.step("vector", lambda e: e.tensor_tensor(
            b["wrk"].t[:], b["lg"].t[:],
            b["mx"].t[:].rearrange("p t -> p t ()").broadcast_to([128, NTI, NE]),
            op=ALU.subtract),
            reads=[b["lg"], b["mx"]], writes=[b["wrk"]])
        ctx.step("scalar", lambda e: e.activation(out=b["wrk"].t[:], in_=b["wrk"].t[:], func=AF.Exp),
                 reads=[b["wrk"]], writes=[b["wrk"]])
        ctx.step("vector", lambda e: e.tensor_tensor(b["w"].t[:], b["wrk"].t[:], b["m"].t[:], op=ALU.mult),
                 reads=[b["wrk"], b["m"]], writes=[b["w"]])
        ctx.step("vector", lambda e: e.tensor_reduce(
            b["psum"].t[:].rearrange("p t -> p t ()"), b["w"].t[:], axis=AX.X, op=ALU.add),
            reads=[b["w"]], writes=[b["psum"]])
        ctx.step("vector", lambda e: e.reciprocal(b["psum"].t[:], b["psum"].t[:]),
                 reads=[b["psum"]], writes=[b["psum"]])
        ctx.step("vector", lambda e: e.tensor_tensor(
            b["w"].t[:], b["w"].t[:],
            b["psum"].t[:].rearrange("p t -> p t ()").broadcast_to([128, NTI, NE]),
            op=ALU.mult),
            reads=[b["w"], b["psum"]], writes=[b["w"]])
        ctx.dma("sync", dio["w_out"].t.ap()[:, :, :], b["w"].t[:], reads=[b["w"]],
                writes=[dio["w_out"]], tag="p4w")
        ctx.step("vector", lambda e: e.tensor_copy(b["mb"].t[:], b["m"].t[:]),
                 reads=[b["m"]], writes=[b["mb"]])
        for g in range(NTI // 8):
            def mmtot(e, g=g):
                return e.matmul(b["pB"].t[0:1, 0:512], long["ones_col"].t[:, 0:1],
                                b["mb"].t[:, g * 8:(g + 1) * 8, :], start=True, stop=True)
            ctx.step("tensor", mmtot, reads=[b["mb"], long["ones_col"]], writes=[b["pB"]],
                     tag=f"tot{g}")
            ctx.step("vector", lambda e, g=g: e.tensor_copy(
                b["totrow"].t[0:1, g * 512:(g + 1) * 512], b["pB"].t[0:1, 0:512]),
                reads=[b["pB"]], writes=[b["totrow"]])
        cur, nxt = b["totrow"], b["brow"]
        s = 1
        while s < NTI:
            ctx.step("vector", lambda e, cur=cur, nxt=nxt, s=s: e.tensor_copy(
                nxt.t[0:1, 0:s * NE], cur.t[0:1, 0:s * NE]),
                reads=[cur], writes=[nxt])
            ctx.step("vector", lambda e, cur=cur, nxt=nxt, s=s: e.tensor_tensor(
                nxt.t[0:1, s * NE:NTI * NE], cur.t[0:1, s * NE:NTI * NE],
                cur.t[0:1, 0:(NTI - s) * NE], op=ALU.add),
                reads=[cur], writes=[nxt])
            cur, nxt = nxt, cur
            s *= 2
        ctx.dma("sync", dio["cnt_out"].t.ap()[:, :],
                cur.t[0:1, (NTI - 1) * NE:NTI * NE],
                reads=[cur], writes=[dio["cnt_out"]], tag="p4cnt")
        bex = nxt
        ctx.step("vector", lambda e, bex=bex: e.memset(bex.t[0:1, 0:NE], 0.0), writes=[bex])
        ctx.step("vector", lambda e, cur=cur, bex=bex: e.tensor_copy(
            bex.t[0:1, NE:NTI * NE], cur.t[0:1, 0:(NTI - 1) * NE]),
            reads=[cur], writes=[bex])
        for ti in range(NTI):
            def mmpos(e, ti=ti, bex=bex):
                e.matmul(b["pP"].t[:, 0:NE], consts["triu128s"].t[:, :],
                         b["mb"].t[:, ti, :], start=True, stop=False)
                return e.matmul(b["pP"].t[:, 0:NE], long["ones_rowf"].t[0:1, :],
                                bex.t[0:1, ti * NE:(ti + 1) * NE], start=False, stop=True)
            ctx.step("tensor", mmpos, reads=[consts["triu128s"], b["mb"], bex,
                                             long["ones_rowf"]],
                     writes=[b["pP"]], tag=f"pos{ti}")
            ctx.step("vector", lambda e, ti=ti: e.tensor_copy(
                b["wrk"].t[:, ti, :], b["pP"].t[:, 0:NE]),
                reads=[b["pP"]], writes=[b["wrk"]])
        ctx.dma("sync", dio["pos_out"].t.ap()[:, :, :], b["wrk"].t[:], reads=[b["wrk"]],
                writes=[dio["pos_out"]], tag="p4pos")

"""L2 program: global slot addresses + per-rank weights.

in:  w_own, pos_own [128, NTI, 64] f32; cnt_all [8, 64] f32; coremask [1, 8] f32
out: addr_r [128, NTI, 8] i32 (global slot = e*CAP + gpos, OOB=16384 invalid);
     w_r [128, NTI, 8] f32
"""
import numpy as np
from contextlib import ExitStack

NE = 64
CAP = 256
OOBS = float(NE * CAP)


def emit_l2(ctx, long, dio, NB):
    NTI = NB * 16
    with ExitStack() as ph:
        sb = lambda n, s, d: Buf(n, ph.enter_context(ctx.nc.sbuf_tensor(n, list(s), d)))
        ps = lambda n, s, d: Buf(n, ph.enter_context(ctx.nc.psum_tensor(n, list(s), d)))
        b = {}
        b["w"] = sb("l2w", (128, NTI, NE), F32)
        b["pos"] = sb("l2pos", (128, NTI, NE), F32)
        b["cnt"] = sb("l2cnt", (8, NE), F32)
        b["cmask"] = sb("l2cmask", (1, 8), F32)
        b["base"] = sb("l2base", (1, NE), F32)
        b["bcs"] = sb("l2bcs", (128, NE), F32)
        b["vsel"] = sb("l2vsel", (128, NTI, NE), F32)
        b["rk"] = sb("l2rk", (128, NTI, NE), F32)
        b["tmp"] = sb("l2tmp", (128, NTI, NE), F32)
        b["gpos"] = sb("l2gpos", (128, NTI, NE), F32)
        b["ecap"] = sb("l2ecap", (128, NE), F32)
        b["arf"] = sb("l2arf", (128, NTI), F32)
        b["eqm"] = sb("l2eqm", (128, NTI, NE), F32)
        b["has"] = sb("l2has", (128, NTI), F32)
        b["ar"] = sb("l2ar", (128, NTI, 8), I32)
        b["wr"] = sb("l2wr", (128, NTI, 8), F32)
        b["cmcol"] = sb("l2cmcol", (8, 1), F32)
        b["pB"] = ps("l2pB", (128, 512), F32)
        ctx.dma("sync", b["w"].t[:], dio["w_own"].t.ap()[:, :, :],
                reads=[dio["w_own"]], writes=[b["w"]])
        ctx.dma("sync", b["pos"].t[:], dio["pos_own"].t.ap()[:, :, :],
                reads=[dio["pos_own"]], writes=[b["pos"]])
        ctx.dma("sync", b["cnt"].t[:], dio["cnt_all"].t.ap()[:, :],
                reads=[dio["cnt_all"]], writes=[b["cnt"]])
        ctx.dma("sync", b["cmask"].t[:], dio["coremask"].t.ap()[:, :],
                reads=[dio["coremask"]], writes=[b["cmask"]])
        def mmt(e):
            return e.matmul(b["pB"].t[0:8, 0:1], b["cmask"].t[0:1, :],
                            long["ones_rowf"].t[:, 0:1], start=True, stop=True)
        ctx.step("tensor", mmt, reads=[b["cmask"], long["ones_rowf"]], writes=[b["pB"]], tag="cmT")
        ctx.step("vector", lambda e: e.tensor_copy(b["cmcol"].t[:], b["pB"].t[0:8, 0:1]),
                 reads=[b["pB"]], writes=[b["cmcol"]])
        def mmb(e):
            return e.matmul(b["pB"].t[0:1, 0:NE], b["cmcol"].t[:, :], b["cnt"].t[:, :],
                            start=True, stop=True)
        ctx.step("tensor", mmb, reads=[b["cmcol"], b["cnt"]], writes=[b["pB"]], tag="baserow")
        ctx.step("vector", lambda e: e.tensor_copy(b["base"].t[:], b["pB"].t[0:1, 0:NE]),
                 reads=[b["pB"]], writes=[b["base"]])
        def mmbc(e):
            return e.matmul(b["pB"].t[:, 0:NE], long["ones_rowf"].t[0:1, :], b["base"].t[0:1, :],
                            start=True, stop=True)
        ctx.step("tensor", mmbc, reads=[b["base"], long["ones_rowf"]], writes=[b["pB"]], tag="bbc")
        ctx.step("vector", lambda e: e.tensor_copy(b["bcs"].t[:], b["pB"].t[:, 0:NE]),
                 reads=[b["pB"]], writes=[b["bcs"]])
        ctx.step("vector", lambda e: e.tensor_tensor(
            b["gpos"].t[:], b["pos"].t[:],
            b["bcs"].t[:].rearrange("p e -> p () e").broadcast_to([128, NTI, NE]), op=ALU.add),
            reads=[b["pos"], b["bcs"]], writes=[b["gpos"]])
        ctx.step("vector", lambda e: e.tensor_scalar(
            b["vsel"].t[:], b["w"].t[:], 0.0, None, op0=ALU.is_gt),
            reads=[b["w"]], writes=[b["vsel"]])
        ctx.step("vector", lambda e: e.tensor_scalar(
            b["tmp"].t[:], b["gpos"].t[:], float(CAP) - 0.5, None, op0=ALU.is_lt),
            reads=[b["gpos"]], writes=[b["tmp"]])
        ctx.step("vector", lambda e: e.tensor_tensor(
            b["vsel"].t[:], b["vsel"].t[:], b["tmp"].t[:], op=ALU.mult),
            reads=[b["vsel"], b["tmp"]], writes=[b["vsel"]])
        ctx.step("gpsimd", lambda e: e.iota(b["ecap"].t[:], pattern=[[CAP, NE]], base=0,
                                            channel_multiplier=0,
                                            allow_small_or_imprecise_dtypes=True),
                 writes=[b["ecap"]])
        ctx.step("vector", lambda e: e.tensor_tensor(
            b["tmp"].t[:], b["gpos"].t[:],
            b["ecap"].t[:].rearrange("p e -> p () e").broadcast_to([128, NTI, NE]), op=ALU.add),
            reads=[b["gpos"], b["ecap"]], writes=[b["tmp"]])
        ctx.step("vector", lambda e: e.tensor_scalar(
            b["tmp"].t[:], b["tmp"].t[:], 1.0, -OOBS, op0=ALU.mult, op1=ALU.add),
            reads=[b["tmp"]], writes=[b["tmp"]])
        ctx.step("vector", lambda e: e.tensor_tensor(
            b["tmp"].t[:], b["tmp"].t[:], b["vsel"].t[:], op=ALU.mult),
            reads=[b["tmp"], b["vsel"]], writes=[b["tmp"]])
        ctx.step("vector", lambda e: e.tensor_scalar(
            b["tmp"].t[:], b["tmp"].t[:], 1.0, OOBS, op0=ALU.mult, op1=ALU.add),
            reads=[b["tmp"]], writes=[b["tmp"]])
        ctx.step("vector", lambda e: e.tensor_tensor(
            b["w"].t[:], b["w"].t[:], b["vsel"].t[:], op=ALU.mult),
            reads=[b["w"], b["vsel"]], writes=[b["w"]])
        for ti in range(NTI):
            ctx.step("vector", lambda e, ti=ti: e.tensor_tensor_scan(
                b["rk"].t[:, ti, :], b["vsel"].t[:, ti, :], b["vsel"].t[:, ti, :],
                0.0, op0=ALU.add, op1=ALU.bypass),
                reads=[b["vsel"]], writes=[b["rk"]])
        ctx.step("vector", lambda e: e.tensor_tensor(
            b["rk"].t[:], b["rk"].t[:], b["vsel"].t[:], op=ALU.mult),
            reads=[b["rk"], b["vsel"]], writes=[b["rk"]])
        for r in range(8):
            ctx.step("vector", lambda e, r=r: e.tensor_scalar(
                b["eqm"].t[:], b["rk"].t[:], float(r + 1), None, op0=ALU.is_equal),
                reads=[b["rk"]], writes=[b["eqm"]])
            ctx.step("vector", lambda e: e.tensor_reduce(
                b["has"].t[:].rearrange("p t -> p t ()"), b["eqm"].t[:], axis=AX.X, op=ALU.add),
                reads=[b["eqm"]], writes=[b["has"]])
            ctx.step("vector", lambda e: e.tensor_tensor(
                b["eqm"].t[:], b["eqm"].t[:], b["tmp"].t[:], op=ALU.mult),
                reads=[b["eqm"], b["tmp"]], writes=[b["eqm"]])
            ctx.step("vector", lambda e: e.tensor_reduce(
                b["arf"].t[:].rearrange("p t -> p t ()"), b["eqm"].t[:], axis=AX.X, op=ALU.add),
                reads=[b["eqm"]], writes=[b["arf"]])
            ctx.step("vector", lambda e: e.scalar_tensor_tensor(
                b["arf"].t[:], b["has"].t[:], -OOBS, b["arf"].t[:], op0=ALU.mult, op1=ALU.add),
                reads=[b["has"], b["arf"]], writes=[b["arf"]])
            ctx.step("vector", lambda e: e.tensor_scalar(
                b["arf"].t[:], b["arf"].t[:], 1.0, OOBS, op0=ALU.mult, op1=ALU.add),
                reads=[b["arf"]], writes=[b["arf"]])
            ctx.step("vector", lambda e, r=r: e.tensor_copy(
                b["ar"].t[:, :, r:r + 1], b["arf"].t[:].rearrange("p t -> p t ()")),
                reads=[b["arf"]], writes=[b["ar"]])
            ctx.step("vector", lambda e, r=r: e.tensor_scalar(
                b["eqm"].t[:], b["rk"].t[:], float(r + 1), None, op0=ALU.is_equal),
                reads=[b["rk"]], writes=[b["eqm"]])
            ctx.step("vector", lambda e: e.tensor_tensor(
                b["eqm"].t[:], b["eqm"].t[:], b["w"].t[:], op=ALU.mult),
                reads=[b["eqm"], b["w"]], writes=[b["eqm"]])
            ctx.step("vector", lambda e: e.tensor_reduce(
                b["arf"].t[:].rearrange("p t -> p t ()"), b["eqm"].t[:], axis=AX.X, op=ALU.add),
                reads=[b["eqm"]], writes=[b["arf"]])
            ctx.step("vector", lambda e, r=r: e.tensor_copy(
                b["wr"].t[:, :, r:r + 1], b["arf"].t[:].rearrange("p t -> p t ()")),
                reads=[b["arf"]], writes=[b["wr"]])
        ctx.dma("sync", dio["addr_r"].t.ap()[:, :, :], b["ar"].t[:],
                reads=[b["ar"]], writes=[dio["addr_r"]], tag="sar")
        ctx.dma("sync", dio["w_r"].t.ap()[:, :, :], b["wr"].t[:],
                reads=[b["wr"]], writes=[dio["w_r"]], tag="swr")

"""L3 program: expert FFN on this core's 8-expert dispatch shard.

in:  disp [2048, 1024] bf16 (rows = 8 experts x 256 slots, token-major);
     wg_t, wu_t [8, 1024, 768] f32; wd_t [8, 768, 1024] f32
out: y [2048, 1024] bf16
"""
import numpy as np
from contextlib import ExitStack

CAP = 256
INTER = 768
NEL = 8


def emit_l3(ctx, consts, long, dio):
    with ExitStack() as ph:
        sb = lambda n, s, d: Buf(n, ph.enter_context(ctx.nc.sbuf_tensor(n, list(s), d)))
        ps = lambda n, s, d: Buf(n, ph.enter_context(ctx.nc.psum_tensor(n, list(s), d)))
        b = {}
        b["dtm"] = sb("l3dtm", (128, 2, 1024), BF16)
        b["dfm"] = sb("l3dfm", (128, 8, CAP), BF16)
        b["wg"] = [sb(f"l3wg{i}", (128, 8, INTER), BF16) for i in range(2)]
        b["wu"] = [sb(f"l3wu{i}", (128, 8, INTER), BF16) for i in range(2)]
        b["wd"] = [sb(f"l3wd{i}", (128, 6, 1024), BF16) for i in range(2)]
        b["sil"] = sb("l3sil", (128, 512), BF16)
        b["gu"] = sb("l3gu", (128, 6, CAP), BF16)
        b["ytm"] = sb("l3ytm", (128, 2, 1024), BF16)
        b["pG"] = ps("l3pG", (128, 512), F32)
        b["pY"] = ps("l3pY", (128, 1024), F32)
        b["pT"] = ps("l3pT", (128, 1024), BF16)
        for ei in range(NEL):
            par = ei % 2
            for dc in range(8):
                ctx.dma("gpsimd", b["wg"][par].t[:, dc, :],
                        dio["wg_t"].t.ap()[ei, dc * 128:(dc + 1) * 128, :],
                        reads=[dio["wg_t"]], writes=[b["wg"][par]], tag=f"l3wg{ei}.{dc}")
                ctx.dma("gpsimd", b["wu"][par].t[:, dc, :],
                        dio["wu_t"].t.ap()[ei, dc * 128:(dc + 1) * 128, :],
                        reads=[dio["wu_t"]], writes=[b["wu"][par]], tag=f"l3wu{ei}.{dc}")
            for ic in range(6):
                ctx.dma("gpsimd", b["wd"][par].t[:, ic, :],
                        dio["wd_t"].t.ap()[ei, ic * 128:(ic + 1) * 128, :],
                        reads=[dio["wd_t"]], writes=[b["wd"][par]], tag=f"l3wd{ei}.{ic}")
            for half in range(2):
                ctx.dma("sync", b["dtm"].t[:, half, :],
                        dio["disp"].t.ap()[ei * CAP + half * 128:ei * CAP + (half + 1) * 128, :],
                        reads=[dio["disp"]], writes=[b["dtm"]], tag=f"l3d{ei}.{half}")
            for rnd in range(2):
                def mmtr(e, rnd=rnd):
                    last = None
                    for i in range(8):
                        dc = (rnd * 8 + i) // 2
                        half = (rnd * 8 + i) % 2
                        last = e.transpose(
                            b["pT"].t[:, i * 128:(i + 1) * 128],
                            b["dtm"].t[:, half, dc * 128:(dc + 1) * 128],
                            consts["ident128"].t[:, :])
                    return last
                ctx.step("tensor", mmtr, reads=[b["dtm"], consts["ident128"]],
                         writes=[b["pT"]], tag=f"l3tr{ei}.{rnd}")
                def evac(e, rnd=rnd):
                    last = None
                    for i in range(8):
                        dc = (rnd * 8 + i) // 2
                        half = (rnd * 8 + i) % 2
                        last = e.tensor_copy(
                            b["dfm"].t[:, dc, half * 128:(half + 1) * 128],
                            b["pT"].t[:, i * 128:(i + 1) * 128])
                    return last
                ctx.step("vector", evac, reads=[b["pT"]], writes=[b["dfm"]],
                         tag=f"l3ev{ei}.{rnd}")
            for ic in range(6):
                def mmgu(e, ic=ic, par=par):
                    last = None
                    for dc in range(8):
                        last = e.matmul(b["pG"].t[:, 0:CAP],
                                        b["wg"][par].t[:, dc, ic * 128:(ic + 1) * 128],
                                        b["dfm"].t[:, dc, :], start=(dc == 0), stop=(dc == 7))
                    for dc in range(8):
                        last = e.matmul(b["pG"].t[:, CAP:2 * CAP],
                                        b["wu"][par].t[:, dc, ic * 128:(ic + 1) * 128],
                                        b["dfm"].t[:, dc, :], start=(dc == 0), stop=(dc == 7))
                    return last
                ctx.step("tensor", mmgu, reads=[b["wg"][par], b["wu"][par], b["dfm"]],
                         writes=[b["pG"]], tag=f"l3gu{ei}.{ic}")
                ctx.step("scalar", lambda e: e.activation(
                    out=b["sil"].t[:, 0:CAP], in_=b["pG"].t[:, 0:CAP], func=AF.Silu),
                    reads=[b["pG"]], writes=[b["sil"]])
                ctx.step("vector", lambda e, ic=ic: e.tensor_tensor(
                    b["gu"].t[:, ic, :], b["pG"].t[:, CAP:2 * CAP], b["sil"].t[:, 0:CAP],
                    op=ALU.mult),
                    reads=[b["pG"], b["sil"]], writes=[b["gu"]])
            for dgrp in range(4):
                def mmy(e, dgrp=dgrp, par=par):
                    last = None
                    for di in range(2):
                        dc2 = dgrp * 2 + di
                        for ic in range(6):
                            last = e.matmul(b["pY"].t[:, di * 512:di * 512 + CAP],
                                            b["wd"][par].t[:, ic, dc2 * 128:(dc2 + 1) * 128],
                                            b["gu"].t[:, ic, :], start=(ic == 0), stop=(ic == 5))
                    return last
                ctx.step("tensor", mmy, reads=[b["wd"][par], b["gu"]], writes=[b["pY"]],
                         tag=f"l3y{ei}.{dgrp}")
                for di in range(2):
                    dc2 = dgrp * 2 + di
                    ctx.step("scalar", lambda e, di=di: e.activation(
                        out=b["sil"].t[:, 0:CAP],
                        in_=b["pY"].t[:, di * 512:di * 512 + CAP], func=AF.Copy),
                        reads=[b["pY"]], writes=[b["sil"]])
                    def mmtrb(e):
                        last = None
                        for half in range(2):
                            last = e.transpose(
                                b["pT"].t[:, half * 128:(half + 1) * 128],
                                b["sil"].t[:, half * 128:(half + 1) * 128],
                                consts["ident128"].t[:, :])
                        return last
                    ctx.step("tensor", mmtrb, reads=[b["sil"], consts["ident128"]],
                             writes=[b["pT"]], tag=f"l3tb{ei}.{dc2}")
                    def evb(e, dc2=dc2):
                        last = None
                        for half in range(2):
                            last = e.tensor_copy(
                                b["ytm"].t[:, half, dc2 * 128:(dc2 + 1) * 128],
                                b["pT"].t[:, half * 128:(half + 1) * 128])
                        return last
                    ctx.step("vector", evb, reads=[b["pT"]], writes=[b["ytm"]],
                             tag=f"l3eb{ei}.{dc2}")
            for half in range(2):
                ctx.dma("gpsimd", dio["y"].t.ap()[ei * CAP + half * 128:ei * CAP + (half + 1) * 128, :],
                        b["ytm"].t[:, half, :], reads=[b["ytm"]], writes=[dio["y"]],
                        tag=f"l3sy{ei}.{half}")

"""Standalone routing pass: from x_fm produce ytm_d (token-major LN(x), bf16),
logits_d (TM), then top-8/w/pos/cnt via emit_p4.

dio keys (with prefix p): {p}x_fm [8,128,TT] f32 in; gw_t [1024,64] in;
{p}ytm_d [TT, 1024] bf16 out; {p}logits_d [128, NTI, 64] f32 internal;
{p}w_out, {p}pos_out [128, NTI, 64] f32 out; {p}cnt_out [1,64] out.
"""
import numpy as np
from contextlib import ExitStack

D = 1024
STRIP = 512
NT = 4
NE = 64


def emit_route(ctx, consts, long, dio, p, NB):
    with ExitStack() as ph:
        sb = lambda n, s, d: Buf(n, ph.enter_context(ctx.nc.sbuf_tensor(n, list(s), d)))
        ps = lambda n, s, d: Buf(n, ph.enter_context(ctx.nc.psum_tensor(n, list(s), d)))
        gwh = sb(p + "gwh", (128, 8, NE), BF16)
        gwl = sb(p + "gwl", (128, 8, NE), BF16)
        gwf = sb(p + "gwf", (128, 8, NE), F32)
        for dc in range(8):
            ctx.dma("sync", gwf.t[:, dc, :], dio["gw_t"].t.ap()[dc * 128:(dc + 1) * 128, :],
                    reads=[dio["gw_t"]], writes=[gwf])
        ctx.step("vector", lambda e: e.tensor_copy(gwh.t[:], gwf.t[:]), reads=[gwf], writes=[gwh])
        ctx.step("vector", lambda e: e.tensor_tensor(gwf.t[:], gwf.t[:], gwh.t[:], op=ALU.subtract),
                 reads=[gwf, gwh], writes=[gwf])
        ctx.step("vector", lambda e: e.tensor_copy(gwl.t[:], gwf.t[:]), reads=[gwf], writes=[gwl])
        b = {}
        b["x"] = sb(p + "rx", (128, 8, STRIP), F32)
        b["yh"] = sb(p + "ryh", (128, 8, STRIP), BF16)
        b["yl"] = sb(p + "ryl", (128, 8, STRIP), BF16)
        b["sq"] = sb(p + "rsq", (128, 8, STRIP), BF16)
        b["ytm"] = sb(p + "rytm", (128, 1024), BF16)
        b["lgst"] = sb(p + "rlg", (128, NT, NE), F32)
        b["srow"] = sb(p + "rsrow", (1, 2 * STRIP), F32)
        b["murow"] = sb(p + "rmur", (1, STRIP), F32)
        b["rstd"] = sb(p + "rrstd", (1, STRIP), F32)
        b["pAB"] = ps(p + "rpAB", (128, 1024), F32)
        b["pR"] = ps(p + "rpR", (1, 512), F32)
        b["pL"] = ps(p + "rpL", (128, 512), F32)
        b["pT"] = ps(p + "rpT", (128, 1024), BF16)
        for st in range(NB * 4):
            col0 = st * STRIP
            for hc in range(2):
                ctx.dma("gpsimd" if hc else "sync", b["x"].t[:, hc * 4:(hc + 1) * 4, :],
                        dio[p + "x_fm"].t.ap()[hc * 4:(hc + 1) * 4, :, col0:col0 + STRIP]
                        .rearrange("c p k -> p c k"),
                        reads=[dio[p + "x_fm"]], writes=[b["x"]], tag=f"{p}rx{st}.{hc}")
            ctx.step("vector", lambda e: e.tensor_copy(b["yh"].t[:], b["x"].t[:]),
                     reads=[b["x"]], writes=[b["yh"]])
            ctx.step("scalar", lambda e: e.activation(
                out=b["sq"].t[:], in_=b["yh"].t[:], func=AF.Square),
                reads=[b["yh"]], writes=[b["sq"]])
            for which, src, off in (("x", b["yh"], 0), ("x2", b["sq"], STRIP)):
                def mmst(e, src=src):
                    last = None
                    for dc in range(8):
                        last = e.matmul(b["pR"].t[0:1, 0:STRIP], long["ones_col"].t[:, 0:1],
                                        src.t[:, dc, :], start=(dc == 0), stop=(dc == 7))
                    return last
                ctx.step("tensor", mmst, reads=[src, long["ones_col"]], writes=[b["pR"]],
                         tag=f"{p}rst{st}.{which}")
                ctx.step("vector", lambda e, off=off: e.tensor_copy(
                    b["srow"].t[0:1, off:off + STRIP], b["pR"].t[0:1, 0:STRIP]),
                    reads=[b["pR"]], writes=[b["srow"]])
            rowmath_ln(ctx, (b["srow"], b["srow"].t[0:1, 0:STRIP]),
                             (b["srow"], b["srow"].t[0:1, STRIP:2 * STRIP]),
                             (b["murow"], b["murow"].t[:]), (b["rstd"], b["rstd"].t[:]))
            def mmab(e):
                e.matmul(b["pAB"].t[:, 0:512], long["ones_rowf"].t[0:1, :], b["rstd"].t[0:1, :],
                         start=True, stop=True)
                return e.matmul(b["pAB"].t[:, 512:1024], long["ones_rowf"].t[0:1, :],
                                b["murow"].t[0:1, :], start=True, stop=True)
            ctx.step("tensor", mmab, reads=[b["rstd"], b["murow"], long["ones_rowf"]],
                     writes=[b["pAB"]], tag=f"{p}rab{st}")
            for dc in range(8):
                ctx.step("vector", lambda e, dc=dc: e.tensor_tensor(
                    b["x"].t[:, dc, :], b["x"].t[:, dc, :], b["pAB"].t[:, 512:1024], op=ALU.add),
                    reads=[b["x"], b["pAB"]], writes=[b["x"]])
                ctx.step("vector", lambda e, dc=dc: e.tensor_tensor(
                    b["x"].t[:, dc, :], b["x"].t[:, dc, :], b["pAB"].t[:, 0:512], op=ALU.mult),
                    reads=[b["x"], b["pAB"]], writes=[b["x"]])
            ctx.step("vector", lambda e: e.tensor_copy(b["yh"].t[:], b["x"].t[:]),
                     reads=[b["x"]], writes=[b["yh"]])
            ctx.step("vector", lambda e: e.tensor_tensor(b["yl"].t[:], b["x"].t[:], b["yh"].t[:],
                                                         op=ALU.subtract),
                     reads=[b["x"], b["yh"]], writes=[b["yl"]])
            for tt in range(NT):
                ti = st * NT + tt
                def mml(e, tt=tt):
                    last = None
                    for dc in range(8):
                        last = e.matmul(b["pL"].t[:, tt * NE:(tt + 1) * NE],
                                        b["yh"].t[:, dc, tt * 128:(tt + 1) * 128],
                                        gwh.t[:, dc, :], start=(dc == 0), stop=False)
                    for dc in range(8):
                        last = e.matmul(b["pL"].t[:, tt * NE:(tt + 1) * NE],
                                        b["yl"].t[:, dc, tt * 128:(tt + 1) * 128],
                                        gwh.t[:, dc, :], start=False, stop=False)
                    for dc in range(8):
                        last = e.matmul(b["pL"].t[:, tt * NE:(tt + 1) * NE],
                                        b["yh"].t[:, dc, tt * 128:(tt + 1) * 128],
                                        gwl.t[:, dc, :], start=False, stop=(dc == 7))
                    return last
                ctx.step("tensor", mml, reads=[b["yh"], b["yl"], gwh, gwl],
                         writes=[b["pL"]], tag=f"{p}rl{st}.{tt}")
                ctx.step("vector", lambda e, tt=tt: e.tensor_copy(
                    b["lgst"].t[:, tt, :], b["pL"].t[:, tt * NE:(tt + 1) * NE]),
                    reads=[b["pL"]], writes=[b["lgst"]])
                def mmtr(e, tt=tt):
                    last = None
                    for dc in range(8):
                        last = e.transpose(b["pT"].t[:, dc * 128:(dc + 1) * 128],
                                           b["yh"].t[:, dc, tt * 128:(tt + 1) * 128],
                                           consts["ident128"].t[:, :])
                    return last
                ctx.step("tensor", mmtr, reads=[b["yh"], consts["ident128"]], writes=[b["pT"]],
                         tag=f"{p}rtr{ti}")
                ctx.step("vector", lambda e: e.tensor_copy(b["ytm"].t[:], b["pT"].t[:]),
                         reads=[b["pT"]], writes=[b["ytm"]])
                ctx.dma("gpsimd", dio[p + "ytm_d"].t.ap()[ti * 128:(ti + 1) * 128, :],
                        b["ytm"].t[:], reads=[b["ytm"]], writes=[dio[p + "ytm_d"]],
                        tag=f"{p}sytm{ti}")
            ctx.dma("sync", dio[p + "logits_d"].t.ap()[:, st * NT:(st + 1) * NT, :],
                    b["lgst"].t[:], reads=[b["lgst"]], writes=[dio[p + "logits_d"]],
                    tag=f"{p}slg{st}")
    ctx.barrier_all()
    emit_p4_pfx(ctx, consts, long, dio, p, NB)


def emit_p4_pfx(ctx, consts, long, dio, p, NB):
    sub = dict(dio)
    sub["logits_d"] = dio[p + "logits_d"]
    sub["w_out"] = dio[p + "w_out"]
    sub["pos_out"] = dio[p + "pos_out"]
    sub["cnt_out"] = dio[p + "cnt_out"]
    emit_p4(ctx, consts, long, sub)

"""attn_b block emitter: standard softmax-attention transformer block with the
reference's odd softmax-over-query-axis, plus the surrounding residual/LN.

Sequence per launch-4:
  in: x1_fm [8,128,TT] f32, yplanes [8, TT, 1024] bf16 (MoE entry planes),
      w_r [128, NTI, 8] f32, in_proj_t [1024, 3072] f32 (host-transposed),
      in_proj_b [1, 3072], out_w_t [1024, 1024], out_b [1,1024],
      lin1_t [1024, 4096], lin1_b [1,4096], lin2_t [4096, 1024], lin2_b [1,1024],
      ln1_g/b, ln2_g/b [1,1024], gw_t [1024, 64]
  out: x3_fm, s3_out, w3/pos3/cnt3 (via emit_routing), y3tm_d
  internal: x2_d f32, y2_d f32, post_d...

Phases:
  B0: delta = sum_r w_r*plane_r -> X2 = x1 + delta (FM via transposes) -> x2_d
      + stats -> y2 = LN(X2) -> y2_d (f32) + y2bf strips -> q/k/v projections
  B1: attention per (b, h) -> post (FM, per b) -> out-proj + out_b -> attn_out
      t = y2 + attn_out -> LN1(g,b) -> xh -> FFN -> +xh -> LN2 -> ret
      X3 = x2 + ret -> x3_fm
"""
import numpy as np
from contextlib import ExitStack

H = 16
DH = 64
D = 1024
T = 2048
STRIP = 512
NT = 4
SC = 1.0 / np.sqrt(DH)
ISQ = 1.0 / np.sqrt(2048.0)


def emit_attnb(ctx, consts, long, dio, NB, upto=99):
    TT = NB * T
    emit_b0(ctx, consts, long, dio, NB)
    if upto < 1:
        return
    ctx.barrier_all()
    emit_b1(ctx, consts, long, dio, NB)
    if upto < 2:
        return
    ctx.barrier_all()
    emit_b2(ctx, consts, long, dio, NB, upto)


def emit_b0(ctx, consts, long, dio, NB):
    with ExitStack() as ph:
        sb = lambda n, s, d: Buf(n, ph.enter_context(ctx.nc.sbuf_tensor(n, list(s), d)))
        ps = lambda n, s, d: Buf(n, ph.enter_context(ctx.nc.psum_tensor(n, list(s), d)))
        b = {}
        b["wr"] = sb("b0_wr", (128, NB * 16, 8), F32)
        b["pl"] = [sb(f"b0_pl{i}", (128, 8, 1024), BF16) for i in range(2)]
        b["acc"] = sb("b0_acc", (128, 1024), F32)
        b["dtb"] = [sb(f"b0_dtb{i}", (128, 1024), BF16) for i in range(2)]
        b["xfm"] = [sb(f"b0_xfm{i}", (128, 8, STRIP), BF16) for i in range(2)]
        b["x"] = [sb(f"b0_x{i}", (128, 8, STRIP), F32) for i in range(2)]
        b["xb"] = [sb(f"b0_xb{i}", (128, 8, STRIP), BF16) for i in range(2)]
        b["sq"] = [sb(f"b0_sq{i}", (128, 8, STRIP), BF16) for i in range(2)]
        b["srow"] = [sb(f"b0_srow{i}", (1, 2 * STRIP), F32) for i in range(2)]
        b["murow"] = [sb(f"b0_mur{i}", (1, STRIP), F32) for i in range(2)]
        b["rstd"] = [sb(f"b0_rstd{i}", (1, STRIP), F32) for i in range(2)]
        b["pT"] = [ps(f"b0_pT{i}", (128, 1024), BF16) for i in range(2)]
        b["pAB"] = [ps(f"b0_pAB{i}", (128, 1024), F32) for i in range(2)]
        b["pR"] = [ps(f"b0_pR{i}", (1, 512), F32) for i in range(2)]
        ctx.dma("sync", b["wr"].t[:], dio["w_r"].t.ap()[:, :, :],
                reads=[dio["w_r"]], writes=[b["wr"]])
        for st in range(NB * 4):
            par = st % 2
            col0 = st * STRIP
            x, xfm = b["x"][par], b["xfm"][par]
            for hc in range(2):
                ctx.dma("gpsimd" if hc else "sync", x.t[:, hc * 4:(hc + 1) * 4, :],
                        dio["x1_fm"].t.ap()[hc * 4:(hc + 1) * 4, :, col0:col0 + STRIP]
                        .rearrange("c p k -> p c k"),
                        reads=[dio["x1_fm"]], writes=[x], tag=f"b0x{st}.{hc}")
            for ti4 in range(4):
                ti = st * 4 + ti4
                tp = ti % 2
                pl, dtb, pT = b["pl"][tp], b["dtb"][tp], b["pT"][tp]
                for hr in range(2):
                    ctx.dma("gpsimd" if hr else "sync", pl.t[:, hr * 4:(hr + 1) * 4, :],
                            dio["yplanes"].t.ap()[hr * 4:(hr + 1) * 4,
                                                  ti * 128:(ti + 1) * 128, :]
                            .rearrange("r t d -> t r d"),
                            reads=[dio["yplanes"]], writes=[pl], tag=f"b0pl{ti}.{hr}")
                ctx.step("vector", lambda e, ti=ti, pl=pl: e.tensor_scalar(
                    b["acc"].t[:], pl.t[:, 0, :], b["wr"].t[:, ti, 0:1], None, op0=ALU.mult),
                    reads=[pl, b["wr"]], writes=[b["acc"]])
                for r in range(1, 8):
                    ctx.step("vector", lambda e, ti=ti, r=r, pl=pl: e.scalar_tensor_tensor(
                        b["acc"].t[:], pl.t[:, r, :], b["wr"].t[:, ti, r:r + 1],
                        b["acc"].t[:], op0=ALU.mult, op1=ALU.add),
                        reads=[pl, b["wr"], b["acc"]], writes=[b["acc"]])
                ctx.step("vector", lambda e, dtb=dtb: e.tensor_copy(dtb.t[:], b["acc"].t[:]),
                         reads=[b["acc"]], writes=[dtb])
                def mmtr(e, dtb=dtb, pT=pT):
                    last = None
                    for dc in range(8):
                        last = e.transpose(pT.t[:, dc * 128:(dc + 1) * 128],
                                           dtb.t[:, dc * 128:(dc + 1) * 128],
                                           consts["ident128"].t[:, :])
                    return last
                ctx.step("tensor", mmtr, reads=[dtb, consts["ident128"]], writes=[pT],
                         tag=f"b0tr{ti}")
                ctx.step("vector", lambda e, ti4=ti4, pT=pT, xfm=xfm: e.tensor_copy(
                    xfm.t[:, :, ti4 * 128:(ti4 + 1) * 128],
                    pT.t[:].rearrange("p (c k) -> p c k", c=8)),
                    reads=[pT], writes=[xfm], tag=f"b0ev{ti}")
            xb, sq = b["xb"][par], b["sq"][par]
            for dc in range(8):
                ctx.step("vector", lambda e, dc=dc, x=x, xfm=xfm: e.tensor_tensor(
                    x.t[:, dc, :], x.t[:, dc, :], xfm.t[:, dc, :], op=ALU.add),
                    reads=[x, xfm], writes=[x])
            ctx.dma("scalar", dio["x2_d"].t.ap()[:, :, col0:col0 + STRIP]
                    .rearrange("c p k -> p c k"), x.t[:, :, :],
                    reads=[x], writes=[dio["x2_d"]], tag=f"b0sx2{st}")
            ctx.step("vector", lambda e, x=x, xb=xb: e.tensor_copy(xb.t[:], x.t[:]),
                     reads=[x], writes=[xb])
            ctx.step("scalar", lambda e, xb=xb, sq=sq: e.activation(
                out=sq.t[:], in_=xb.t[:], func=AF.Square),
                reads=[xb], writes=[sq])
            pR, srow = b["pR"][par], b["srow"][par]
            for which, src2 in (("x", xb), ("x2", sq)):
                def mmst(e, src2=src2, pR=pR):
                    last = None
                    for dc in range(8):
                        last = e.matmul(pR.t[0:1, 0:STRIP], long["ones_col"].t[:, 0:1],
                                        src2.t[:, dc, :], start=(dc == 0), stop=(dc == 7))
                    return last
                ctx.step("tensor", mmst, reads=[src2, long["ones_col"]], writes=[pR],
                         tag=f"b0st{st}.{which}")
                off = 0 if which == "x" else STRIP
                ctx.step("vector", lambda e, off=off, pR=pR, srow=srow: e.tensor_copy(
                    srow.t[0:1, off:off + STRIP], pR.t[0:1, 0:STRIP]),
                    reads=[pR], writes=[srow])
            murow, rstd = b["murow"][par], b["rstd"][par]
            rowmath_ln(ctx, (srow, srow.t[0:1, 0:STRIP]),
                             (srow, srow.t[0:1, STRIP:2 * STRIP]),
                             (murow, murow.t[:]), (rstd, rstd.t[:]))
            pAB = b["pAB"][par]
            def mmab(e, pAB=pAB, rstd=rstd, murow=murow):
                e.matmul(pAB.t[:, 0:512], long["ones_rowf"].t[0:1, :],
                         rstd.t[0:1, :], start=True, stop=True)
                return e.matmul(pAB.t[:, 512:1024], long["ones_rowf"].t[0:1, :],
                                murow.t[0:1, :], start=True, stop=True)
            ctx.step("tensor", mmab, reads=[rstd, murow, long["ones_rowf"]],
                     writes=[pAB], tag=f"b0ab{st}")
            for dc in range(8):
                ctx.step("vector", lambda e, dc=dc, x=x, pAB=pAB: e.tensor_tensor(
                    x.t[:, dc, :], x.t[:, dc, :], pAB.t[:, 512:1024], op=ALU.add),
                    reads=[x, pAB], writes=[x])
                ctx.step("vector", lambda e, dc=dc, x=x, pAB=pAB: e.tensor_tensor(
                    x.t[:, dc, :], x.t[:, dc, :], pAB.t[:, 0:512], op=ALU.mult),
                    reads=[x, pAB], writes=[x])
            ctx.dma("scalar", dio["y2_d"].t.ap()[:, :, col0:col0 + STRIP]
                    .rearrange("c p k -> p c k"), x.t[:, :, :],
                    reads=[x], writes=[dio["y2_d"]], tag=f"b0sy2{st}")


def emit_b1(ctx, consts, long, dio, NB):
    with ExitStack() as ph:
        sb = lambda n, s, d: Buf(n, ph.enter_context(ctx.nc.sbuf_tensor(n, list(s), d)))
        ps = lambda n, s, d: Buf(n, ph.enter_context(ctx.nc.psum_tensor(n, list(s), d)))
        Wq = sb("b1_Wq", (128, 8, 1024), BF16)
        Wk = sb("b1_Wk", (128, 8, 1024), BF16)
        Wv = sb("b1_Wv", (128, 8, 1024), BF16)
        for dc in range(8):
            ctx.dma("gpsimd", Wq.t[:, dc, :], dio["in_proj_t"].t.ap()[dc * 128:(dc + 1) * 128, 0:1024],
                    reads=[dio["in_proj_t"]], writes=[Wq], tag=f"b1wq{dc}")
            ctx.dma("gpsimd", Wk.t[:, dc, :], dio["in_proj_t"].t.ap()[dc * 128:(dc + 1) * 128, 1024:2048],
                    reads=[dio["in_proj_t"]], writes=[Wk], tag=f"b1wk{dc}")
            ctx.dma("gpsimd", Wv.t[:, dc, :], dio["in_proj_t"].t.ap()[dc * 128:(dc + 1) * 128, 2048:3072],
                    reads=[dio["in_proj_t"]], writes=[Wv], tag=f"b1wv{dc}")
        bq = sb("b1_bq", (128, 8), F32)
        bk = sb("b1_bk", (128, 8), F32)
        bvr = sb("b1_bvr", (1, 1024), F32)
        btmp = sb("b1_btmp", (1, 1024), F32)
        pC = ps("b1_pS0", (128, 1024), F32)
        for (name, off, dst) in (("bq", 0, bq), ("bk", 1024, bk)):
            ctx.dma("sync", btmp.t[:], dio["in_proj_b"].t.ap()[0:1, off:off + 1024],
                    reads=[dio["in_proj_b"]], writes=[btmp], tag=f"b1{name}")
            def mmc(e):
                last = None
                for dc in range(8):
                    last = e.matmul(pC.t[:, dc:dc + 1], btmp.t[0:1, dc * 128:(dc + 1) * 128],
                                    long["ones_rowf"].t[:, 0:1], start=True, stop=True)
                return last
            ctx.step("tensor", mmc, reads=[btmp, long["ones_rowf"]], writes=[pC], tag=f"b1c{name}")
            ctx.step("vector", lambda e, dst=dst: e.tensor_copy(dst.t[:], pC.t[:, 0:8]),
                     reads=[pC], writes=[dst])
        ctx.dma("sync", bvr.t[:], dio["in_proj_b"].t.ap()[0:1, 2048:3072],
                reads=[dio["in_proj_b"]], writes=[bvr], tag="b1bv")
        bvb = sb("b1_bvb", (1, 1024), BF16)
        ctx.step("vector", lambda e: e.tensor_copy(bvb.t[:], bvr.t[:]), reads=[bvr], writes=[bvb])

        b = {}
        b["y2b"] = sb("b1_y2b", (128, 8, STRIP), BF16)
        b["qf"] = sb("b1_qf", (128, 8, T), BF16)
        b["kf"] = sb("b1_kf", (128, 8, T), BF16)
        b["vt"] = sb("b1_vt", (128, 16, 1024), BF16)
        b["vs"] = [[sb(f"b1_vs{i}_{j}", (128, DH), BF16) for j in range(2)] for i in range(2)]
        b["A"] = [[sb(f"b1_A{i}_{j}", (128, T), BF16) for j in range(2)] for i in range(2)]
        b["z"] = [[sb(f"b1_z{i}_{j}", (128, 4), F32) for j in range(2)] for i in range(2)]
        b["zr"] = [[sb(f"b1_zr{i}_{j}", (128, 1), F32) for j in range(2)] for i in range(2)]
        b["postf"] = [sb(f"b1_postf{i}", (128, T), BF16) for i in range(2)]
        b["pS"] = [pC, ps("b1_pS1", (128, 1024), F32)]
        b["pP"] = ps("b1_pP", (128, 2048), F32)
        for bi in range(NB):
            emit_b1_b(ctx, consts, long, dio, b, Wq, Wk, Wv, bq, bk, bvb, bi)


def emit_b1_b(ctx, consts, long, dio, b, Wq, Wk, Wv, bq, bk, bvb, bi):
    for si in range(NT):
        col0 = bi * T + si * STRIP
        for hc in range(2):
            ctx.dma("gpsimd", b["y2b"].t[:, hc * 4:(hc + 1) * 4, :],
                    dio["y2_d"].t.ap()[hc * 4:(hc + 1) * 4, :, col0:col0 + STRIP]
                    .rearrange("c p k -> p c k"),
                    reads=[dio["y2_d"]], writes=[b["y2b"]], tag=f"b1y{bi}.{si}.{hc}")
        for fc in range(8):
            for wj, (which, W, bias) in enumerate((("q", Wq, bq), ("k", Wk, bk))):
                pS = b["pS"][(fc * 2 + wj) % 2]
                def mmqk(e, fc=fc, W=W, si=si, pS=pS):
                    last = None
                    for dc in range(8):
                        last = e.matmul(pS.t[:, 0:STRIP],
                                        W.t[:, dc, fc * 128:(fc + 1) * 128],
                                        b["y2b"].t[:, dc, :], start=(dc == 0), stop=(dc == 7))
                    return last
                ctx.step("tensor", mmqk, reads=[W, b["y2b"]], writes=[pS],
                         tag=f"b1{which}{bi}.{si}.{fc}")
                dst = b["qf"] if which == "q" else b["kf"]
                bb = bq if which == "q" else bk
                ctx.step("vector", lambda e, fc=fc, si=si, dst=dst, bb=bb, pS=pS: e.tensor_scalar(
                    dst.t[:, fc, si * STRIP:(si + 1) * STRIP], pS.t[:, 0:STRIP],
                    bb.t[:, fc:fc + 1], None, op0=ALU.add),
                    reads=[pS, bb], writes=[dst])
        for tt in range(NT):
            ti = si * NT + tt
            pS = b["pS"][tt % 2]
            def mmv(e, tt=tt, pS=pS):
                last = None
                for dc in range(8):
                    lhsT = b["y2b"].t[:, dc, tt * 128:(tt + 1) * 128]
                    for half in range(2):
                        last = e.matmul(pS.t[:, half * 512:(half + 1) * 512], lhsT,
                                        Wv.t[:, dc, half * 512:(half + 1) * 512],
                                        start=(dc == 0), stop=False)
                for half in range(2):
                    last = e.matmul(pS.t[:, half * 512:(half + 1) * 512],
                                    long["ones_row"].t[0:1, :],
                                    bvb.t[0:1, half * 512:(half + 1) * 512],
                                    start=False, stop=True)
                return last
            ctx.step("tensor", mmv, reads=[b["y2b"], Wv, long["ones_row"], bvb],
                     writes=[pS], tag=f"b1v{bi}.{si}.{tt}")
            ctx.step("vector", lambda e, ti=ti, pS=pS: e.tensor_copy(
                b["vt"].t[:, ti, :], pS.t[:, 0:1024]),
                reads=[pS], writes=[b["vt"]])
    for hp in range(H // 2):
        fc = hp
        for sblk in range(17):
            par = sblk % 2

            def emit_th(th, fc=fc, sblk=sblk, par=par):
                def mms2(e, fc=fc, sblk=sblk, th=th):
                    last = None
                    for tq in range(2):
                        for hi in range(2):
                            hb = hi * 64
                            last = e.matmul(
                                b["pS"][hi].t[:, tq * 512:(tq + 1) * 512],
                                b["kf"].t[hb:hb + 64, fc, sblk * 128:(sblk + 1) * 128],
                                b["qf"].t[hb:hb + 64, fc, th * 1024 + tq * 512:th * 1024 + (tq + 1) * 512],
                                start=True, stop=True, tile_position=(hb, 0))
                    return last
                ctx.step("tensor", mms2, reads=[b["kf"], b["qf"]],
                         writes=[b["pS"][0], b["pS"][1]], tag=f"b1s{bi}.{hp}.{sblk}.{th}")
                for hi in range(2):
                    A, z = b["A"][hi][par], b["z"][hi][par]
                    ctx.step("scalar", lambda e, th=th, hi=hi, A=A, z=z: e.activation(
                        out=A.t[:, th * 1024:(th + 1) * 1024],
                        in_=b["pS"][hi].t[:, 0:1024],
                        func=AF.Exp, scale=SC,
                        accum_out=z.t[:, th * 2:th * 2 + 1]),
                        reads=[b["pS"][hi]], writes=[A, z],
                        tag=f"b1e{bi}.{hp}.{sblk}.{th}.{hi}")

            def emit_mmp_half(half, sblk=sblk):
                pb = sblk - 1
                pp = pb % 2
                def mmp(e, pb=pb, pp=pp, half=half):
                    last = None
                    for tq in (2 * half, 2 * half + 1):
                        for hi in range(2):
                            hb = hi * 64
                            last = e.matmul(
                                b["pP"].t[hb:hb + 64, tq * 512:(tq + 1) * 512],
                                b["vs"][hi][pp].t[:, :],
                                b["A"][hi][pp].t[:, tq * 512:(tq + 1) * 512],
                                start=(pb == 0), stop=(pb == 15),
                                tile_position=(0, hb))
                    return last
                ctx.step("tensor", mmp,
                         reads=[b["vs"][0][pp], b["vs"][1][pp], b["A"][0][pp], b["A"][1][pp]],
                         writes=[b["pP"]], tag=f"b1p{bi}.{hp}.{sblk - 1}.{half}")

            if sblk < 16:
                emit_th(0)
            if sblk >= 1:
                emit_mmp_half(0)
            if sblk < 16:
                emit_th(1)
                for hi in range(2):
                    z, zr, vs = b["z"][hi][par], b["zr"][hi][par], b["vs"][hi][par]
                    h = 2 * hp + hi
                    ctx.step("vector", lambda e, z=z, zr=zr: e.tensor_tensor(
                        zr.t[:], z.t[:, 0:1], z.t[:, 2:3], op=ALU.add),
                        reads=[z], writes=[zr])
                    ctx.step("vector", lambda e, zr=zr: e.reciprocal(zr.t[:], zr.t[:]),
                             reads=[zr], writes=[zr])
                    ctx.step("vector", lambda e, h=h, sblk=sblk, zr=zr, vs=vs: e.tensor_scalar(
                        vs.t[:], b["vt"].t[:, sblk, h * 64:(h + 1) * 64],
                        zr.t[:, 0:1], None, op0=ALU.mult),
                        reads=[b["vt"], zr], writes=[vs])
            if sblk >= 1:
                emit_mmp_half(1)
        pf = b["postf"][hp % 2]
        for hi in range(2):
            hb = hi * 64
            ctx.step("scalar", lambda e, hb=hb, pf=pf: e.activation(
                out=pf.t[hb:hb + 64, :], in_=b["pP"].t[hb:hb + 64, 0:T],
                func=AF.Gelu),
                reads=[b["pP"]], writes=[pf], tag=f"b1g{bi}.{hp}.{hi}")
        ctx.dma("sync", dio["post_d"].t.ap()[fc, :, bi * T:(bi + 1) * T],
                pf.t[:, :], reads=[pf], writes=[dio["post_d"]], tag=f"b1sp{bi}.{fc}")


def emit_b2(ctx, consts, long, dio, NB, upto=99):
    emit_b2a(ctx, consts, long, dio, NB)
    if upto < 3:
        return
    ctx.barrier_all()
    emit_b2b(ctx, consts, long, dio, NB)


def emit_b2a(ctx, consts, long, dio, NB):
    with ExitStack() as ph:
        sb = lambda n, s, d: Buf(n, ph.enter_context(ctx.nc.sbuf_tensor(n, list(s), d)))
        ps = lambda n, s, d: Buf(n, ph.enter_context(ctx.nc.psum_tensor(n, list(s), d)))
        Wo = sb("b2_Wo", (128, 8, 1024), BF16)
        for dc in range(8):
            ctx.dma("gpsimd", Wo.t[:, dc, :], dio["out_w_t"].t.ap()[dc * 128:(dc + 1) * 128, :],
                    reads=[dio["out_w_t"]], writes=[Wo], tag=f"b2wo{dc}")
        ctx.step("vector", lambda e: e.tensor_scalar(
            Wo.t[:], Wo.t[:], ISQ, None, op0=ALU.mult), reads=[Wo], writes=[Wo])
        cols = {}
        pC = ps("b2_pC", (128, 512), F32)
        for n in ("ob", "g1", "bb1"):
            cols[n] = sb("b2_c" + n, (128, 8), F32)
        with ExitStack() as rph:
            rows = {}
            for n, dn, width in (("ob", "out_b", 1024), ("g1", "ln1_g", 1024),
                                 ("bb1", "ln1_b", 1024)):
                rows[n] = Buf("b2_r" + n, rph.enter_context(
                    ctx.nc.sbuf_tensor("b2_r" + n, [1, width], F32)))
                ctx.dma("sync", rows[n].t[:], dio[dn].t.ap()[0:1, :], reads=[dio[dn]],
                        writes=[rows[n]], tag=f"b2r{n}")
            for n in ("ob", "g1", "bb1"):
                def mmc(e, n=n):
                    last = None
                    for c in range(8):
                        last = e.matmul(pC.t[:, c:c + 1], rows[n].t[0:1, c * 128:(c + 1) * 128],
                                        long["ones_rowf"].t[:, 0:1], start=True, stop=True)
                    return last
                ctx.step("tensor", mmc, reads=[rows[n], long["ones_rowf"]], writes=[pC], tag=f"b2c{n}")
                ctx.step("vector", lambda e, n=n: e.tensor_copy(cols[n].t[:], pC.t[:, 0:8]),
                         reads=[pC], writes=[cols[n]])
            ctx.barrier_all()
        b = {}
        b["post"] = [sb(f"b2_post{i}", (128, 8, STRIP), BF16) for i in range(2)]
        b["y2"] = [sb(f"b2_y2{i}", (128, 8, STRIP), F32) for i in range(2)]
        b["t"] = [sb(f"b2_t{i}", (128, 8, STRIP), F32) for i in range(2)]
        b["tb"] = [sb(f"b2_tb{i}", (128, 8, STRIP), BF16) for i in range(2)]
        b["srow"] = [sb(f"b2_srow{i}", (1, 2 * STRIP), F32) for i in range(2)]
        b["murow"] = [sb(f"b2_mur{i}", (1, STRIP), F32) for i in range(2)]
        b["rstd"] = [sb(f"b2_rstd{i}", (1, STRIP), F32) for i in range(2)]
        b["pF"] = [ps(f"b2_pF{i}", (128, STRIP), F32) for i in range(2)]
        b["pAB"] = [ps(f"b2_pAB{i}", (128, 1024), F32) for i in range(2)]
        b["pR"] = ps("b2_pR", (1, 512), F32)
        for st in range(NB * 4):
            par = st % 2
            col0 = st * STRIP
            post, y2, t, tb = b["post"][par], b["y2"][par], b["t"][par], b["tb"][par]
            srow, murow, rstd = b["srow"][par], b["murow"][par], b["rstd"][par]
            pAB = b["pAB"][par]
            for hc in range(2):
                ctx.dma("gpsimd" if hc else "sync", post.t[:, hc * 4:(hc + 1) * 4, :],
                        dio["post_d"].t.ap()[hc * 4:(hc + 1) * 4, :, col0:col0 + STRIP]
                        .rearrange("c p k -> p c k"),
                        reads=[dio["post_d"]], writes=[post], tag=f"b2p{st}.{hc}")
                ctx.dma("gpsimd" if hc else "sync", y2.t[:, hc * 4:(hc + 1) * 4, :],
                        dio["y2_d"].t.ap()[hc * 4:(hc + 1) * 4, :, col0:col0 + STRIP]
                        .rearrange("c p k -> p c k"),
                        reads=[dio["y2_d"]], writes=[y2], tag=f"b2y{st}.{hc}")
            for dc in range(8):
                pF = b["pF"][dc % 2]
                def mmo(e, dc=dc, pF=pF, post=post):
                    last = None
                    for vc in range(8):
                        last = e.matmul(pF.t[:, 0:STRIP], Wo.t[:, vc, dc * 128:(dc + 1) * 128],
                                        post.t[:, vc, :], start=(vc == 0), stop=(vc == 7))
                    return last
                ctx.step("tensor", mmo, reads=[Wo, post], writes=[pF], tag=f"b2o{st}.{dc}")
                ctx.step("vector", lambda e, dc=dc, pF=pF, t=t: e.tensor_scalar(
                    t.t[:, dc, :], pF.t[:, 0:STRIP], cols["ob"].t[:, dc:dc + 1], None,
                    op0=ALU.add),
                    reads=[pF, cols["ob"]], writes=[t])
                ctx.step("vector", lambda e, dc=dc, t=t, y2=y2: e.tensor_tensor(
                    t.t[:, dc, :], t.t[:, dc, :], y2.t[:, dc, :], op=ALU.add),
                    reads=[t, y2], writes=[t])
            ctx.step("vector", lambda e, t=t, tb=tb: e.tensor_copy(tb.t[:], t.t[:]),
                     reads=[t], writes=[tb])
            def mms1(e, tb=tb):
                last = None
                for dc in range(8):
                    last = e.matmul(b["pR"].t[0:1, 0:STRIP], long["ones_col"].t[:, 0:1],
                                    tb.t[:, dc, :], start=(dc == 0), stop=(dc == 7))
                return last
            ctx.step("tensor", mms1, reads=[tb, long["ones_col"]], writes=[b["pR"]],
                     tag=f"b2s1{st}")
            ctx.step("vector", lambda e, srow=srow: e.tensor_copy(
                srow.t[0:1, 0:STRIP], b["pR"].t[0:1, 0:STRIP]),
                reads=[b["pR"]], writes=[srow])
            ctx.step("scalar", lambda e, tb=tb: e.activation(
                out=tb.t[:], in_=tb.t[:], func=AF.Square),
                reads=[tb], writes=[tb])
            def mms2(e, tb=tb):
                last = None
                for dc in range(8):
                    last = e.matmul(b["pR"].t[0:1, 0:STRIP], long["ones_col"].t[:, 0:1],
                                    tb.t[:, dc, :], start=(dc == 0), stop=(dc == 7))
                return last
            ctx.step("tensor", mms2, reads=[tb, long["ones_col"]], writes=[b["pR"]],
                     tag=f"b2s2{st}")
            ctx.step("vector", lambda e, srow=srow: e.tensor_copy(
                srow.t[0:1, STRIP:2 * STRIP], b["pR"].t[0:1, 0:STRIP]),
                reads=[b["pR"]], writes=[srow])
            rowmath_ln(ctx, (srow, srow.t[0:1, 0:STRIP]),
                             (srow, srow.t[0:1, STRIP:2 * STRIP]),
                             (murow, murow.t[:]), (rstd, rstd.t[:]))
            def mmab(e, pAB=pAB, rstd=rstd, murow=murow):
                e.matmul(pAB.t[:, 0:512], long["ones_rowf"].t[0:1, :], rstd.t[0:1, :],
                         start=True, stop=True)
                return e.matmul(pAB.t[:, 512:1024], long["ones_rowf"].t[0:1, :],
                                murow.t[0:1, :], start=True, stop=True)
            ctx.step("tensor", mmab, reads=[rstd, murow, long["ones_rowf"]],
                     writes=[pAB], tag=f"b2ab{st}")
            for dc in range(8):
                ctx.step("vector", lambda e, dc=dc, y2=y2, t=t, pAB=pAB: e.tensor_tensor(
                    y2.t[:, dc, :], t.t[:, dc, :], pAB.t[:, 512:1024], op=ALU.add),
                    reads=[t, pAB], writes=[y2])
                ctx.step("vector", lambda e, dc=dc, y2=y2, pAB=pAB: e.tensor_tensor(
                    y2.t[:, dc, :], y2.t[:, dc, :], pAB.t[:, 0:512], op=ALU.mult),
                    reads=[y2, pAB], writes=[y2])
                ctx.step("vector", lambda e, dc=dc, y2=y2: e.tensor_scalar(
                    y2.t[:, dc, :], y2.t[:, dc, :], cols["g1"].t[:, dc:dc + 1],
                    None, op0=ALU.mult),
                    reads=[y2, cols["g1"]], writes=[y2])
                ctx.step("vector", lambda e, dc=dc, y2=y2: e.tensor_scalar(
                    y2.t[:, dc, :], y2.t[:, dc, :], cols["bb1"].t[:, dc:dc + 1],
                    None, op0=ALU.add),
                    reads=[y2, cols["bb1"]], writes=[y2])
            ctx.dma("scalar", dio["xh_d"].t.ap()[:, :, col0:col0 + STRIP]
                    .rearrange("c p k -> p c k"), y2.t[:, :, :],
                    reads=[y2], writes=[dio["xh_d"]], tag=f"b2sxh{st}")


def emit_b2b(ctx, consts, long, dio, NB):
    with ExitStack() as ph:
        sb = lambda n, s, d: Buf(n, ph.enter_context(ctx.nc.sbuf_tensor(n, list(s), d)))
        ps = lambda n, s, d: Buf(n, ph.enter_context(ctx.nc.psum_tensor(n, list(s), d)))
        W1 = sb("b2_W1", (128, 8, 4096), BF16)
        W2 = sb("b2_W2", (128, 32, 1024), BF16)
        for dc in range(8):
            ctx.dma("gpsimd", W1.t[:, dc, :], dio["lin1_t"].t.ap()[dc * 128:(dc + 1) * 128, :],
                    reads=[dio["lin1_t"]], writes=[W1], tag=f"b2w1{dc}")
        for fc in range(32):
            ctx.dma("gpsimd", W2.t[:, fc, :], dio["lin2_t"].t.ap()[fc * 128:(fc + 1) * 128, :],
                    reads=[dio["lin2_t"]], writes=[W2], tag=f"b2w2{fc}")
        cols = {}
        pC = ps("b2b_pC", (128, 512), F32)
        for n, width in (("b1", 4096), ("b2", 1024), ("g2", 1024), ("bb2", 1024)):
            cols[n] = sb("b2b_c" + n, (128, width // 128), F32)
        with ExitStack() as rph:
            rows = {}
            for n, dn, width in (("b1", "lin1_b", 4096), ("b2", "lin2_b", 1024),
                                 ("g2", "ln2_g", 1024), ("bb2", "ln2_b", 1024)):
                rows[n] = Buf("b2b_r" + n, rph.enter_context(
                    ctx.nc.sbuf_tensor("b2b_r" + n, [1, width], F32)))
                ctx.dma("sync", rows[n].t[:], dio[dn].t.ap()[0:1, :], reads=[dio[dn]],
                        writes=[rows[n]], tag=f"b2br{n}")
            for n, width in (("b1", 4096), ("b2", 1024), ("g2", 1024), ("bb2", 1024)):
                nch = width // 128
                def mmc2(e, n=n, nch=nch):
                    last = None
                    for c in range(nch):
                        last = e.matmul(pC.t[:, c:c + 1],
                                        rows[n].t[0:1, c * 128:(c + 1) * 128],
                                        long["ones_rowf"].t[:, 0:1], start=True, stop=True)
                    return last
                ctx.step("tensor", mmc2, reads=[rows[n], long["ones_rowf"]], writes=[pC], tag=f"b2bc{n}")
                ctx.step("vector", lambda e, n=n, nch=nch: e.tensor_copy(cols[n].t[:], pC.t[:, 0:nch]),
                         reads=[pC], writes=[cols[n]])
            ctx.barrier_all()
        b = {}
        b["xhf"] = sb("b2b_xhf", (128, 8, STRIP), F32)
        b["x2f"] = sb("b2b_x2f", (128, 8, STRIP), F32)
        b["xhb"] = sb("b2b_xhb", (128, 8, STRIP), BF16)
        b["ff"] = sb("b2b_ff", (128, 16, STRIP), BF16)
        b["ub"] = b["xhb"]
        b["srow"] = sb("b2b_srow", (1, 2 * STRIP), F32)
        b["murow"] = sb("b2b_mur", (1, STRIP), F32)
        b["rstd"] = sb("b2b_rstd", (1, STRIP), F32)
        b["pF"] = [ps(f"b2b_pF{i}", (128, STRIP), F32) for i in range(2)]
        b["pW"] = [ps(f"b2b_pW{i}", (128, STRIP), F32) for i in range(2)]
        b["pAB"] = ps("b2b_pAB", (128, 1024), F32)
        b["pR"] = ps("b2b_pR", (1, 512), F32)
        for st in range(NB * 4):
            emit_b2b_strip(ctx, long, dio, b, W1, W2, cols, st)


def emit_b2b_strip(ctx, long, dio, b, W1, W2, cols, st):
    col0 = st * STRIP
    for hc in range(2):
        ctx.dma("gpsimd" if hc else "sync", b["xhf"].t[:, hc * 4:(hc + 1) * 4, :],
                dio["xh_d"].t.ap()[hc * 4:(hc + 1) * 4, :, col0:col0 + STRIP]
                .rearrange("c p k -> p c k"),
                reads=[dio["xh_d"]], writes=[b["xhf"]], tag=f"b2bxh{st}.{hc}")
    ctx.dma("scalar", b["x2f"].t[:, :, :],
            dio["x2_d"].t.ap()[:, :, col0:col0 + STRIP].rearrange("c p k -> p c k"),
            reads=[dio["x2_d"]], writes=[b["x2f"]], tag=f"b2bx2{st}")
    ctx.step("vector", lambda e: e.tensor_copy(b["xhb"].t[:], b["xhf"].t[:]),
             reads=[b["xhf"]], writes=[b["xhb"]])
    for half in range(2):
        for fc2 in range(16):
            fc = half * 16 + fc2
            pF = b["pF"][fc2 % 2]
            def mmf(e, fc=fc, pF=pF):
                last = None
                for dc in range(8):
                    last = e.matmul(pF.t[:, 0:STRIP], W1.t[:, dc, fc * 128:(fc + 1) * 128],
                                    b["xhb"].t[:, dc, :], start=(dc == 0), stop=(dc == 7))
                return last
            ctx.step("tensor", mmf, reads=[W1, b["xhb"]], writes=[pF], tag=f"b2f{st}.{fc}")
            ctx.step("scalar", lambda e, fc=fc, fc2=fc2, pF=pF: e.activation(
                out=b["ff"].t[:, fc2, :], in_=pF.t[:, 0:STRIP], func=AF.Gelu,
                bias=cols["b1"].t[:, fc:fc + 1]),
                reads=[pF, cols["b1"]], writes=[b["ff"]])
        for dc in range(8):
            pW = b["pW"][dc % 2]
            def mmw2(e, dc=dc, half=half, pW=pW):
                last = None
                for fc2 in range(16):
                    last = e.matmul(pW.t[:, 0:STRIP],
                                    W2.t[:, half * 16 + fc2, dc * 128:(dc + 1) * 128],
                                    b["ff"].t[:, fc2, :], start=(fc2 == 0), stop=(fc2 == 15))
                return last
            ctx.step("tensor", mmw2, reads=[W2, b["ff"]], writes=[pW], tag=f"b2l2{st}.{half}.{dc}")
            ctx.step("vector", lambda e, dc=dc, pW=pW: e.tensor_tensor(
                b["xhf"].t[:, dc, :], b["xhf"].t[:, dc, :], pW.t[:, 0:STRIP], op=ALU.add),
                reads=[b["xhf"], pW], writes=[b["xhf"]])
    for dc in range(8):
        ctx.step("vector", lambda e, dc=dc: e.tensor_scalar(
            b["xhf"].t[:, dc, :], b["xhf"].t[:, dc, :], cols["b2"].t[:, dc:dc + 1], None,
            op0=ALU.add),
            reads=[b["xhf"], cols["b2"]], writes=[b["xhf"]])
    ctx.step("vector", lambda e: e.tensor_copy(b["ub"].t[:], b["xhf"].t[:]),
             reads=[b["xhf"]], writes=[b["ub"]])
    def mms1(e):
        last = None
        for dc in range(8):
            last = e.matmul(b["pR"].t[0:1, 0:STRIP], long["ones_col"].t[:, 0:1],
                            b["ub"].t[:, dc, :], start=(dc == 0), stop=(dc == 7))
        return last
    ctx.step("tensor", mms1, reads=[b["ub"], long["ones_col"]], writes=[b["pR"]], tag=f"b2bs1{st}")
    ctx.step("vector", lambda e: e.tensor_copy(b["srow"].t[0:1, 0:STRIP], b["pR"].t[0:1, 0:STRIP]),
             reads=[b["pR"]], writes=[b["srow"]])
    ctx.step("scalar", lambda e: e.activation(
        out=b["ub"].t[:], in_=b["ub"].t[:], func=AF.Square),
        reads=[b["ub"]], writes=[b["ub"]])
    def mms2(e):
        last = None
        for dc in range(8):
            last = e.matmul(b["pR"].t[0:1, 0:STRIP], long["ones_col"].t[:, 0:1],
                            b["ub"].t[:, dc, :], start=(dc == 0), stop=(dc == 7))
        return last
    ctx.step("tensor", mms2, reads=[b["ub"], long["ones_col"]], writes=[b["pR"]], tag=f"b2bs2{st}")
    ctx.step("vector", lambda e: e.tensor_copy(b["srow"].t[0:1, STRIP:2 * STRIP], b["pR"].t[0:1, 0:STRIP]),
             reads=[b["pR"]], writes=[b["srow"]])
    rowmath_ln(ctx, (b["srow"], b["srow"].t[0:1, 0:STRIP]),
                     (b["srow"], b["srow"].t[0:1, STRIP:2 * STRIP]),
                     (b["murow"], b["murow"].t[:]), (b["rstd"], b["rstd"].t[:]))
    def mmab2(e):
        e.matmul(b["pAB"].t[:, 0:512], long["ones_rowf"].t[0:1, :], b["rstd"].t[0:1, :],
                 start=True, stop=True)
        return e.matmul(b["pAB"].t[:, 512:1024], long["ones_rowf"].t[0:1, :],
                        b["murow"].t[0:1, :], start=True, stop=True)
    ctx.step("tensor", mmab2, reads=[b["rstd"], b["murow"], long["ones_rowf"]],
             writes=[b["pAB"]], tag=f"b2ba{st}")
    for dc in range(8):
        ctx.step("vector", lambda e, dc=dc: e.tensor_tensor(
            b["xhf"].t[:, dc, :], b["xhf"].t[:, dc, :], b["pAB"].t[:, 512:1024], op=ALU.add),
            reads=[b["xhf"], b["pAB"]], writes=[b["xhf"]])
        ctx.step("vector", lambda e, dc=dc: e.tensor_tensor(
            b["xhf"].t[:, dc, :], b["xhf"].t[:, dc, :], b["pAB"].t[:, 0:512], op=ALU.mult),
            reads=[b["xhf"], b["pAB"]], writes=[b["xhf"]])
        ctx.step("vector", lambda e, dc=dc: e.tensor_scalar(
            b["xhf"].t[:, dc, :], b["xhf"].t[:, dc, :], cols["g2"].t[:, dc:dc + 1],
            None, op0=ALU.mult),
            reads=[b["xhf"], cols["g2"]], writes=[b["xhf"]])
        ctx.step("vector", lambda e, dc=dc: e.tensor_scalar(
            b["xhf"].t[:, dc, :], b["xhf"].t[:, dc, :], cols["bb2"].t[:, dc:dc + 1],
            None, op0=ALU.add),
            reads=[b["xhf"], cols["bb2"]], writes=[b["xhf"]])
        ctx.step("vector", lambda e, dc=dc: e.tensor_tensor(
            b["x2f"].t[:, dc, :], b["x2f"].t[:, dc, :], b["xhf"].t[:, dc, :], op=ALU.add),
            reads=[b["x2f"], b["xhf"]], writes=[b["x2f"]])
    ctx.dma("scalar", dio["x3_fm"].t.ap()[:, :, col0:col0 + STRIP]
            .rearrange("c p k -> p c k"), b["x2f"].t[:, :, :],
            reads=[b["x2f"]], writes=[dio["x3_fm"]], tag=f"b2bsx3{st}")

"""Full kernel assembly: 7-launch pipeline for nn_mk_hybrid_d1.

kernel(**inputs) -> np.ndarray (32, 2048, 1024) f32.
"""
import numpy as np
import ml_dtypes
from contextlib import ExitStack
import concourse.bass as bass
from concourse.bass_utils import run_bass_kernel_spmd

NCORES = 8
NB = 4
T = 2048
TT = NB * T
NTI = NB * 16
NE = 64
CAP = 256
NSLOT = NE * CAP
BF = ml_dtypes.bfloat16

_cache = {}


def _dram(nc, dio, name, shape, dtype, kind):
    dio[name] = Buf(name, nc.dram_tensor(name, list(shape), dtype, kind=kind))


def build_l1():
    nc = bass.Bass("TRN2", target_bir_lowering=False)
    dio = {}
    _dram(nc, dio, "x_fm", [8, 128, TT], F32, "ExternalInput")
    for n, shp in (("Wq", [1024, 1024]), ("Wk", [1024, 1024]), ("Wv", [1024, 1024]),
                   ("Wg", [1024, 1024]), ("Wo", [1024, 1024]),
                   ("w1", [1024, 2736]), ("w2", [1024, 2736]), ("w3", [2736, 1024]),
                   ("g_attn", [1, 1024]), ("g_ffn", [1, 1024]), ("gw_t", [1024, 64])):
        _dram(nc, dio, n, shp, F32, "ExternalInput")
    _dram(nc, dio, "x1_fm", [8, 128, TT], F32, "ExternalOutput")
    _dram(nc, dio, "r1w_out", [128, NTI, NE], F32, "ExternalOutput")
    _dram(nc, dio, "r1pos_out", [128, NTI, NE], F32, "ExternalOutput")
    _dram(nc, dio, "r1cnt_out", [1, NE], F32, "ExternalOutput")
    _dram(nc, dio, "r1ytm_d", [TT, 1024], BF16, "ExternalOutput")
    _dram(nc, dio, "ofm_d", [8, 128, TT], BF16, "Internal")
    _dram(nc, dio, "xpl_d", [8, 128, TT], F32, "Internal")
    _dram(nc, dio, "xlg2_d", [8, 128, TT], BF16, "Internal")
    _dram(nc, dio, "rows_d", [2, TT], F32, "Internal")
    _dram(nc, dio, "r1logits_d", [128, NTI, NE], F32, "Internal")
    with ExitStack() as stack:
        ctx = Ctx(nc, stack)
        consts = make_gla_consts(ctx)
        long = make_long(ctx)
        emit_l1(ctx, dio, long=long, consts=consts)
        ctx.barrier_all()
        dio["r1x_fm"] = dio["x1_fm"]
        emit_route(ctx, consts, long, dio, "r1", NB)
        ctx.finalize(final_waits=ctx.dma_final_waits())
    return nc


def build_l2():
    nc = bass.Bass("TRN2", target_bir_lowering=False)
    dio = {}
    _dram(nc, dio, "w_own", [128, NTI, NE], F32, "ExternalInput")
    _dram(nc, dio, "pos_own", [128, NTI, NE], F32, "ExternalInput")
    _dram(nc, dio, "cnt_all", [8, NE], F32, "ExternalInput")
    _dram(nc, dio, "coremask", [1, 8], F32, "ExternalInput")
    _dram(nc, dio, "addr_r", [128, NTI, 8], I32, "ExternalOutput")
    _dram(nc, dio, "w_r", [128, NTI, 8], F32, "ExternalOutput")
    with ExitStack() as stack:
        ctx = Ctx(nc, stack)
        long = make_long(ctx)
        emit_l2(ctx, long, dio, NB)
        ctx.finalize(final_waits=ctx.dma_final_waits())
    return nc


def build_l3():
    nc = bass.Bass("TRN2", target_bir_lowering=False)
    dio = {}
    _dram(nc, dio, "disp", [2048, 1024], BF16, "ExternalInput")
    _dram(nc, dio, "wg_t", [8, 1024, 768], F32, "ExternalInput")
    _dram(nc, dio, "wu_t", [8, 1024, 768], F32, "ExternalInput")
    _dram(nc, dio, "wd_t", [8, 768, 1024], F32, "ExternalInput")
    _dram(nc, dio, "y", [2048, 1024], BF16, "ExternalOutput")
    with ExitStack() as stack:
        ctx = Ctx(nc, stack)
        consts = make_gla_consts(ctx)
        long = make_long(ctx)
        emit_l3(ctx, consts, long, dio)
        ctx.finalize(final_waits=ctx.dma_final_waits())
    return nc


def build_l4():
    nc = bass.Bass("TRN2", target_bir_lowering=False)
    dio = {}
    _dram(nc, dio, "x1_fm", [8, 128, TT], F32, "ExternalInput")
    _dram(nc, dio, "yplanes", [8, TT, 1024], BF16, "ExternalInput")
    _dram(nc, dio, "w_r", [128, NTI, 8], F32, "ExternalInput")
    _dram(nc, dio, "in_proj_t", [1024, 3072], F32, "ExternalInput")
    _dram(nc, dio, "in_proj_b", [1, 3072], F32, "ExternalInput")
    _dram(nc, dio, "out_w_t", [1024, 1024], F32, "ExternalInput")
    _dram(nc, dio, "out_b", [1, 1024], F32, "ExternalInput")
    _dram(nc, dio, "lin1_t", [1024, 4096], F32, "ExternalInput")
    _dram(nc, dio, "lin1_b", [1, 4096], F32, "ExternalInput")
    _dram(nc, dio, "lin2_t", [4096, 1024], F32, "ExternalInput")
    _dram(nc, dio, "lin2_b", [1, 1024], F32, "ExternalInput")
    for n in ("ln1_g", "ln1_b", "ln2_g", "ln2_b"):
        _dram(nc, dio, n, [1, 1024], F32, "ExternalInput")
    _dram(nc, dio, "gw_t", [1024, 64], F32, "ExternalInput")
    _dram(nc, dio, "x3_fm", [8, 128, TT], F32, "ExternalOutput")
    _dram(nc, dio, "r3w_out", [128, NTI, NE], F32, "ExternalOutput")
    _dram(nc, dio, "r3pos_out", [128, NTI, NE], F32, "ExternalOutput")
    _dram(nc, dio, "r3cnt_out", [1, NE], F32, "ExternalOutput")
    _dram(nc, dio, "r3ytm_d", [TT, 1024], BF16, "ExternalOutput")
    _dram(nc, dio, "delta_d", [8, 128, TT], BF16, "Internal")
    _dram(nc, dio, "x2_d", [8, 128, TT], F32, "Internal")
    _dram(nc, dio, "y2_d", [8, 128, TT], F32, "Internal")
    _dram(nc, dio, "post_d", [8, 128, TT], BF16, "Internal")
    _dram(nc, dio, "xh_d", [8, 128, TT], F32, "Internal")
    _dram(nc, dio, "r3logits_d", [128, NTI, NE], F32, "Internal")
    with ExitStack() as stack:
        ctx = Ctx(nc, stack)
        consts = make_gla_consts(ctx)
        long = make_long(ctx)
        emit_attnb(ctx, consts, long, dio, NB)
        ctx.barrier_all()
        dio["r3x_fm"] = dio["x3_fm"]
        emit_route(ctx, consts, long, dio, "r3", NB)
        ctx.finalize(final_waits=ctx.dma_final_waits())
    return nc


def build_l7():
    nc = bass.Bass("TRN2", target_bir_lowering=False)
    dio = {}
    _dram(nc, dio, "x3_fm", [8, 128, TT], F32, "ExternalInput")
    _dram(nc, dio, "yplanes", [8, TT, 1024], BF16, "ExternalInput")
    _dram(nc, dio, "w_r", [128, NTI, 8], F32, "ExternalInput")
    _dram(nc, dio, "x4_fm", [8, 128, TT], F32, "ExternalOutput")
    with ExitStack() as stack:
        ctx = Ctx(nc, stack)
        consts = make_gla_consts(ctx)
        with ExitStack() as ph:
            sb = lambda n, s, d: Buf(n, ph.enter_context(nc.sbuf_tensor(n, list(s), d)))
            ps = lambda n, s, d: Buf(n, ph.enter_context(nc.psum_tensor(n, list(s), d)))
            b = {}
            b["wr"] = sb("l7wr", (128, NTI, 8), F32)
            b["pl"] = [sb(f"l7pl{i}", (128, 8, 1024), BF16) for i in range(2)]
            b["acc"] = sb("l7acc", (128, 1024), F32)
            b["dtb"] = [sb(f"l7dtb{i}", (128, 1024), BF16) for i in range(2)]
            b["xc"] = [sb(f"l7xc{i}", (128, 8, 128), F32) for i in range(2)]
            b["pT"] = [ps(f"l7pT{i}", (128, 1024), BF16) for i in range(2)]
            ctx.dma("sync", b["wr"].t[:], dio["w_r"].t.ap()[:, :, :],
                    reads=[dio["w_r"]], writes=[b["wr"]])
            for ti in range(NTI):
                tp = ti % 2
                pl, dtb, pT, xc = b["pl"][tp], b["dtb"][tp], b["pT"][tp], b["xc"][tp]
                for hr in range(2):
                    ctx.dma("gpsimd" if hr else "sync", pl.t[:, hr * 4:(hr + 1) * 4, :],
                            dio["yplanes"].t.ap()[hr * 4:(hr + 1) * 4,
                                                  ti * 128:(ti + 1) * 128, :]
                            .rearrange("r t d -> t r d"),
                            reads=[dio["yplanes"]], writes=[pl], tag=f"l7p{ti}.{hr}")
                ctx.dma("gpsimd" if tp else "sync", xc.t[:, :, :],
                        dio["x3_fm"].t.ap()[:, :, ti * 128:(ti + 1) * 128]
                        .rearrange("c p k -> p c k"),
                        reads=[dio["x3_fm"]], writes=[xc], tag=f"l7x{ti}")
                ctx.step("vector", lambda e, ti=ti, pl=pl: e.tensor_scalar(
                    b["acc"].t[:], pl.t[:, 0, :], b["wr"].t[:, ti, 0:1], None, op0=ALU.mult),
                    reads=[pl, b["wr"]], writes=[b["acc"]])
                for r in range(1, 8):
                    ctx.step("vector", lambda e, ti=ti, r=r, pl=pl: e.scalar_tensor_tensor(
                        b["acc"].t[:], pl.t[:, r, :], b["wr"].t[:, ti, r:r + 1],
                        b["acc"].t[:], op0=ALU.mult, op1=ALU.add),
                        reads=[pl, b["wr"], b["acc"]], writes=[b["acc"]])
                ctx.step("vector", lambda e, dtb=dtb: e.tensor_copy(dtb.t[:], b["acc"].t[:]),
                         reads=[b["acc"]], writes=[dtb])
                def mmtr(e, dtb=dtb, pT=pT):
                    last = None
                    for dc in range(8):
                        last = e.transpose(pT.t[:, dc * 128:(dc + 1) * 128],
                                           dtb.t[:, dc * 128:(dc + 1) * 128],
                                           consts["ident128"].t[:, :])
                    return last
                ctx.step("tensor", mmtr, reads=[dtb, consts["ident128"]], writes=[pT],
                         tag=f"l7tr{ti}")
                ctx.step("vector", lambda e, xc=xc, pT=pT: e.tensor_tensor(
                    xc.t[:, :, :], xc.t[:, :, :],
                    pT.t[:].rearrange("p (c k) -> p c k", c=8), op=ALU.add),
                    reads=[xc, pT], writes=[xc], tag=f"l7add{ti}")
                ctx.dma("scalar", dio["x4_fm"].t.ap()[:, :, ti * 128:(ti + 1) * 128]
                        .rearrange("c p k -> p c k"), xc.t[:, :, :],
                        reads=[xc], writes=[dio["x4_fm"]], tag=f"l7s{ti}")
        ctx.finalize(final_waits=ctx.dma_final_waits())
    return nc


def get_programs():
    if "l1" not in _cache:
        _cache["l1"] = build_l1()
        _cache["l2"] = build_l2()
        _cache["l3"] = build_l3()
        _cache["l4"] = build_l4()
        _cache["l7"] = build_l7()
    return _cache


def fm_tile(x):
    nb = x.shape[0]
    return np.ascontiguousarray(
        x.transpose(2, 0, 1).reshape(8, 128, nb * T))


def fm_untile(a, nb):
    return np.ascontiguousarray(a.reshape(1024, nb, T).transpose(1, 2, 0))


TRACE = False
TLOG = []


def _run(nc, in_maps, **kw):
    import time as _t
    t0 = _t.time()
    r = run_bass_kernel_spmd(nc, in_maps, core_ids=list(range(NCORES)),
                             trace=TRACE, **kw)
    if TRACE:
        TLOG.append((r.exec_time_ns, r.mean_exec_time_ns, round(_t.time() - t0, 2)))
    return r


def moe_host_scatter(res_w, res_pos, res_cnt, res_ytm, progs, moe_w):
    cnt_all = np.concatenate([res_cnt[c] for c in range(NCORES)], axis=0)
    l2_maps = []
    for c in range(NCORES):
        cm = np.zeros((1, 8), np.float32)
        cm[0, :c] = 1.0
        l2_maps.append({"w_own": res_w[c], "pos_own": res_pos[c],
                        "cnt_all": cnt_all, "coremask": cm})
    r2 = _run(progs["l2"], l2_maps)
    addr = np.concatenate(
        [r2.results[c]["addr_r"].transpose(1, 0, 2).reshape(TT, 8) for c in range(NCORES)],
        axis=0)
    w_r = [np.ascontiguousarray(r2.results[c]["w_r"]) for c in range(NCORES)]
    y_all = np.concatenate([res_ytm[c] for c in range(NCORES)], axis=0)
    disp = np.zeros((NSLOT + 1, 1024), BF)
    for r in range(8):
        disp[addr[:, r]] = y_all
    l3_maps = []
    for c in range(NCORES):
        l3_maps.append({"disp": np.ascontiguousarray(disp[c * 2048:(c + 1) * 2048]),
                        "wg_t": moe_w["wg_t"][c], "wu_t": moe_w["wu_t"][c],
                        "wd_t": moe_w["wd_t"][c]})
    r3 = _run(progs["l3"], l3_maps)
    y_big = np.zeros((NSLOT + 1, 1024), BF)
    y_big[:NSLOT] = np.concatenate([r3.results[c]["y"] for c in range(NCORES)], axis=0)
    yplanes = []
    for c in range(NCORES):
        pl = np.stack([y_big[addr[c * TT:(c + 1) * TT, r]] for r in range(8)], axis=0)
        yplanes.append(np.ascontiguousarray(pl))
    return yplanes, w_r


def kernel(**inputs):
    progs = get_programs()
    x = np.asarray(inputs["L0_attn_a_x"])
    wg_t = [np.ascontiguousarray(np.asarray(inputs["L0_ffn_w_gate"])[c * 8:(c + 1) * 8].transpose(0, 2, 1))
            for c in range(NCORES)]
    wu_t = [np.ascontiguousarray(np.asarray(inputs["L0_ffn_w_up"])[c * 8:(c + 1) * 8].transpose(0, 2, 1))
            for c in range(NCORES)]
    wd_t = [np.ascontiguousarray(np.asarray(inputs["L0_ffn_w_down"])[c * 8:(c + 1) * 8].transpose(0, 2, 1))
            for c in range(NCORES)]
    moe_w = {"wg_t": wg_t, "wu_t": wu_t, "wd_t": wd_t}
    gw_t = np.ascontiguousarray(np.asarray(inputs["L0_ffn_gate_w"]).T)

    a = inputs
    l1_common = {
        "Wq": np.asarray(a["L0_attn_a_Wq"]), "Wk": np.asarray(a["L0_attn_a_Wk"]),
        "Wv": np.asarray(a["L0_attn_a_Wv"]), "Wg": np.asarray(a["L0_attn_a_Wg"]),
        "Wo": np.asarray(a["L0_attn_a_Wo"]),
        "w1": np.asarray(a["L0_attn_a_w1"]), "w2": np.asarray(a["L0_attn_a_w2"]),
        "w3": np.asarray(a["L0_attn_a_w3"]),
        "g_attn": np.asarray(a["L0_attn_a_g_attn"]).reshape(1, 1024),
        "g_ffn": np.asarray(a["L0_attn_a_g_ffn"]).reshape(1, 1024),
        "gw_t": gw_t,
    }
    l1_maps = []
    for c in range(NCORES):
        m = dict(l1_common)
        m["x_fm"] = fm_tile(x[c * NB:(c + 1) * NB])
        l1_maps.append(m)
    r1 = _run(progs["l1"], l1_maps)
    res_w = [r1.results[c]["r1w_out"] for c in range(NCORES)]
    res_pos = [r1.results[c]["r1pos_out"] for c in range(NCORES)]
    res_cnt = [r1.results[c]["r1cnt_out"] for c in range(NCORES)]
    res_ytm = [r1.results[c]["r1ytm_d"] for c in range(NCORES)]
    x1 = [r1.results[c]["x1_fm"] for c in range(NCORES)]

    yplanes, w_r = moe_host_scatter(res_w, res_pos, res_cnt, res_ytm, progs, moe_w)

    l4_common = {
        "in_proj_t": np.ascontiguousarray(np.asarray(a["L0_attn_b_in_proj_w"]).T),
        "in_proj_b": np.asarray(a["L0_attn_b_in_proj_b"]).reshape(1, 3072),
        "out_w_t": np.ascontiguousarray(np.asarray(a["L0_attn_b_out_w"]).T),
        "out_b": np.asarray(a["L0_attn_b_out_b"]).reshape(1, 1024),
        "lin1_t": np.ascontiguousarray(np.asarray(a["L0_attn_b_lin1_w"]).T),
        "lin1_b": np.asarray(a["L0_attn_b_lin1_b"]).reshape(1, 4096),
        "lin2_t": np.ascontiguousarray(np.asarray(a["L0_attn_b_lin2_w"]).T),
        "lin2_b": np.asarray(a["L0_attn_b_lin2_b"]).reshape(1, 1024),
        "ln1_g": np.asarray(a["L0_attn_b_ln1_g"]).reshape(1, 1024),
        "ln1_b": np.asarray(a["L0_attn_b_ln1_b"]).reshape(1, 1024),
        "ln2_g": np.asarray(a["L0_attn_b_ln2_g"]).reshape(1, 1024),
        "ln2_b": np.asarray(a["L0_attn_b_ln2_b"]).reshape(1, 1024),
        "gw_t": gw_t,
    }
    l4_maps = []
    for c in range(NCORES):
        m = dict(l4_common)
        m["x1_fm"] = x1[c]
        m["yplanes"] = yplanes[c]
        m["w_r"] = w_r[c]
        l4_maps.append(m)
    r4 = _run(progs["l4"], l4_maps)
    res_w3 = [r4.results[c]["r3w_out"] for c in range(NCORES)]
    res_pos3 = [r4.results[c]["r3pos_out"] for c in range(NCORES)]
    res_cnt3 = [r4.results[c]["r3cnt_out"] for c in range(NCORES)]
    res_ytm3 = [r4.results[c]["r3ytm_d"] for c in range(NCORES)]
    x3 = [r4.results[c]["x3_fm"] for c in range(NCORES)]

    yplanes2, w_r2 = moe_host_scatter(res_w3, res_pos3, res_cnt3, res_ytm3, progs, moe_w)

    l7_maps = []
    for c in range(NCORES):
        l7_maps.append({"x3_fm": x3[c], "yplanes": yplanes2[c], "w_r": w_r2[c]})
    r7 = _run(progs["l7"], l7_maps)
    out = np.concatenate([fm_untile(r7.results[c]["x4_fm"], NB) for c in range(NCORES)], axis=0)
    return out


# version 07 (final)
# speedup vs baseline: 1.0187x; 1.0187x over previous
import math
import numpy as np
from contextlib import ExitStack

import concourse.bass as bass
import concourse.mybir as mybir

F32 = mybir.dt.float32
BF16 = mybir.dt.bfloat16
I32 = mybir.dt.int32
AF = mybir.ActivationFunctionType
ALU = mybir.AluOpType
AX = mybir.AxisListType

ENGines = ("tensor", "vector", "scalar", "gpsimd", "sync")


class Sem:
    __slots__ = ("h", "n", "name")

    def __init__(self, h, name):
        self.h = h
        self.n = 0
        self.name = name


class Buf:
    __slots__ = ("name", "t", "writer", "readers")

    def __init__(self, name, t=None):
        self.name = name
        self.t = t
        self.writer = None
        self.readers = []

    def ap(self):
        return self.t[:]


class Eng:
    __slots__ = ("name", "ctx", "sem", "waits", "thunks", "is_dma")

    def __init__(self, name, ctx, sem, is_dma=False):
        self.name = name
        self.ctx = ctx
        self.sem = sem
        self.waits = {}
        self.thunks = []
        self.is_dma = is_dma


class Ctx:
    def __init__(self, nc, stack):
        self.nc = nc
        self.stack = stack
        self.sems = []
        self.engs = {}
        for name in ENGines:
            s = self._sem(f"sem_{name}")
            self.engs[name] = Eng(name, self, s, is_dma=(name == "sync"))
        self.NDS = 8
        self.dma_sems = {q: [self._sem(f"dma_{q}{i}") for i in range(self.NDS)]
                         for q in ("sync", "gpsimd", "scalar")}
        self.dma_rot = {q: 0 for q in self.dma_sems}
        self._buf_id = 0
        self._cvals = self.sbuf("ctx_cvals", (128, 16), F32)
        self._cmap = {}

    def constcol(self, val, lo=0, hi=128):
        if val not in self._cmap:
            assert len(self._cmap) < 16
            col = len(self._cmap)
            self._cmap[val] = col
            self.step("gpsimd", lambda e: e.memset(self._cvals.t[:, col:col + 1], float(val)),
                      writes=[self._cvals])
        col = self._cmap[val]
        return self._cvals.t[lo:hi, col:col + 1]

    def _sem(self, name):
        h = self.stack.enter_context(self.nc.semaphore(name))
        s = Sem(h, name)
        self.sems.append(s)
        return s

    def sbuf(self, name, shape, dtype):
        t = self.stack.enter_context(self.nc.sbuf_tensor(name, list(shape), dtype))
        return Buf(name, t)

    def psum(self, name, shape, dtype):
        t = self.stack.enter_context(self.nc.psum_tensor(name, list(shape), dtype))
        return Buf(name, t)

    def dram(self, name, shape, dtype, kind="Internal"):
        t = self.nc.dram_tensor(name, list(shape), dtype, kind=kind)
        return Buf(name, t)

    def alias(self, name):
        return Buf(name, None)

    SAFE_SELF = True

    def _wait(self, eng, sem, val):
        if val <= 0:
            return
        if sem is eng.sem and not self.SAFE_SELF:
            return
        if eng.waits.get(sem.name, 0) >= val:
            return
        eng.waits[sem.name] = val
        e = self.engs
        eng.thunks.append(("wait", sem.h, val))

    def step(self, eng_name, emit, reads=(), writes=(), sem=None, inc=1, tag=""):
        eng = self.engs[eng_name]
        csem = sem if sem is not None else eng.sem
        for b in reads:
            if b.writer is not None:
                self._wait(eng, b.writer[0], b.writer[1])
        for b in writes:
            if b.writer is not None:
                self._wait(eng, b.writer[0], b.writer[1])
            for (s, v) in b.readers:
                self._wait(eng, s, v)
        eng.thunks.append(("emit", emit, csem.h, inc, tag))
        csem.n += inc
        wtoken = (csem, csem.n)
        for b in writes:
            b.writer = wtoken
            b.readers = []
        for b in reads:
            b.readers.append(wtoken)
        return wtoken

    def dma(self, queue, out, in_, reads=(), writes=(), tag="", emit=None, **kw):
        sems = self.dma_sems[queue]
        i = self.dma_rot[queue]
        self.dma_rot[queue] = (i + 1) % self.NDS
        sem = sems[i]
        if sem.n > 0:
            self._wait(self.engs[queue], sem, sem.n)
        if emit is None:
            emit = lambda e: e.dma_start(out=out, in_=in_, **kw)
        return self.step(queue, emit, reads=reads, writes=writes, sem=sem, inc=16, tag=tag)

    def dma_final_waits(self):
        return [(s, s.n) for q in self.dma_sems for s in self.dma_sems[q] if s.n > 0]

    def barrier_all(self, eng_names=ENGines):
        snap = [(s, s.n) for s in self.sems]
        for en in eng_names:
            eng = self.engs[en]
            for s, v in snap:
                self._wait(eng, s, v)

    def finalize(self, final_waits=()):
        nc = self.nc
        block = self.stack.enter_context(nc.Block())
        for name in ENGines:
            eng = self.engs[name]
            thunks = eng.thunks

            def body(e, thunks=thunks, name=name):
                for th in thunks:
                    if th[0] == "wait":
                        e.wait_ge(th[1], th[2])
                    else:
                        _, emit, semh, inc, tag = th
                        try:
                            instr = emit(e)
                        except Exception as ex:
                            raise RuntimeError(f"emit failed at tag={tag!r} eng={name}: {ex}") from ex
                        if inc:
                            instr.then_inc(semh, inc)
            getattr(block, name)(body)

        if final_waits:
            def tail(e):
                for s, v in final_waits:
                    e.wait_ge(s.h, v)
            block.sync(tail)


def ceil_div(a, b):
    return (a + b - 1) // b



def emit_identity(ctx, ident_buf, diff_buf, n=128):
    ctx.step("gpsimd", lambda e: e.iota(
        diff_buf.t[:], pattern=[[1, n]], base=0, channel_multiplier=-1,
        allow_small_or_imprecise_dtypes=True), writes=[diff_buf])
    ctx.step("gpsimd", lambda e: e.tensor_scalar(
        ident_buf.t[:], diff_buf.t[:], 0.0, None, op0=ALU.is_equal),
        reads=[diff_buf], writes=[ident_buf])

"""GLA block emitter (one sequence b at a time), shared by probe and kernel.

Layouts (per b, T tokens, H=16 heads, dk=dv=64, D=1024):
  h input: DRAM FM tiled [8, 128, T] fp32 (d-chunk, partition, token)
  weights Wq,Wk,Wv,Wg: SBUF bf16, shape (128, 8, 1024): [d-part, d-chunk, feature]
  output o: FM strips (128, 8, STRIP) bf16 handed to a callback
            (partition = v-dim within 128-feature chunk, block j = feature chunk)
State S: SBUF (128, 8, 64) fp32; pair tile j: partitions 0-63 = head 2j k-dims,
64-127 = head 2j+1; free = v-dims.
"""
import numpy as np

STAGE = 99

H = 16
DK = 64
D = 1024
CH = 64
STRIP = 512
NT = STRIP // 128


def make_gla_consts(ctx):
    c = {}
    c["diff"] = ctx.sbuf("gla_diff", (128, 128), F32)
    c["triu128s"] = ctx.sbuf("gla_triu128s", (128, 128), BF16)
    c["ident128"] = ctx.sbuf("gla_ident128", (128, 128), BF16)
    c["triu2"] = ctx.sbuf("gla_triu2", (128, CH), F32)
    c["ident2"] = ctx.sbuf("gla_ident2", (128, CH), BF16)
    c["mask2"] = ctx.sbuf("gla_mask2", (128, CH), BF16)
    c["ones2"] = ctx.sbuf("gla_ones2", (128, 1), F32)
    def emit_iota(e):
        return e.iota(c["diff"].t[:, :], pattern=[[1, 128]], base=0,
                      channel_multiplier=-1, allow_small_or_imprecise_dtypes=True)
    ctx.step("gpsimd", emit_iota, writes=[c["diff"]])
    ctx.step("gpsimd", lambda e: e.tensor_scalar(
        c["triu128s"].t[:], c["diff"].t[:, :], 0.5, None, op0=ALU.is_gt),
        reads=[c["diff"]], writes=[c["triu128s"]])
    ctx.step("gpsimd", lambda e: e.tensor_scalar(
        c["ident128"].t[:], c["diff"].t[:, :], 0.0, None, op0=ALU.is_equal),
        reads=[c["diff"]], writes=[c["ident128"]])
    ctx.step("gpsimd", lambda e: e.tensor_scalar_add(
        c["diff"].t[64:128, 0:CH], c["diff"].t[64:128, 0:CH], 64.0),
        reads=[c["diff"]], writes=[c["diff"]])
    ctx.step("gpsimd", lambda e: e.tensor_scalar(
        c["triu2"].t[:], c["diff"].t[:, 0:CH], -0.5, None, op0=ALU.is_gt),
        reads=[c["diff"]], writes=[c["triu2"]])
    ctx.step("gpsimd", lambda e: e.tensor_scalar(
        c["mask2"].t[:], c["diff"].t[:, 0:CH], -0.5, None, op0=ALU.is_gt),
        reads=[c["diff"]], writes=[c["mask2"]])
    ctx.step("gpsimd", lambda e: e.tensor_scalar(
        c["ident2"].t[:], c["diff"].t[:, 0:CH], 0.0, None, op0=ALU.is_equal),
        reads=[c["diff"]], writes=[c["ident2"]])
    ctx.step("gpsimd", lambda e: e.memset(c["ones2"].t[:], 1.0), writes=[c["ones2"]])
    return c


def alloc_gla_bufs(ctx, nstrip_resident=1):
    b = {}
    b["hbf"] = ctx.sbuf("gla_hbf", (128, 8, STRIP), BF16)
    b["qt"] = ctx.sbuf("gla_qt", (128, NT, D), BF16)
    b["kt"] = ctx.sbuf("gla_kt", (128, NT, D), BF16)
    b["vt"] = ctx.sbuf("gla_vt", (128, NT, D), BF16)
    b["lsz"] = ctx.sbuf("gla_lsz", (128, NT, D), F32)
    b["sq"] = ctx.sbuf("gla_sq", (128, D), BF16)
    b["nrm"] = ctx.sbuf("gla_nrm", (128, 2 * H), F32)
    b["qtld"] = ctx.sbuf("gla_qtld", (128, D), BF16)
    b["ktld"] = ctx.sbuf("gla_ktld", (128, D), BF16)
    b["Ep"] = ctx.sbuf("gla_Ep", (128, D), BF16)
    b["Em"] = ctx.sbuf("gla_Em", (128, D), BF16)
    b["qkf"] = ctx.sbuf("gla_qkf", (128, 16, CH), BF16)
    b["dec"] = ctx.sbuf("gla_dec", (128, 8), F32)
    b["S"] = ctx.sbuf("gla_S", (128, 8, DK), F32)
    b["Sb"] = ctx.sbuf("gla_Sb", (128, 8, DK), BF16)
    b["t1"] = ctx.sbuf("gla_t1", (128, 8, DK), F32)
    b["pp"] = ctx.sbuf("gla_pp", (128, H * CH), BF16)
    b["ofm"] = ctx.sbuf("gla_ofm", (128, 8, STRIP), BF16)
    b["oext"] = ctx.sbuf("gla_oext", (128, 512), BF16)
    b["pA1"] = ctx.psum("gla_pA1", (128, 1024), F32)
    b["pA2"] = ctx.psum("gla_pA2", (128, 1024), F32)
    b["pT"] = ctx.psum("gla_pT", (128, 1024), BF16)
    b["pDec"] = ctx.psum("gla_pDec", (128, 512), F32)
    b["pO"] = ctx.psum("gla_pO", (128, 512), F32)
    b["pU"] = ctx.psum("gla_pU", (128, 512), F32)
    return b


def emit_gla_b(ctx, consts, b, W, T, load_h_strip, store_o_strip):
    nstrip = T // STRIP
    ctx.step("vector", lambda e: e.memset(b["S"].t[:], 0.0), writes=[b["S"]])
    ctx.step("vector", lambda e: e.memset(b["Sb"].t[:], 0.0), writes=[b["Sb"]])

    for si in range(nstrip):
        load_h_strip(si)
        for tt in range(NT):
            for pas, names in enumerate((("q", "k"), ("v", "g"))):
                def emit_mm(e, tt=tt, names=names):
                    last = None
                    for dc in range(8):
                        lhsT = b["hbf"].t[:, dc, tt * 128:(tt + 1) * 128]
                        for wi, wn in enumerate(names):
                            dst = b["pA1"] if wi == 0 else b["pA2"]
                            for half in range(2):
                                last = e.matmul(
                                    dst.t[:, half * 512:(half + 1) * 512],
                                    lhsT,
                                    W[wn].t[:, dc, half * 512:(half + 1) * 512],
                                    start=(dc == 0), stop=(dc == 7))
                    return last
                ctx.step("tensor", emit_mm,
                         reads=[b["hbf"], W[names[0]], W[names[1]]],
                         writes=[b["pA1"], b["pA2"]],
                         tag=f"proj{si}.{tt}.{pas}")
                if pas == 0:
                    for wi, (src, dst) in enumerate((("q", "qt"), ("k", "kt"))):
                        ps = b["pA1"] if wi == 0 else b["pA2"]
                        ctx.step("scalar", lambda e, ps=ps: e.activation(
                            out=b["sq"].t[:], in_=ps.t[:, 0:1024],
                            func=AF.Square),
                            reads=[ps], writes=[b["sq"]])
                        ctx.step("vector", lambda e, wi=wi: e.tensor_reduce(
                            b["nrm"].t[:, wi * H:(wi + 1) * H].rearrange("p h -> p h ()"),
                            b["sq"].t[:].rearrange("p (h d) -> p h d", h=H),
                            axis=AX.X, op=ALU.add),
                            reads=[b["sq"]], writes=[b["nrm"]])
                    ctx.step("vector", lambda e: e.tensor_scalar_max(
                        b["nrm"].t[:], b["nrm"].t[:], 1e-24),
                        reads=[b["nrm"]], writes=[b["nrm"]])
                    ctx.step("scalar", lambda e: e.activation(
                        out=b["nrm"].t[:], in_=b["nrm"].t[:], func=AF.Ln),
                        reads=[b["nrm"]], writes=[b["nrm"]])
                    ctx.step("scalar", lambda e: e.activation(
                        out=b["nrm"].t[:], in_=b["nrm"].t[:], func=AF.Exp, scale=-0.5),
                        reads=[b["nrm"]], writes=[b["nrm"]])
                    for wi, dst in enumerate(("qt", "kt")):
                        ps = b["pA1"] if wi == 0 else b["pA2"]
                        ctx.step("vector", lambda e, ps=ps, wi=wi, dst=dst, tt=tt: e.tensor_tensor(
                            b[dst].t[:, tt, :].rearrange("p (h d) -> p h d", h=H),
                            ps.t[:, 0:1024].rearrange("p (h d) -> p h d", h=H),
                            b["nrm"].t[:, wi * H:(wi + 1) * H].rearrange("p h -> p h ()").broadcast_to([128, H, DK]),
                            op=ALU.mult),
                            reads=[ps, b["nrm"]], writes=[b[dst]])
                else:
                    ctx.step("scalar", lambda e, tt=tt: e.activation(
                        out=b["vt"].t[:, tt, :], in_=b["pA1"].t[:, 0:1024], func=AF.Copy),
                        reads=[b["pA1"]], writes=[b["vt"]])
                    ctx.step("scalar", lambda e, tt=tt: e.activation(
                        out=b["lsz"].t[:, tt, :], in_=b["pA2"].t[:, 0:1024], func=AF.Sigmoid),
                        reads=[b["pA2"]], writes=[b["lsz"]])
                    ctx.step("scalar", lambda e, tt=tt: e.activation(
                        out=b["lsz"].t[:, tt, :], in_=b["lsz"].t[:, tt, :], func=AF.Ln),
                        reads=[b["lsz"]], writes=[b["lsz"]])

        for ci in range(STRIP // CH if STAGE >= 1 else 0):
            tt = (ci * CH) // 128
            hf = (ci * CH) % 128
            emit_gla_chunk(ctx, consts, b, si, ci, tt, hf)

        store_o_strip(si)


def emit_gla_chunk(ctx, consts, b, si, ci, tt, hf):
    tslice = slice(hf, hf + CH)
    def mm_la(e):
        last = None
        for half in range(2):
            last = e.matmul(b["pA1"].t[tslice, half * 512:(half + 1) * 512],
                            consts["triu2"].t[tslice, :],
                            b["lsz"].t[tslice, tt, half * 512:(half + 1) * 512],
                            start=True, stop=True,
                            tile_position=(hf, hf))
        return last
    ctx.step("tensor", mm_la, reads=[consts["triu2"], b["lsz"]], writes=[b["pA1"]],
             tag=f"la{si}.{ci}")
    ctx.step("scalar", lambda e: e.activation(
        out=b["Ep"].t[tslice, :], in_=b["pA1"].t[tslice, 0:1024], func=AF.Exp),
        reads=[b["pA1"]], writes=[b["Ep"]])
    ctx.step("scalar", lambda e: e.activation(
        out=b["Em"].t[tslice, :], in_=b["pA1"].t[tslice, 0:1024], func=AF.Exp, scale=-1.0),
        reads=[b["pA1"]], writes=[b["Em"]])
    def mm_dec(e):
        last = None
        for j in range(8):
            last = e.matmul(b["pDec"].t[:, j:j + 1],
                            b["lsz"].t[tslice, tt, j * 128:(j + 1) * 128],
                            consts["ones2"].t[tslice, :],
                            start=True, stop=True,
                            tile_position=(hf, 0))
        return last
    ctx.step("tensor", mm_dec, reads=[b["lsz"], consts["ones2"]], writes=[b["pDec"]],
             tag=f"dec{si}.{ci}")
    ctx.step("scalar", lambda e: e.activation(
        out=b["dec"].t[:], in_=b["pDec"].t[:, 0:8], func=AF.Exp),
        reads=[b["pDec"]], writes=[b["dec"]])
    ctx.step("vector", lambda e: e.tensor_tensor(
        b["qtld"].t[tslice, :], b["qt"].t[tslice, tt, :], b["Ep"].t[tslice, :], op=ALU.mult),
        reads=[b["qt"], b["Ep"]], writes=[b["qtld"]])
    ctx.step("vector", lambda e: e.tensor_tensor(
        b["ktld"].t[tslice, :], b["kt"].t[tslice, tt, :], b["Em"].t[tslice, :], op=ALU.mult),
        reads=[b["kt"], b["Em"]], writes=[b["ktld"]])
    if STAGE < 2:
        return
    def mm_tr(e):
        last = None
        for src, base in ((b["qtld"], 0), (b["ktld"], 8)):
            for j in range(8):
                last = e.transpose(
                    b["pT"].t[:, (base + j) * CH:(base + j + 1) * CH],
                    src.t[tslice, j * 128:(j + 1) * 128],
                    consts["ident2"].t[tslice, :])
        return last
    ctx.step("tensor", mm_tr, reads=[b["qtld"], b["ktld"], consts["ident2"]],
             writes=[b["pT"]], tag=f"tr{si}.{ci}")
    ctx.step("vector", lambda e: e.tensor_copy(b["qkf"].t[:], b["pT"].t[:]),
             reads=[b["pT"]], writes=[b["qkf"]])
    if STAGE < 3:
        return
    def ppoff(h):
        return (h % 2) * 512 + (h // 2) * 64
    def mm_pp(e):
        last = None
        for h in range(H):
            j, hb = h // 2, (h % 2) * 64
            last = e.matmul(
                b["pA1"].t[tslice, ppoff(h):ppoff(h) + 64],
                b["qkf"].t[hb:hb + 64, 8 + j, :],
                b["qkf"].t[hb:hb + 64, j, :],
                start=True, stop=True, tile_position=(hb, hf))
        return last
    ctx.step("tensor", mm_pp, reads=[b["qkf"]], writes=[b["pA1"]], tag=f"pp{si}.{ci}")
    if STAGE < 3.7:
        return
    ctx.step("vector", lambda e: e.tensor_tensor(
        b["pp"].t[tslice, :].rearrange("p (g t) -> p g t", g=H),
        b["pA1"].t[tslice, 0:H * CH].rearrange("p (g t) -> p g t", g=H),
        consts["mask2"].t[tslice, :].rearrange("p t -> p () t").broadcast_to([CH, H, CH]),
        op=ALU.mult),
        reads=[b["pA1"], consts["mask2"]], writes=[b["pp"]])
    if STAGE < 4:
        return
    def mm_o(e):
        last = None
        for h in range(H):
            j, hb = h // 2, (h % 2) * 64
            last = e.matmul(
                b["pA2"].t[hb:hb + 64, j * 64:(j + 1) * 64],
                b["Sb"].t[hb:hb + 64, j, :],
                b["qkf"].t[hb:hb + 64, j, :],
                start=True, stop=True, tile_position=(hb, hb))
            last = e.matmul(
                b["pO"].t[hb:hb + 64, j * 64:(j + 1) * 64],
                b["vt"].t[tslice, tt, h * 64:(h + 1) * 64],
                b["pp"].t[tslice, ppoff(h):ppoff(h) + 64],
                start=True, stop=True, tile_position=(hf, hb))
        return last
    ctx.step("tensor", mm_o, reads=[b["Sb"], b["qkf"], b["vt"], b["pp"]],
             writes=[b["pO"], b["pA2"]], tag=f"o{si}.{ci}")
    if STAGE < 5:
        return
    def mm_u(e):
        last = None
        for h in range(H):
            j, hb = h // 2, (h % 2) * 64
            last = e.matmul(
                b["pU"].t[hb:hb + 64, j * 64:(j + 1) * 64],
                b["ktld"].t[tslice, h * 64:(h + 1) * 64],
                b["vt"].t[tslice, tt, h * 64:(h + 1) * 64],
                start=True, stop=True, tile_position=(hf, hb))
        return last
    ctx.step("tensor", mm_u, reads=[b["ktld"], b["vt"]], writes=[b["pU"]],
             tag=f"u{si}.{ci}")
    ctx.step("scalar", lambda e: e.activation(
        out=b["oext"].t[:], in_=b["pA2"].t[:, 0:512], func=AF.Copy),
        reads=[b["pA2"]], writes=[b["oext"]])
    ctx.step("vector", lambda e: e.tensor_tensor(
        b["ofm"].t[:, :, ci * CH:(ci + 1) * CH],
        b["pO"].t[:, 0:512].rearrange("p (j t) -> p j t", j=8),
        b["oext"].t[:].rearrange("p (j t) -> p j t", j=8),
        op=ALU.add),
        reads=[b["pO"], b["oext"]], writes=[b["ofm"]])
    ctx.step("vector", lambda e: e.tensor_tensor(
        b["t1"].t[:], b["S"].t[:],
        b["pU"].t[:, 0:512].rearrange("p (j v) -> p j v", j=8), op=ALU.add),
        reads=[b["S"], b["pU"]], writes=[b["t1"]])
    ctx.step("vector", lambda e: e.tensor_tensor(
        b["S"].t[:], b["t1"].t[:],
        b["dec"].t[:].rearrange("p j -> p j ()").broadcast_to([128, 8, DK]),
        op=ALU.mult),
        reads=[b["t1"], b["dec"]], writes=[b["S"]])
    ctx.step("vector", lambda e: e.tensor_copy(b["Sb"].t[:], b["S"].t[:]),
             reads=[b["S"]], writes=[b["Sb"]])


def gla_reference(h, Wq, Wk, Wv, Wg):
    T = h.shape[0]
    q = (h @ Wq).reshape(T, H, DK)
    k = (h @ Wk).reshape(T, H, DK)
    v = (h @ Wv).reshape(T, H, DK)
    alpha = 1.0 / (1.0 + np.exp(-(h @ Wg))).reshape(T, H, DK)
    def l2n(x):
        n = np.sqrt((x * x).sum(-1, keepdims=True))
        return x / np.maximum(n, 1e-12)
    q = l2n(q); k = l2n(k)
    S = np.zeros((H, DK, DK))
    o = np.zeros((T, H, DK))
    for t in range(T):
        S = alpha[t][:, :, None] * S + k[t][:, :, None] * v[t][:, None, :]
        o[t] = np.einsum("hkv,hk->hv", S, q[t])
    return o.reshape(T, H * DK)

"""L1 program: attn_a block (LN -> rmsnorm -> GLA -> rmsnorm -> gated FFN)
plus MoE routing (logits, top-8, weights, local cumsum positions, counts).

Per-core DRAM I/O (NB sequences of T=2048):
 in:  x_fm [8,128,NB*T] f32 (FM tiled); Wq,Wk,Wv,Wg,Wo [1024,1024];
      w1,w2 [1024,2736]; w3 [2736,1024]; g_attn,g_ffn [1,1024]; gw_t [1024,64]
 out: x1_fm [8,128,NB*T] f32; s1_out [2,NB*T] f32 (A1=rstd1, B1=-mu1*rstd1);
      w_out [128, NB*16, 64] f32 (TM; 0 for unselected);
      pos_out [128, NB*16, 64] f32 (local exclusive prefix count);
      cnt_out [1, 64] f32
 internal: ofm_d bf16; xl_d, xpl_d f32; xlg2_d bf16; rows_d [2,NB*T];
           logits_d [128, NB*16, 64] f32
"""
import numpy as np
from contextlib import ExitStack

NB = 4
T = 2048
NSTR = T // STRIP
DFF = 2736
NFC = (DFF + 127) // 128
NE = 64
CAP = 256
BIG = float(2 ** 40)


def fc_rows(fc):
    return 128 if fc < NFC - 1 else DFF - 128 * (NFC - 1)


def make_long(ctx):
    long = {}
    long["ones_row"] = ctx.sbuf("ones_row", (1, 128), BF16)
    long["ones_rowf"] = ctx.sbuf("ones_rowf", (1, 128), F32)
    long["ones_col"] = ctx.sbuf("ones_col", (128, 1), BF16)
    for n in ("ones_row", "ones_rowf", "ones_col"):
        ctx.step("gpsimd", lambda e, n=n: e.memset(long[n].t[:], 1.0), writes=[long[n]])
    return long


UPTO = 99


def emit_l1(ctx, dio, long=None, consts=None):
    if consts is None:
        consts = make_gla_consts(ctx)
    if long is None:
        long = make_long(ctx)
    emit_p1(ctx, consts, long, dio)
    if UPTO < 1:
        return long
    ctx.barrier_all()
    emit_p2(ctx, long, dio)
    if UPTO < 2:
        return long
    ctx.barrier_all()
    emit_p3(ctx, long, dio)
    return long


def rowmath_ln(ctx, sx, sx2, murow, rstd):
    (sxb, sxa), (sx2b, sx2a) = sx, sx2
    (mb, ma), (rb, ra) = murow, rstd
    ctx.step("vector", lambda e: e.tensor_scalar(ma, sxa, -1.0 / D, None, op0=ALU.mult),
             reads=[sxb], writes=[mb])
    ctx.step("vector", lambda e: e.tensor_tensor(ra, ma, ma, op=ALU.mult),
             reads=[mb], writes=[rb])
    ctx.step("vector", lambda e: e.scalar_tensor_tensor(ra, sx2a, 1.0 / D, ra,
                                                        op0=ALU.mult, op1=ALU.subtract),
             reads=[sx2b, rb], writes=[rb])
    bias5 = ctx.constcol(1e-5, 0, 1)
    ctx.step("scalar", lambda e: e.activation(out=ra, in_=ra, func=AF.Ln, bias=bias5),
             reads=[rb, ctx._cvals], writes=[rb])
    ctx.step("scalar", lambda e: e.activation(out=ra, in_=ra, func=AF.Exp, scale=-0.5),
             reads=[rb], writes=[rb])



def emit_p1(ctx, consts, long, dio):
    with ExitStack() as ph:
        sb = lambda n, s, d: Buf(n, ph.enter_context(ctx.nc.sbuf_tensor(n, list(s), d)))
        ps = lambda n, s, d: Buf(n, ph.enter_context(ctx.nc.psum_tensor(n, list(s), d)))
        ga = sb("p1_ga", (1, 1024), F32)
        ctx.dma("sync", ga.t[:], dio["g_attn"].t.ap()[:, :], writes=[ga])
        W = {}
        for wn, dn in (("q", "Wq"), ("k", "Wk"), ("v", "Wv"), ("g", "Wg")):
            W[wn] = sb(f"W{wn}s", (128, 8, 1024), BF16)
            for dc in range(8):
                ctx.dma("gpsimd", W[wn].t[:, dc, :], dio[dn].t.ap()[dc * 128:(dc + 1) * 128, :],
                        writes=[W[wn]], tag=f"ldW{wn}{dc}")
        b = {}
        b["hbf"] = sb("p1_xb", (128, 8, STRIP), BF16)
        b["sq"] = sb("p1_sq", (128, 8, STRIP), BF16)
        b["srows"] = sb("p1_srows", (1, 2 * T), F32)
        b["crow"] = sb("p1_crow", (1, T), F32)
        b["murow"] = sb("p1_murow", (1, T), F32)
        b["murb"] = sb("p1_murb", (1, T), BF16)
        b["rstd"] = sb("p1_rstd", (1, T), F32)
        b["ccol"] = sb("p1_ccol", (128, 16), F32)
        b["qt"] = sb("p1_qt", (128, NT, D), BF16)
        b["kt"] = sb("p1_kt", (128, NT, D), BF16)
        b["vt"] = sb("p1_vt", (128, NT, D), BF16)
        b["lsz"] = sb("p1_lsz", (128, NT, D), F32)
        b["nrm"] = sb("p1_nrm", (128, 2 * H), F32)
        b["sqn"] = sb("p1_sqn", (128, D), BF16)
        b["qtld"] = sb("p1_qtld", (128, D), BF16)
        b["ktld"] = sb("p1_ktld", (128, D), BF16)
        b["Ep"] = sb("p1_Ep", (128, D), BF16)
        b["Em"] = sb("p1_Em", (128, D), BF16)
        b["qkf"] = sb("p1_qkf", (128, 16, CH), BF16)
        b["dec"] = sb("p1_dec", (128, 8), F32)
        b["S"] = sb("p1_S", (128, 8, DK), F32)
        b["Sb"] = sb("p1_Sb", (128, 8, DK), BF16)
        b["t1"] = sb("p1_t1", (128, 8, DK), F32)
        b["pp"] = sb("p1_pp", (128, H * CH), BF16)
        b["oext"] = sb("p1_oext", (128, 512), BF16)
        b["ofm"] = sb("p1_ofm", (128, 8, STRIP), BF16)
        b["gcol"] = sb("p1_gcol", (128, 8), F32)
        b["pA1"] = ps("p1_pA1", (128, 1024), F32)
        b["pA2"] = ps("p1_pA2", (128, 1024), F32)
        b["pT"] = ps("p1_pT", (128, 1024), BF16)
        b["pDec"] = ps("p1_pDec", (128, 512), F32)
        b["pO"] = ps("p1_pO", (128, 512), F32)
        b["pU"] = ps("p1_pU", (128, 512), F32)

        def mmgc(e):
            last = None
            for dc in range(8):
                last = e.matmul(b["pDec"].t[:, dc:dc + 1],
                                ga.t[:, dc * 128:(dc + 1) * 128],
                                long["ones_rowf"].t[:, 0:1], start=True, stop=True)
            return last
        ctx.step("tensor", mmgc, reads=[ga, long["ones_rowf"]], writes=[b["pDec"]], tag="gacol")
        ctx.step("vector", lambda e: e.tensor_copy(b["gcol"].t[:], b["pDec"].t[:, 0:8]),
                 reads=[b["pDec"]], writes=[b["gcol"]])
        for wn in ("q", "k", "v", "g"):
            for dc in range(8):
                ctx.step("vector", lambda e, wn=wn, dc=dc: e.tensor_scalar(
                    W[wn].t[:, dc, :], W[wn].t[:, dc, :], b["gcol"].t[:, dc:dc + 1],
                    None, op0=ALU.mult),
                    reads=[W[wn], b["gcol"]], writes=[W[wn]])
        gwr = {}
        for wn in ("q", "k", "v", "g"):
            gwr[wn] = sb(f"gWr_{wn}", (1, 1024), BF16)
            gtmp = b["pA1"]
            def mmgw(e, wn=wn):
                last = None
                for half in range(2):
                    for dc in range(8):
                        last = e.matmul(gtmp.t[0:1, half * 512:(half + 1) * 512],
                                        long["ones_col"].t[:, 0:1],
                                        W[wn].t[:, dc, half * 512:(half + 1) * 512],
                                        start=(dc == 0), stop=(dc == 7))
                return last
            ctx.step("tensor", mmgw, reads=[long["ones_col"], W[wn]], writes=[gtmp], tag=f"gW{wn}")
            ctx.step("vector", lambda e, wn=wn: e.tensor_copy(gwr[wn].t[:], gtmp.t[0:1, 0:1024]),
                     reads=[gtmp], writes=[gwr[wn]])

        for bi in range(NB):
            emit_p1_b(ctx, consts, long, dio, W, gwr, b, bi)


def emit_p1_b(ctx, consts, long, dio, W, gwr, b, bi):
    for si in range(NSTR):
        col0 = bi * T + si * STRIP
        for dc in range(8):
            ctx.dma("gpsimd", b["hbf"].t[:, dc, :],
                    dio["x_fm"].t.ap()[dc, :, col0:col0 + STRIP],
                    reads=[dio["x_fm"]], writes=[b["hbf"]], tag=f"p1a_x{bi}.{si}.{dc}")
        ctx.step("scalar", lambda e: e.activation(
            out=b["sq"].t[:], in_=b["hbf"].t[:], func=AF.Square),
            reads=[b["hbf"]], writes=[b["sq"]])
        for which, src in (("x", b["hbf"]), ("x2", b["sq"])):
            def mmst(e, src=src):
                last = None
                for dc in range(8):
                    last = e.matmul(b["pO"].t[0:1, 0:STRIP], long["ones_col"].t[:, 0:1],
                                    src.t[:, dc, :], start=(dc == 0), stop=(dc == 7))
                return last
            ctx.step("tensor", mmst, reads=[src, long["ones_col"]], writes=[b["pO"]],
                     tag=f"p1a_st{bi}.{si}.{which}")
            off = 0 if which == "x" else T
            ctx.step("vector", lambda e, si=si, off=off: e.tensor_copy(
                b["srows"].t[0:1, off + si * STRIP:off + (si + 1) * STRIP], b["pO"].t[0:1, 0:STRIP]),
                reads=[b["pO"]], writes=[b["srows"]])
    rowmath_ln(ctx, (b["srows"], b["srows"].t[0:1, 0:T]), (b["srows"], b["srows"].t[0:1, T:2 * T]),
               (b["murow"], b["murow"].t[:]), (b["rstd"], b["rstd"].t[:]))
    ctx.step("vector", lambda e: e.tensor_copy(b["murb"].t[:], b["murow"].t[:]),
             reads=[b["murow"]], writes=[b["murb"]])
    ctx.step("vector", lambda e: e.tensor_tensor(b["crow"].t[:], b["rstd"].t[:], b["rstd"].t[:], op=ALU.mult),
             reads=[b["rstd"]], writes=[b["crow"]])
    ctx.step("vector", lambda e: e.tensor_scalar(b["crow"].t[:], b["crow"].t[:], -1e-5, 1.0,
                                                 op0=ALU.mult, op1=ALU.add),
             reads=[b["crow"]], writes=[b["crow"]])
    bias6 = ctx.constcol(1e-6, 0, 1)
    ctx.step("scalar", lambda e: e.activation(out=b["crow"].t[:], in_=b["crow"].t[:],
                                              func=AF.Ln, bias=bias6),
             reads=[b["crow"], ctx._cvals], writes=[b["crow"]])
    ctx.step("scalar", lambda e: e.activation(out=b["crow"].t[:], in_=b["crow"].t[:],
                                              func=AF.Exp, scale=-0.5),
             reads=[b["crow"]], writes=[b["crow"]])
    ctx.step("vector", lambda e: e.tensor_tensor(b["crow"].t[:], b["crow"].t[:], b["rstd"].t[:], op=ALU.mult),
             reads=[b["crow"], b["rstd"]], writes=[b["crow"]])
    ctx.dma("sync", dio["rows_d"].t.ap()[0:1, bi * T:(bi + 1) * T], b["rstd"].t[0:1, :],
            reads=[b["rstd"]], writes=[dio["rows_d"]], tag=f"strowA{bi}")
    ctx.step("vector", lambda e: e.tensor_tensor(b["srows"].t[0:1, 0:T], b["murow"].t[:],
                                                 b["rstd"].t[:], op=ALU.mult),
             reads=[b["murow"], b["rstd"]], writes=[b["srows"]])
    ctx.dma("sync", dio["rows_d"].t.ap()[1:2, bi * T:(bi + 1) * T], b["srows"].t[0:1, 0:T],
            reads=[b["srows"]], writes=[dio["rows_d"]], tag=f"strowB{bi}")
    def mmcc(e):
        last = None
        for ti in range(16):
            last = e.matmul(b["pDec"].t[:, ti:ti + 1],
                            b["crow"].t[0:1, ti * 128:(ti + 1) * 128],
                            long["ones_rowf"].t[:, 0:1], start=True, stop=True)
        return last
    ctx.step("tensor", mmcc, reads=[b["crow"], long["ones_rowf"]], writes=[b["pDec"]], tag=f"ccol{bi}")
    ctx.step("vector", lambda e: e.tensor_copy(b["ccol"].t[:], b["pDec"].t[:, 0:16]),
             reads=[b["pDec"]], writes=[b["ccol"]])

    ctx.step("vector", lambda e: e.memset(b["S"].t[:], 0.0), writes=[b["S"]])
    ctx.step("vector", lambda e: e.memset(b["Sb"].t[:], 0.0), writes=[b["Sb"]])
    for si in range(NSTR):
        col0 = bi * T + si * STRIP
        for dc in range(8):
            ctx.dma("gpsimd", b["hbf"].t[:, dc, :],
                    dio["x_fm"].t.ap()[dc, :, col0:col0 + STRIP],
                    reads=[dio["x_fm"]], writes=[b["hbf"]], tag=f"p1b_x{bi}.{si}.{dc}")
        for tt in range(NT):
            tglob = si * NT + tt
            for pas, names in enumerate((("q", "k"), ("v", "g"))):
                def emit_mm(e, tt=tt, names=names, tglob=tglob):
                    last = None
                    for dc in range(8):
                        lhsT = b["hbf"].t[:, dc, tt * 128:(tt + 1) * 128]
                        for wi, wn in enumerate(names):
                            dst = b["pA1"] if wi == 0 else b["pA2"]
                            for half in range(2):
                                last = e.matmul(
                                    dst.t[:, half * 512:(half + 1) * 512], lhsT,
                                    W[wn].t[:, dc, half * 512:(half + 1) * 512],
                                    start=(dc == 0), stop=False)
                    for wi, wn in enumerate(names):
                        dst = b["pA1"] if wi == 0 else b["pA2"]
                        for half in range(2):
                            last = e.matmul(
                                dst.t[:, half * 512:(half + 1) * 512],
                                b["murb"].t[0:1, tglob * 128:(tglob + 1) * 128],
                                gwr[wn].t[0:1, half * 512:(half + 1) * 512],
                                start=False, stop=True)
                    return last
                ctx.step("tensor", emit_mm,
                         reads=[b["hbf"], W[names[0]], W[names[1]], b["murb"],
                                gwr[names[0]], gwr[names[1]]],
                         writes=[b["pA1"], b["pA2"]], tag=f"proj{bi}.{si}.{tt}.{pas}")
                ccol_ap = b["ccol"].t[:, tglob:tglob + 1]
                if pas == 0:
                    for wi, dst in enumerate(("qt", "kt")):
                        psrc = b["pA1"] if wi == 0 else b["pA2"]
                        ctx.step("scalar", lambda e, psrc=psrc: e.activation(
                            out=b["sqn"].t[:], in_=psrc.t[:, 0:1024], func=AF.Square),
                            reads=[psrc], writes=[b["sqn"]])
                        ctx.step("vector", lambda e, wi=wi: e.tensor_reduce(
                            b["nrm"].t[:, wi * H:(wi + 1) * H].rearrange("p h -> p h ()"),
                            b["sqn"].t[:].rearrange("p (h d) -> p h d", h=H),
                            axis=AX.X, op=ALU.add),
                            reads=[b["sqn"]], writes=[b["nrm"]])
                    ctx.step("vector", lambda e: e.tensor_scalar_max(
                        b["nrm"].t[:], b["nrm"].t[:], 1e-24),
                        reads=[b["nrm"]], writes=[b["nrm"]])
                    ctx.step("scalar", lambda e: e.activation(
                        out=b["nrm"].t[:], in_=b["nrm"].t[:], func=AF.Ln),
                        reads=[b["nrm"]], writes=[b["nrm"]])
                    ctx.step("scalar", lambda e: e.activation(
                        out=b["nrm"].t[:], in_=b["nrm"].t[:], func=AF.Exp, scale=-0.5),
                        reads=[b["nrm"]], writes=[b["nrm"]])
                    for wi, dst in enumerate(("qt", "kt")):
                        psrc = b["pA1"] if wi == 0 else b["pA2"]
                        ctx.step("vector", lambda e, psrc=psrc, wi=wi, dst=dst, tt=tt: e.tensor_tensor(
                            b[dst].t[:, tt, :].rearrange("p (h d) -> p h d", h=H),
                            psrc.t[:, 0:1024].rearrange("p (h d) -> p h d", h=H),
                            b["nrm"].t[:, wi * H:(wi + 1) * H].rearrange("p h -> p h ()").broadcast_to([128, H, DK]),
                            op=ALU.mult),
                            reads=[psrc, b["nrm"]], writes=[b[dst]])
                else:
                    ctx.step("vector", lambda e, tt=tt, ccol_ap=ccol_ap: e.tensor_scalar(
                        b["vt"].t[:, tt, :], b["pA1"].t[:, 0:1024], ccol_ap, None, op0=ALU.mult),
                        reads=[b["pA1"], b["ccol"]], writes=[b["vt"]])
                    ctx.step("scalar", lambda e, tt=tt, ccol_ap=ccol_ap: e.activation(
                        out=b["lsz"].t[:, tt, :], in_=b["pA2"].t[:, 0:1024], func=AF.Sigmoid,
                        scale=ccol_ap),
                        reads=[b["pA2"], b["ccol"]], writes=[b["lsz"]])
                    ctx.step("scalar", lambda e, tt=tt: e.activation(
                        out=b["lsz"].t[:, tt, :], in_=b["lsz"].t[:, tt, :], func=AF.Ln),
                        reads=[b["lsz"]], writes=[b["lsz"]])
        for ci in range(STRIP // CH):
            tt = (ci * CH) // 128
            hf = (ci * CH) % 128
            emit_gla_chunk(ctx, consts, b, si, ci, tt, hf)
        for j in range(8):
            ctx.dma("gpsimd", dio["ofm_d"].t.ap()[j, :, col0:col0 + STRIP],
                    b["ofm"].t[:, j, :], reads=[b["ofm"]], writes=[dio["ofm_d"]],
                    tag=f"st_ofm{bi}.{si}.{j}")



def emit_p2(ctx, long, dio):
    with ExitStack() as ph:
        sb = lambda n, s, d: Buf(n, ph.enter_context(ctx.nc.sbuf_tensor(n, list(s), d)))
        ps = lambda n, s, d: Buf(n, ph.enter_context(ctx.nc.psum_tensor(n, list(s), d)))
        gf = sb("p2_gf", (1, 1024), F32)
        ctx.dma("sync", gf.t[:], dio["g_ffn"].t.ap()[:, :], writes=[gf])
        Wo = sb("Wos", (128, 8, 1024), BF16)
        for dc in range(8):
            ctx.dma("gpsimd", Wo.t[:, dc, :], dio["Wo"].t.ap()[dc * 128:(dc + 1) * 128, :],
                    writes=[Wo], tag=f"ldWo{dc}")
        b = {}
        b["x"] = [sb(f"p2_x{i}", (128, 8, STRIP), F32) for i in range(2)]
        b["ofm"] = [sb(f"p2_ofm{i}", (128, 8, STRIP), BF16) for i in range(2)]
        b["xlB"] = sb("p2_xlB", (128, 8, T), F32)
        b["xlb"] = [sb(f"p2_xlb{i}", (128, 8, STRIP), BF16) for i in range(2)]
        b["rowA"] = [sb(f"p2_rowA{i}", (1, STRIP), F32) for i in range(2)]
        b["rowB"] = [sb(f"p2_rowB{i}", (1, STRIP), F32) for i in range(2)]
        b["rows2"] = sb("p2_rows2", (1, T), F32)
        b["rq2"] = sb("p2_rq2", (1, T), F32)
        b["pAB"] = [ps(f"p2_pAB{i}", (128, 1024), F32) for i in range(2)]
        b["pG"] = [ps(f"p2_pG{i}", (128, 512), F32) for i in range(2)]
        b["pR"] = ps("p2_pR", (1, 512), F32)
        b["pGA"] = ps("p2_pGA", (128, 512), F32)
        for bi in range(NB):
            for si in range(NSTR):
                par = si % 2
                col0 = bi * T + si * STRIP
                x, ofm, xlb = b["x"][par], b["ofm"][par], b["xlb"][par]
                rowA, rowB, pAB = b["rowA"][par], b["rowB"][par], b["pAB"][par]
                for hc in range(2):
                    ctx.dma("gpsimd" if hc else "sync", x.t[:, hc * 4:(hc + 1) * 4, :],
                            dio["x_fm"].t.ap()[hc * 4:(hc + 1) * 4, :, col0:col0 + STRIP]
                            .rearrange("c p k -> p c k"),
                            reads=[dio["x_fm"]], writes=[x], tag=f"p2x{bi}.{si}.{hc}")
                    ctx.dma("gpsimd" if hc else "sync", ofm.t[:, hc * 4:(hc + 1) * 4, :],
                            dio["ofm_d"].t.ap()[hc * 4:(hc + 1) * 4, :, col0:col0 + STRIP]
                            .rearrange("c p k -> p c k"),
                            reads=[dio["ofm_d"]], writes=[ofm], tag=f"p2o{bi}.{si}.{hc}")
                ctx.dma("sync", rowA.t[:], dio["rows_d"].t.ap()[0:1, col0:col0 + STRIP],
                        reads=[dio["rows_d"]], writes=[rowA], tag=f"p2rA{bi}.{si}")
                ctx.dma("sync", rowB.t[:], dio["rows_d"].t.ap()[1:2, col0:col0 + STRIP],
                        reads=[dio["rows_d"]], writes=[rowB], tag=f"p2rB{bi}.{si}")
                def mmab(e, rowA=rowA, rowB=rowB, pAB=pAB):
                    e.matmul(pAB.t[:, 0:512], long["ones_rowf"].t[0:1, :],
                             rowA.t[0:1, :], start=True, stop=True)
                    return e.matmul(pAB.t[:, 512:1024], long["ones_rowf"].t[0:1, :],
                                    rowB.t[0:1, :], start=True, stop=True)
                ctx.step("tensor", mmab, reads=[rowA, rowB, long["ones_rowf"]],
                         writes=[pAB], tag=f"p2ab{bi}.{si}")
                for dc in range(8):
                    pG = b["pG"][dc % 2]
                    def mmg(e, dc=dc, pG=pG, ofm=ofm):
                        last = None
                        for vc in range(8):
                            last = e.matmul(pG.t[:, 0:STRIP],
                                            Wo.t[:, vc, dc * 128:(dc + 1) * 128],
                                            ofm.t[:, vc, :],
                                            start=(vc == 0), stop=(vc == 7))
                        return last
                    ctx.step("tensor", mmg, reads=[Wo, ofm], writes=[pG],
                             tag=f"p2g{bi}.{si}.{dc}")
                    xlc = b["xlB"].t[:, dc, si * STRIP:(si + 1) * STRIP]
                    ctx.step("vector", lambda e, dc=dc, x=x, pAB=pAB, xlc=xlc: e.tensor_tensor(
                        xlc, x.t[:, dc, :], pAB.t[:, 0:512], op=ALU.mult),
                        reads=[x, pAB], writes=[b["xlB"]])
                    ctx.step("vector", lambda e, dc=dc, pAB=pAB, xlc=xlc: e.tensor_tensor(
                        xlc, xlc, pAB.t[:, 512:1024], op=ALU.add),
                        reads=[b["xlB"], pAB], writes=[b["xlB"]])
                    ctx.step("vector", lambda e, dc=dc, pG=pG, xlc=xlc: e.tensor_tensor(
                        xlc, xlc, pG.t[:, 0:STRIP], op=ALU.add),
                        reads=[b["xlB"], pG], writes=[b["xlB"]])
                    ctx.step("vector", lambda e, dc=dc, x=x, xlc=xlc: e.tensor_tensor(
                        x.t[:, dc, :], x.t[:, dc, :], xlc, op=ALU.add),
                        reads=[x, b["xlB"]], writes=[x])
                ctx.dma("scalar", dio["xpl_d"].t.ap()[:, :, col0:col0 + STRIP]
                        .rearrange("c p k -> p c k"), x.t[:, :, :],
                        reads=[x], writes=[dio["xpl_d"]], tag=f"p2sxp{bi}.{si}")
                ctx.step("vector", lambda e, xlb=xlb, si=si: e.tensor_copy(
                    xlb.t[:], b["xlB"].t[:, :, si * STRIP:(si + 1) * STRIP]),
                    reads=[b["xlB"]], writes=[xlb])
                ctx.step("scalar", lambda e, xlb=xlb: e.activation(
                    out=xlb.t[:], in_=xlb.t[:], func=AF.Square),
                    reads=[xlb], writes=[xlb])
                def mmst(e, xlb=xlb):
                    last = None
                    for dc in range(8):
                        last = e.matmul(b["pR"].t[0:1, 0:STRIP], long["ones_col"].t[:, 0:1],
                                        xlb.t[:, dc, :], start=(dc == 0), stop=(dc == 7))
                    return last
                ctx.step("tensor", mmst, reads=[xlb, long["ones_col"]], writes=[b["pR"]],
                         tag=f"p2st{bi}.{si}")
                ctx.step("vector", lambda e, si=si: e.tensor_copy(
                    b["rows2"].t[0:1, si * STRIP:(si + 1) * STRIP], b["pR"].t[0:1, 0:STRIP]),
                    reads=[b["pR"]], writes=[b["rows2"]])
            bias6b = ctx.constcol(1e-6, 0, 1)
            ctx.step("scalar", lambda e: e.activation(out=b["rq2"].t[:], in_=b["rows2"].t[:],
                                                      func=AF.Ln, bias=bias6b, scale=1.0 / D),
                     reads=[b["rows2"], ctx._cvals], writes=[b["rq2"]])
            ctx.step("scalar", lambda e: e.activation(out=b["rq2"].t[:], in_=b["rq2"].t[:],
                                                      func=AF.Exp, scale=-0.5),
                     reads=[b["rq2"]], writes=[b["rq2"]])
            for si in range(NSTR):
                par = si % 2
                col0 = bi * T + si * STRIP
                xlb = b["xlb"][par]
                for dc in range(8):
                    def mmga(e, dc=dc, si=si):
                        return e.matmul(b["pGA"].t[:, 0:STRIP],
                                        gf.t[0:1, dc * 128:(dc + 1) * 128],
                                        b["rq2"].t[0:1, si * STRIP:(si + 1) * STRIP],
                                        start=True, stop=True)
                    ctx.step("tensor", mmga, reads=[gf, b["rq2"]], writes=[b["pGA"]],
                             tag=f"p2ga{bi}.{si}.{dc}")
                    ctx.step("vector", lambda e, dc=dc, si=si, xlb=xlb: e.tensor_tensor(
                        xlb.t[:, dc, :], b["xlB"].t[:, dc, si * STRIP:(si + 1) * STRIP],
                        b["pGA"].t[:, 0:STRIP], op=ALU.mult),
                        reads=[b["xlB"], b["pGA"]], writes=[xlb])
                ctx.dma("scalar", dio["xlg2_d"].t.ap()[:, :, col0:col0 + STRIP]
                        .rearrange("c p k -> p c k"), xlb.t[:, :, :],
                        reads=[xlb], writes=[dio["xlg2_d"]], tag=f"p2sxg{bi}.{si}")



def emit_p3(ctx, long, dio):
    with ExitStack() as ph:
        sb = lambda n, s, d: Buf(n, ph.enter_context(ctx.nc.sbuf_tensor(n, list(s), d)))
        ps = lambda n, s, d: Buf(n, ph.enter_context(ctx.nc.psum_tensor(n, list(s), d)))
        W1 = sb("w1s", (128, 8, DFF), BF16)
        W2 = sb("w2s", (128, 8, DFF), BF16)
        W3 = sb("w3s", (128, NFC, 1024), BF16)
        for dc in range(8):
            ctx.dma("gpsimd", W1.t[:, dc, :], dio["w1"].t.ap()[dc * 128:(dc + 1) * 128, :],
                    writes=[W1], tag=f"ldw1{dc}")
            ctx.dma("gpsimd", W2.t[:, dc, :], dio["w2"].t.ap()[dc * 128:(dc + 1) * 128, :],
                    writes=[W2], tag=f"ldw2{dc}")
        for fc in range(NFC):
            r = fc_rows(fc)
            ctx.dma("gpsimd", W3.t[0:r, fc, :], dio["w3"].t.ap()[fc * 128:fc * 128 + r, :],
                    writes=[W3], tag=f"ldw3{fc}")
        b = {}
        b["xlg"] = [sb(f"p3_xlg{i}", (128, 8, STRIP), BF16) for i in range(2)]
        b["ffin"] = sb("p3_ffin", (128, NFC, STRIP), BF16)
        b["sil"] = sb("p3_sil", (128, STRIP), BF16)
        b["xp"] = [sb(f"p3_xp{i}", (128, 8, STRIP), F32) for i in range(2)]
        b["pF"] = [ps(f"p3_pF{i}", (128, 1024), F32) for i in range(2)]
        b["pW3"] = [ps(f"p3_pW3{i}", (128, STRIP), F32) for i in range(2)]
        for bi in range(NB):
            for si in range(NSTR):
                emit_p3_strip(ctx, long, dio, b, W1, W2, W3, bi, si)


def emit_p3_strip(ctx, long, dio, b, W1, W2, W3, bi, si):
    col0 = bi * T + si * STRIP
    par = (bi * NSTR + si) % 2
    xlg, xp = b["xlg"][par], b["xp"][par]
    for hc in range(2):
        ctx.dma("gpsimd" if hc else "sync", xlg.t[:, hc * 4:(hc + 1) * 4, :],
                dio["xlg2_d"].t.ap()[hc * 4:(hc + 1) * 4, :, col0:col0 + STRIP]
                .rearrange("c p k -> p c k"),
                reads=[dio["xlg2_d"]], writes=[xlg], tag=f"p3xlg{bi}.{si}.{hc}")
    ctx.dma("scalar", xp.t[:, :, :],
            dio["xpl_d"].t.ap()[:, :, col0:col0 + STRIP].rearrange("c p k -> p c k"),
            reads=[dio["xpl_d"]], writes=[xp], tag=f"p3xp{bi}.{si}")
    for fc in range(NFC):
        r = fc_rows(fc)
        pF = b["pF"][fc % 2]
        def mmf(e, fc=fc, r=r, pF=pF, xlg=xlg):
            last = None
            for dc in range(8):
                last = e.matmul(pF.t[0:r, 0:STRIP],
                                W1.t[:, dc, fc * 128:fc * 128 + r],
                                xlg.t[:, dc, :], start=(dc == 0), stop=(dc == 7))
            for dc in range(8):
                last = e.matmul(pF.t[0:r, 512:512 + STRIP],
                                W2.t[:, dc, fc * 128:fc * 128 + r],
                                xlg.t[:, dc, :], start=(dc == 0), stop=(dc == 7))
            return last
        ctx.step("tensor", mmf, reads=[W1, W2, xlg], writes=[pF],
                 tag=f"p3f{bi}.{si}.{fc}")
        ctx.step("scalar", lambda e, r=r, pF=pF: e.activation(
            out=b["sil"].t[0:r, :], in_=pF.t[0:r, 0:STRIP], func=AF.Silu),
            reads=[pF], writes=[b["sil"]])
        ctx.step("vector", lambda e, fc=fc, r=r, pF=pF: e.tensor_tensor(
            b["ffin"].t[0:r, fc, :], pF.t[0:r, 512:512 + STRIP], b["sil"].t[0:r, :],
            op=ALU.mult),
            reads=[pF, b["sil"]], writes=[b["ffin"]])
    for dc in range(8):
        pW3 = b["pW3"][dc % 2]
        def mmw3(e, dc=dc, pW3=pW3):
            last = None
            for fc in range(NFC):
                r = fc_rows(fc)
                last = e.matmul(pW3.t[:, 0:STRIP],
                                W3.t[0:r, fc, dc * 128:(dc + 1) * 128],
                                b["ffin"].t[0:r, fc, :],
                                start=(fc == 0), stop=(fc == NFC - 1))
            return last
        ctx.step("tensor", mmw3, reads=[W3, b["ffin"]], writes=[pW3],
                 tag=f"p3w3{bi}.{si}.{dc}")
        ctx.step("vector", lambda e, dc=dc, pW3=pW3, xp=xp: e.tensor_tensor(
            xp.t[:, dc, :], xp.t[:, dc, :], pW3.t[:, 0:STRIP], op=ALU.add),
            reads=[xp, pW3], writes=[xp])
    ctx.dma("scalar", dio["x1_fm"].t.ap()[:, :, col0:col0 + STRIP]
            .rearrange("c p k -> p c k"), xp.t[:, :, :],
            reads=[xp], writes=[dio["x1_fm"]], tag=f"p3sx1{bi}.{si}")



def emit_p4(ctx, consts, long, dio):
    NTI = NB * 16
    with ExitStack() as ph:
        sb = lambda n, s, d: Buf(n, ph.enter_context(ctx.nc.sbuf_tensor(n, list(s), d)))
        ps = lambda n, s, d: Buf(n, ph.enter_context(ctx.nc.psum_tensor(n, list(s), d)))
        b = {}
        b["lg"] = sb("p4_lg", (128, NTI, NE), F32)
        b["wrk"] = sb("p4_wrk", (128, NTI, NE), F32)
        b["mx"] = sb("p4_mx", (128, NTI), F32)
        b["eq"] = sb("p4_eq", (128, NTI, NE), F32)
        b["m"] = sb("p4_m", (128, NTI, NE), F32)
        b["mb"] = sb("p4_mb", (128, NTI, NE), BF16)
        b["w"] = sb("p4_w", (128, NTI, NE), F32)
        b["psum"] = sb("p4_psum", (128, NTI), F32)
        b["base"] = sb("p4_base", (1, NE), F32)
        b["totrow"] = sb("p4_totrow", (1, NTI * NE), F32)
        b["brow"] = sb("p4_brow", (1, NTI * NE), F32)
        b["pP"] = ps("p4_pP", (128, 512), F32)
        b["pB"] = ps("p4_pB", (1, 512), F32)
        ctx.dma("sync", b["lg"].t[:], dio["logits_d"].t.ap()[:, :, :], reads=[dio["logits_d"]],
                writes=[b["lg"]], tag="p4lg")
        ctx.step("vector", lambda e: e.tensor_copy(b["wrk"].t[:], b["lg"].t[:]),
                 reads=[b["lg"]], writes=[b["wrk"]])
        ctx.step("vector", lambda e: e.memset(b["m"].t[:], 0.0), writes=[b["m"]])
        for it in range(8):
            ctx.step("vector", lambda e: e.tensor_reduce(
                b["mx"].t[:].rearrange("p t -> p t ()"), b["wrk"].t[:], axis=AX.X, op=ALU.max),
                reads=[b["wrk"]], writes=[b["mx"]])
            ctx.step("vector", lambda e: e.tensor_tensor(
                b["eq"].t[:], b["wrk"].t[:],
                b["mx"].t[:].rearrange("p t -> p t ()").broadcast_to([128, NTI, NE]),
                op=ALU.is_ge),
                reads=[b["wrk"], b["mx"]], writes=[b["eq"]])
            ctx.step("vector", lambda e: e.tensor_tensor(
                b["m"].t[:], b["m"].t[:], b["eq"].t[:], op=ALU.add),
                reads=[b["m"], b["eq"]], writes=[b["m"]])
            if it < 7:
                ctx.step("vector", lambda e: e.scalar_tensor_tensor(
                    b["wrk"].t[:], b["eq"].t[:], -BIG, b["wrk"].t[:], op0=ALU.mult, op1=ALU.add),
                    reads=[b["eq"], b["wrk"]], writes=[b["wrk"]])
        ctx.step("vector", lambda e: e.tensor_scalar_min(b["m"].t[:], b["m"].t[:], 1.0),
                 reads=[b["m"]], writes=[b["m"]])
        ctx.step("vector", lambda e: e.tensor_reduce(
            b["mx"].t[:].rearrange("p t -> p t ()"), b["lg"].t[:], axis=AX.X, op=ALU.max),
            reads=[b["lg"]], writes=[b["mx"]])
        ctx.step("vector", lambda e: e.tensor_tensor(
            b["wrk"].t[:], b["lg"].t[:],
            b["mx"].t[:].rearrange("p t -> p t ()").broadcast_to([128, NTI, NE]),
            op=ALU.subtract),
            reads=[b["lg"], b["mx"]], writes=[b["wrk"]])
        ctx.step("scalar", lambda e: e.activation(out=b["wrk"].t[:], in_=b["wrk"].t[:], func=AF.Exp),
                 reads=[b["wrk"]], writes=[b["wrk"]])
        ctx.step("vector", lambda e: e.tensor_tensor(b["w"].t[:], b["wrk"].t[:], b["m"].t[:], op=ALU.mult),
                 reads=[b["wrk"], b["m"]], writes=[b["w"]])
        ctx.step("vector", lambda e: e.tensor_reduce(
            b["psum"].t[:].rearrange("p t -> p t ()"), b["w"].t[:], axis=AX.X, op=ALU.add),
            reads=[b["w"]], writes=[b["psum"]])
        ctx.step("vector", lambda e: e.reciprocal(b["psum"].t[:], b["psum"].t[:]),
                 reads=[b["psum"]], writes=[b["psum"]])
        ctx.step("vector", lambda e: e.tensor_tensor(
            b["w"].t[:], b["w"].t[:],
            b["psum"].t[:].rearrange("p t -> p t ()").broadcast_to([128, NTI, NE]),
            op=ALU.mult),
            reads=[b["w"], b["psum"]], writes=[b["w"]])
        ctx.dma("sync", dio["w_out"].t.ap()[:, :, :], b["w"].t[:], reads=[b["w"]],
                writes=[dio["w_out"]], tag="p4w")
        ctx.step("vector", lambda e: e.tensor_copy(b["mb"].t[:], b["m"].t[:]),
                 reads=[b["m"]], writes=[b["mb"]])
        for g in range(NTI // 8):
            def mmtot(e, g=g):
                return e.matmul(b["pB"].t[0:1, 0:512], long["ones_col"].t[:, 0:1],
                                b["mb"].t[:, g * 8:(g + 1) * 8, :], start=True, stop=True)
            ctx.step("tensor", mmtot, reads=[b["mb"], long["ones_col"]], writes=[b["pB"]],
                     tag=f"tot{g}")
            ctx.step("vector", lambda e, g=g: e.tensor_copy(
                b["totrow"].t[0:1, g * 512:(g + 1) * 512], b["pB"].t[0:1, 0:512]),
                reads=[b["pB"]], writes=[b["totrow"]])
        cur, nxt = b["totrow"], b["brow"]
        s = 1
        while s < NTI:
            ctx.step("vector", lambda e, cur=cur, nxt=nxt, s=s: e.tensor_copy(
                nxt.t[0:1, 0:s * NE], cur.t[0:1, 0:s * NE]),
                reads=[cur], writes=[nxt])
            ctx.step("vector", lambda e, cur=cur, nxt=nxt, s=s: e.tensor_tensor(
                nxt.t[0:1, s * NE:NTI * NE], cur.t[0:1, s * NE:NTI * NE],
                cur.t[0:1, 0:(NTI - s) * NE], op=ALU.add),
                reads=[cur], writes=[nxt])
            cur, nxt = nxt, cur
            s *= 2
        ctx.dma("sync", dio["cnt_out"].t.ap()[:, :],
                cur.t[0:1, (NTI - 1) * NE:NTI * NE],
                reads=[cur], writes=[dio["cnt_out"]], tag="p4cnt")
        bex = nxt
        ctx.step("vector", lambda e, bex=bex: e.memset(bex.t[0:1, 0:NE], 0.0), writes=[bex])
        ctx.step("vector", lambda e, cur=cur, bex=bex: e.tensor_copy(
            bex.t[0:1, NE:NTI * NE], cur.t[0:1, 0:(NTI - 1) * NE]),
            reads=[cur], writes=[bex])
        for ti in range(NTI):
            def mmpos(e, ti=ti, bex=bex):
                e.matmul(b["pP"].t[:, 0:NE], consts["triu128s"].t[:, :],
                         b["mb"].t[:, ti, :], start=True, stop=False)
                return e.matmul(b["pP"].t[:, 0:NE], long["ones_rowf"].t[0:1, :],
                                bex.t[0:1, ti * NE:(ti + 1) * NE], start=False, stop=True)
            ctx.step("tensor", mmpos, reads=[consts["triu128s"], b["mb"], bex,
                                             long["ones_rowf"]],
                     writes=[b["pP"]], tag=f"pos{ti}")
            ctx.step("vector", lambda e, ti=ti: e.tensor_copy(
                b["wrk"].t[:, ti, :], b["pP"].t[:, 0:NE]),
                reads=[b["pP"]], writes=[b["wrk"]])
        ctx.dma("sync", dio["pos_out"].t.ap()[:, :, :], b["wrk"].t[:], reads=[b["wrk"]],
                writes=[dio["pos_out"]], tag="p4pos")

"""L2 program: global slot addresses + per-rank weights.

in:  w_own, pos_own [128, NTI, 64] f32; cnt_all [8, 64] f32; coremask [1, 8] f32
out: addr_r [128, NTI, 8] i32 (global slot = e*CAP + gpos, OOB=16384 invalid);
     w_r [128, NTI, 8] f32
"""
import numpy as np
from contextlib import ExitStack

NE = 64
CAP = 256
OOBS = float(NE * CAP)


def emit_l2(ctx, long, dio, NB):
    NTI = NB * 16
    with ExitStack() as ph:
        sb = lambda n, s, d: Buf(n, ph.enter_context(ctx.nc.sbuf_tensor(n, list(s), d)))
        ps = lambda n, s, d: Buf(n, ph.enter_context(ctx.nc.psum_tensor(n, list(s), d)))
        b = {}
        b["w"] = sb("l2w", (128, NTI, NE), F32)
        b["pos"] = sb("l2pos", (128, NTI, NE), F32)
        b["cnt"] = sb("l2cnt", (8, NE), F32)
        b["cmask"] = sb("l2cmask", (1, 8), F32)
        b["base"] = sb("l2base", (1, NE), F32)
        b["bcs"] = sb("l2bcs", (128, NE), F32)
        b["vsel"] = sb("l2vsel", (128, NTI, NE), F32)
        b["rk"] = sb("l2rk", (128, NTI, NE), F32)
        b["tmp"] = sb("l2tmp", (128, NTI, NE), F32)
        b["gpos"] = sb("l2gpos", (128, NTI, NE), F32)
        b["ecap"] = sb("l2ecap", (128, NE), F32)
        b["arf"] = sb("l2arf", (128, NTI), F32)
        b["eqm"] = sb("l2eqm", (128, NTI, NE), F32)
        b["has"] = sb("l2has", (128, NTI), F32)
        b["ar"] = sb("l2ar", (128, NTI, 8), I32)
        b["wr"] = sb("l2wr", (128, NTI, 8), F32)
        b["cmcol"] = sb("l2cmcol", (8, 1), F32)
        b["pB"] = ps("l2pB", (128, 512), F32)
        ctx.dma("sync", b["w"].t[:], dio["w_own"].t.ap()[:, :, :],
                reads=[dio["w_own"]], writes=[b["w"]])
        ctx.dma("sync", b["pos"].t[:], dio["pos_own"].t.ap()[:, :, :],
                reads=[dio["pos_own"]], writes=[b["pos"]])
        ctx.dma("sync", b["cnt"].t[:], dio["cnt_all"].t.ap()[:, :],
                reads=[dio["cnt_all"]], writes=[b["cnt"]])
        ctx.dma("sync", b["cmask"].t[:], dio["coremask"].t.ap()[:, :],
                reads=[dio["coremask"]], writes=[b["cmask"]])
        def mmt(e):
            return e.matmul(b["pB"].t[0:8, 0:1], b["cmask"].t[0:1, :],
                            long["ones_rowf"].t[:, 0:1], start=True, stop=True)
        ctx.step("tensor", mmt, reads=[b["cmask"], long["ones_rowf"]], writes=[b["pB"]], tag="cmT")
        ctx.step("vector", lambda e: e.tensor_copy(b["cmcol"].t[:], b["pB"].t[0:8, 0:1]),
                 reads=[b["pB"]], writes=[b["cmcol"]])
        def mmb(e):
            return e.matmul(b["pB"].t[0:1, 0:NE], b["cmcol"].t[:, :], b["cnt"].t[:, :],
                            start=True, stop=True)
        ctx.step("tensor", mmb, reads=[b["cmcol"], b["cnt"]], writes=[b["pB"]], tag="baserow")
        ctx.step("vector", lambda e: e.tensor_copy(b["base"].t[:], b["pB"].t[0:1, 0:NE]),
                 reads=[b["pB"]], writes=[b["base"]])
        def mmbc(e):
            return e.matmul(b["pB"].t[:, 0:NE], long["ones_rowf"].t[0:1, :], b["base"].t[0:1, :],
                            start=True, stop=True)
        ctx.step("tensor", mmbc, reads=[b["base"], long["ones_rowf"]], writes=[b["pB"]], tag="bbc")
        ctx.step("vector", lambda e: e.tensor_copy(b["bcs"].t[:], b["pB"].t[:, 0:NE]),
                 reads=[b["pB"]], writes=[b["bcs"]])
        ctx.step("vector", lambda e: e.tensor_tensor(
            b["gpos"].t[:], b["pos"].t[:],
            b["bcs"].t[:].rearrange("p e -> p () e").broadcast_to([128, NTI, NE]), op=ALU.add),
            reads=[b["pos"], b["bcs"]], writes=[b["gpos"]])
        ctx.step("vector", lambda e: e.tensor_scalar(
            b["vsel"].t[:], b["w"].t[:], 0.0, None, op0=ALU.is_gt),
            reads=[b["w"]], writes=[b["vsel"]])
        ctx.step("vector", lambda e: e.tensor_scalar(
            b["tmp"].t[:], b["gpos"].t[:], float(CAP) - 0.5, None, op0=ALU.is_lt),
            reads=[b["gpos"]], writes=[b["tmp"]])
        ctx.step("vector", lambda e: e.tensor_tensor(
            b["vsel"].t[:], b["vsel"].t[:], b["tmp"].t[:], op=ALU.mult),
            reads=[b["vsel"], b["tmp"]], writes=[b["vsel"]])
        ctx.step("gpsimd", lambda e: e.iota(b["ecap"].t[:], pattern=[[CAP, NE]], base=0,
                                            channel_multiplier=0,
                                            allow_small_or_imprecise_dtypes=True),
                 writes=[b["ecap"]])
        ctx.step("vector", lambda e: e.tensor_tensor(
            b["tmp"].t[:], b["gpos"].t[:],
            b["ecap"].t[:].rearrange("p e -> p () e").broadcast_to([128, NTI, NE]), op=ALU.add),
            reads=[b["gpos"], b["ecap"]], writes=[b["tmp"]])
        ctx.step("vector", lambda e: e.tensor_scalar(
            b["tmp"].t[:], b["tmp"].t[:], 1.0, -OOBS, op0=ALU.mult, op1=ALU.add),
            reads=[b["tmp"]], writes=[b["tmp"]])
        ctx.step("vector", lambda e: e.tensor_tensor(
            b["tmp"].t[:], b["tmp"].t[:], b["vsel"].t[:], op=ALU.mult),
            reads=[b["tmp"], b["vsel"]], writes=[b["tmp"]])
        ctx.step("vector", lambda e: e.tensor_scalar(
            b["tmp"].t[:], b["tmp"].t[:], 1.0, OOBS, op0=ALU.mult, op1=ALU.add),
            reads=[b["tmp"]], writes=[b["tmp"]])
        ctx.step("vector", lambda e: e.tensor_tensor(
            b["w"].t[:], b["w"].t[:], b["vsel"].t[:], op=ALU.mult),
            reads=[b["w"], b["vsel"]], writes=[b["w"]])
        for ti in range(NTI):
            ctx.step("vector", lambda e, ti=ti: e.tensor_tensor_scan(
                b["rk"].t[:, ti, :], b["vsel"].t[:, ti, :], b["vsel"].t[:, ti, :],
                0.0, op0=ALU.add, op1=ALU.bypass),
                reads=[b["vsel"]], writes=[b["rk"]])
        ctx.step("vector", lambda e: e.tensor_tensor(
            b["rk"].t[:], b["rk"].t[:], b["vsel"].t[:], op=ALU.mult),
            reads=[b["rk"], b["vsel"]], writes=[b["rk"]])
        for r in range(8):
            ctx.step("vector", lambda e, r=r: e.tensor_scalar(
                b["eqm"].t[:], b["rk"].t[:], float(r + 1), None, op0=ALU.is_equal),
                reads=[b["rk"]], writes=[b["eqm"]])
            ctx.step("vector", lambda e: e.tensor_reduce(
                b["has"].t[:].rearrange("p t -> p t ()"), b["eqm"].t[:], axis=AX.X, op=ALU.add),
                reads=[b["eqm"]], writes=[b["has"]])
            ctx.step("vector", lambda e: e.tensor_tensor(
                b["eqm"].t[:], b["eqm"].t[:], b["tmp"].t[:], op=ALU.mult),
                reads=[b["eqm"], b["tmp"]], writes=[b["eqm"]])
            ctx.step("vector", lambda e: e.tensor_reduce(
                b["arf"].t[:].rearrange("p t -> p t ()"), b["eqm"].t[:], axis=AX.X, op=ALU.add),
                reads=[b["eqm"]], writes=[b["arf"]])
            ctx.step("vector", lambda e: e.scalar_tensor_tensor(
                b["arf"].t[:], b["has"].t[:], -OOBS, b["arf"].t[:], op0=ALU.mult, op1=ALU.add),
                reads=[b["has"], b["arf"]], writes=[b["arf"]])
            ctx.step("vector", lambda e: e.tensor_scalar(
                b["arf"].t[:], b["arf"].t[:], 1.0, OOBS, op0=ALU.mult, op1=ALU.add),
                reads=[b["arf"]], writes=[b["arf"]])
            ctx.step("vector", lambda e, r=r: e.tensor_copy(
                b["ar"].t[:, :, r:r + 1], b["arf"].t[:].rearrange("p t -> p t ()")),
                reads=[b["arf"]], writes=[b["ar"]])
            ctx.step("vector", lambda e, r=r: e.tensor_scalar(
                b["eqm"].t[:], b["rk"].t[:], float(r + 1), None, op0=ALU.is_equal),
                reads=[b["rk"]], writes=[b["eqm"]])
            ctx.step("vector", lambda e: e.tensor_tensor(
                b["eqm"].t[:], b["eqm"].t[:], b["w"].t[:], op=ALU.mult),
                reads=[b["eqm"], b["w"]], writes=[b["eqm"]])
            ctx.step("vector", lambda e: e.tensor_reduce(
                b["arf"].t[:].rearrange("p t -> p t ()"), b["eqm"].t[:], axis=AX.X, op=ALU.add),
                reads=[b["eqm"]], writes=[b["arf"]])
            ctx.step("vector", lambda e, r=r: e.tensor_copy(
                b["wr"].t[:, :, r:r + 1], b["arf"].t[:].rearrange("p t -> p t ()")),
                reads=[b["arf"]], writes=[b["wr"]])
        ctx.dma("sync", dio["addr_r"].t.ap()[:, :, :], b["ar"].t[:],
                reads=[b["ar"]], writes=[dio["addr_r"]], tag="sar")
        ctx.dma("sync", dio["w_r"].t.ap()[:, :, :], b["wr"].t[:],
                reads=[b["wr"]], writes=[dio["w_r"]], tag="swr")

"""L3 program: expert FFN on this core's 8-expert dispatch shard.

in:  disp [2048, 1024] bf16 (rows = 8 experts x 256 slots, token-major);
     wg_t, wu_t [8, 1024, 768] f32; wd_t [8, 768, 1024] f32
out: y [2048, 1024] bf16
"""
import numpy as np
from contextlib import ExitStack

CAP = 256
INTER = 768
NEL = 8


def emit_l3(ctx, consts, long, dio):
    with ExitStack() as ph:
        sb = lambda n, s, d: Buf(n, ph.enter_context(ctx.nc.sbuf_tensor(n, list(s), d)))
        ps = lambda n, s, d: Buf(n, ph.enter_context(ctx.nc.psum_tensor(n, list(s), d)))
        b = {}
        b["dtm"] = sb("l3dtm", (128, 2, 1024), BF16)
        b["dfm"] = sb("l3dfm", (128, 8, CAP), BF16)
        b["wg"] = [sb(f"l3wg{i}", (128, 8, INTER), BF16) for i in range(2)]
        b["wu"] = [sb(f"l3wu{i}", (128, 8, INTER), BF16) for i in range(2)]
        b["wd"] = [sb(f"l3wd{i}", (128, 6, 1024), BF16) for i in range(2)]
        b["sil"] = sb("l3sil", (128, 512), BF16)
        b["gu"] = sb("l3gu", (128, 6, CAP), BF16)
        b["ytm"] = sb("l3ytm", (128, 2, 1024), BF16)
        b["pG"] = ps("l3pG", (128, 512), F32)
        b["pY"] = ps("l3pY", (128, 1024), F32)
        b["pT"] = ps("l3pT", (128, 1024), BF16)
        for ei in range(NEL):
            par = ei % 2
            for dc in range(8):
                ctx.dma("gpsimd", b["wg"][par].t[:, dc, :],
                        dio["wg_t"].t.ap()[ei, dc * 128:(dc + 1) * 128, :],
                        reads=[dio["wg_t"]], writes=[b["wg"][par]], tag=f"l3wg{ei}.{dc}")
                ctx.dma("gpsimd", b["wu"][par].t[:, dc, :],
                        dio["wu_t"].t.ap()[ei, dc * 128:(dc + 1) * 128, :],
                        reads=[dio["wu_t"]], writes=[b["wu"][par]], tag=f"l3wu{ei}.{dc}")
            for ic in range(6):
                ctx.dma("gpsimd", b["wd"][par].t[:, ic, :],
                        dio["wd_t"].t.ap()[ei, ic * 128:(ic + 1) * 128, :],
                        reads=[dio["wd_t"]], writes=[b["wd"][par]], tag=f"l3wd{ei}.{ic}")
            for half in range(2):
                ctx.dma("sync", b["dtm"].t[:, half, :],
                        dio["disp"].t.ap()[ei * CAP + half * 128:ei * CAP + (half + 1) * 128, :],
                        reads=[dio["disp"]], writes=[b["dtm"]], tag=f"l3d{ei}.{half}")
            for rnd in range(2):
                def mmtr(e, rnd=rnd):
                    last = None
                    for i in range(8):
                        dc = (rnd * 8 + i) // 2
                        half = (rnd * 8 + i) % 2
                        last = e.transpose(
                            b["pT"].t[:, i * 128:(i + 1) * 128],
                            b["dtm"].t[:, half, dc * 128:(dc + 1) * 128],
                            consts["ident128"].t[:, :])
                    return last
                ctx.step("tensor", mmtr, reads=[b["dtm"], consts["ident128"]],
                         writes=[b["pT"]], tag=f"l3tr{ei}.{rnd}")
                def evac(e, rnd=rnd):
                    last = None
                    for i in range(8):
                        dc = (rnd * 8 + i) // 2
                        half = (rnd * 8 + i) % 2
                        last = e.tensor_copy(
                            b["dfm"].t[:, dc, half * 128:(half + 1) * 128],
                            b["pT"].t[:, i * 128:(i + 1) * 128])
                    return last
                ctx.step("vector", evac, reads=[b["pT"]], writes=[b["dfm"]],
                         tag=f"l3ev{ei}.{rnd}")
            for ic in range(6):
                def mmgu(e, ic=ic, par=par):
                    last = None
                    for dc in range(8):
                        last = e.matmul(b["pG"].t[:, 0:CAP],
                                        b["wg"][par].t[:, dc, ic * 128:(ic + 1) * 128],
                                        b["dfm"].t[:, dc, :], start=(dc == 0), stop=(dc == 7))
                    for dc in range(8):
                        last = e.matmul(b["pG"].t[:, CAP:2 * CAP],
                                        b["wu"][par].t[:, dc, ic * 128:(ic + 1) * 128],
                                        b["dfm"].t[:, dc, :], start=(dc == 0), stop=(dc == 7))
                    return last
                ctx.step("tensor", mmgu, reads=[b["wg"][par], b["wu"][par], b["dfm"]],
                         writes=[b["pG"]], tag=f"l3gu{ei}.{ic}")
                ctx.step("scalar", lambda e: e.activation(
                    out=b["sil"].t[:, 0:CAP], in_=b["pG"].t[:, 0:CAP], func=AF.Silu),
                    reads=[b["pG"]], writes=[b["sil"]])
                ctx.step("vector", lambda e, ic=ic: e.tensor_tensor(
                    b["gu"].t[:, ic, :], b["pG"].t[:, CAP:2 * CAP], b["sil"].t[:, 0:CAP],
                    op=ALU.mult),
                    reads=[b["pG"], b["sil"]], writes=[b["gu"]])
            for dgrp in range(4):
                def mmy(e, dgrp=dgrp, par=par):
                    last = None
                    for di in range(2):
                        dc2 = dgrp * 2 + di
                        for ic in range(6):
                            last = e.matmul(b["pY"].t[:, di * 512:di * 512 + CAP],
                                            b["wd"][par].t[:, ic, dc2 * 128:(dc2 + 1) * 128],
                                            b["gu"].t[:, ic, :], start=(ic == 0), stop=(ic == 5))
                    return last
                ctx.step("tensor", mmy, reads=[b["wd"][par], b["gu"]], writes=[b["pY"]],
                         tag=f"l3y{ei}.{dgrp}")
                for di in range(2):
                    dc2 = dgrp * 2 + di
                    ctx.step("scalar", lambda e, di=di: e.activation(
                        out=b["sil"].t[:, 0:CAP],
                        in_=b["pY"].t[:, di * 512:di * 512 + CAP], func=AF.Copy),
                        reads=[b["pY"]], writes=[b["sil"]])
                    def mmtrb(e):
                        last = None
                        for half in range(2):
                            last = e.transpose(
                                b["pT"].t[:, half * 128:(half + 1) * 128],
                                b["sil"].t[:, half * 128:(half + 1) * 128],
                                consts["ident128"].t[:, :])
                        return last
                    ctx.step("tensor", mmtrb, reads=[b["sil"], consts["ident128"]],
                             writes=[b["pT"]], tag=f"l3tb{ei}.{dc2}")
                    def evb(e, dc2=dc2):
                        last = None
                        for half in range(2):
                            last = e.tensor_copy(
                                b["ytm"].t[:, half, dc2 * 128:(dc2 + 1) * 128],
                                b["pT"].t[:, half * 128:(half + 1) * 128])
                        return last
                    ctx.step("vector", evb, reads=[b["pT"]], writes=[b["ytm"]],
                             tag=f"l3eb{ei}.{dc2}")
            for half in range(2):
                ctx.dma("gpsimd", dio["y"].t.ap()[ei * CAP + half * 128:ei * CAP + (half + 1) * 128, :],
                        b["ytm"].t[:, half, :], reads=[b["ytm"]], writes=[dio["y"]],
                        tag=f"l3sy{ei}.{half}")

"""Standalone routing pass: from x_fm produce ytm_d (token-major LN(x), bf16),
logits_d (TM), then top-8/w/pos/cnt via emit_p4.

dio keys (with prefix p): {p}x_fm [8,128,TT] f32 in; gw_t [1024,64] in;
{p}ytm_d [TT, 1024] bf16 out; {p}logits_d [128, NTI, 64] f32 internal;
{p}w_out, {p}pos_out [128, NTI, 64] f32 out; {p}cnt_out [1,64] out.
"""
import numpy as np
from contextlib import ExitStack

D = 1024
STRIP = 512
NT = 4
NE = 64


def emit_route(ctx, consts, long, dio, p, NB):
    with ExitStack() as ph:
        sb = lambda n, s, d: Buf(n, ph.enter_context(ctx.nc.sbuf_tensor(n, list(s), d)))
        ps = lambda n, s, d: Buf(n, ph.enter_context(ctx.nc.psum_tensor(n, list(s), d)))
        gwh = sb(p + "gwh", (128, 8, NE), BF16)
        gwl = sb(p + "gwl", (128, 8, NE), BF16)
        gwf = sb(p + "gwf", (128, 8, NE), F32)
        for dc in range(8):
            ctx.dma("sync", gwf.t[:, dc, :], dio["gw_t"].t.ap()[dc * 128:(dc + 1) * 128, :],
                    reads=[dio["gw_t"]], writes=[gwf])
        ctx.step("vector", lambda e: e.tensor_copy(gwh.t[:], gwf.t[:]), reads=[gwf], writes=[gwh])
        ctx.step("vector", lambda e: e.tensor_tensor(gwf.t[:], gwf.t[:], gwh.t[:], op=ALU.subtract),
                 reads=[gwf, gwh], writes=[gwf])
        ctx.step("vector", lambda e: e.tensor_copy(gwl.t[:], gwf.t[:]), reads=[gwf], writes=[gwl])
        b = {}
        b["x"] = sb(p + "rx", (128, 8, STRIP), F32)
        b["yh"] = sb(p + "ryh", (128, 8, STRIP), BF16)
        b["yl"] = sb(p + "ryl", (128, 8, STRIP), BF16)
        b["sq"] = sb(p + "rsq", (128, 8, STRIP), BF16)
        b["ytm"] = sb(p + "rytm", (128, 1024), BF16)
        b["lgst"] = sb(p + "rlg", (128, NT, NE), F32)
        b["srow"] = sb(p + "rsrow", (1, 2 * STRIP), F32)
        b["murow"] = sb(p + "rmur", (1, STRIP), F32)
        b["rstd"] = sb(p + "rrstd", (1, STRIP), F32)
        b["pAB"] = ps(p + "rpAB", (128, 1024), F32)
        b["pR"] = ps(p + "rpR", (1, 512), F32)
        b["pL"] = ps(p + "rpL", (128, 512), F32)
        b["pT"] = ps(p + "rpT", (128, 1024), BF16)
        for st in range(NB * 4):
            col0 = st * STRIP
            for hc in range(2):
                ctx.dma("gpsimd" if hc else "sync", b["x"].t[:, hc * 4:(hc + 1) * 4, :],
                        dio[p + "x_fm"].t.ap()[hc * 4:(hc + 1) * 4, :, col0:col0 + STRIP]
                        .rearrange("c p k -> p c k"),
                        reads=[dio[p + "x_fm"]], writes=[b["x"]], tag=f"{p}rx{st}.{hc}")
            ctx.step("vector", lambda e: e.tensor_copy(b["yh"].t[:], b["x"].t[:]),
                     reads=[b["x"]], writes=[b["yh"]])
            ctx.step("scalar", lambda e: e.activation(
                out=b["sq"].t[:], in_=b["yh"].t[:], func=AF.Square),
                reads=[b["yh"]], writes=[b["sq"]])
            for which, src, off in (("x", b["yh"], 0), ("x2", b["sq"], STRIP)):
                def mmst(e, src=src):
                    last = None
                    for dc in range(8):
                        last = e.matmul(b["pR"].t[0:1, 0:STRIP], long["ones_col"].t[:, 0:1],
                                        src.t[:, dc, :], start=(dc == 0), stop=(dc == 7))
                    return last
                ctx.step("tensor", mmst, reads=[src, long["ones_col"]], writes=[b["pR"]],
                         tag=f"{p}rst{st}.{which}")
                ctx.step("vector", lambda e, off=off: e.tensor_copy(
                    b["srow"].t[0:1, off:off + STRIP], b["pR"].t[0:1, 0:STRIP]),
                    reads=[b["pR"]], writes=[b["srow"]])
            rowmath_ln(ctx, (b["srow"], b["srow"].t[0:1, 0:STRIP]),
                             (b["srow"], b["srow"].t[0:1, STRIP:2 * STRIP]),
                             (b["murow"], b["murow"].t[:]), (b["rstd"], b["rstd"].t[:]))
            def mmab(e):
                e.matmul(b["pAB"].t[:, 0:512], long["ones_rowf"].t[0:1, :], b["rstd"].t[0:1, :],
                         start=True, stop=True)
                return e.matmul(b["pAB"].t[:, 512:1024], long["ones_rowf"].t[0:1, :],
                                b["murow"].t[0:1, :], start=True, stop=True)
            ctx.step("tensor", mmab, reads=[b["rstd"], b["murow"], long["ones_rowf"]],
                     writes=[b["pAB"]], tag=f"{p}rab{st}")
            for dc in range(8):
                ctx.step("vector", lambda e, dc=dc: e.tensor_tensor(
                    b["x"].t[:, dc, :], b["x"].t[:, dc, :], b["pAB"].t[:, 512:1024], op=ALU.add),
                    reads=[b["x"], b["pAB"]], writes=[b["x"]])
                ctx.step("vector", lambda e, dc=dc: e.tensor_tensor(
                    b["x"].t[:, dc, :], b["x"].t[:, dc, :], b["pAB"].t[:, 0:512], op=ALU.mult),
                    reads=[b["x"], b["pAB"]], writes=[b["x"]])
            ctx.step("vector", lambda e: e.tensor_copy(b["yh"].t[:], b["x"].t[:]),
                     reads=[b["x"]], writes=[b["yh"]])
            ctx.step("vector", lambda e: e.tensor_tensor(b["yl"].t[:], b["x"].t[:], b["yh"].t[:],
                                                         op=ALU.subtract),
                     reads=[b["x"], b["yh"]], writes=[b["yl"]])
            for tt in range(NT):
                ti = st * NT + tt
                def mml(e, tt=tt):
                    last = None
                    for dc in range(8):
                        last = e.matmul(b["pL"].t[:, tt * NE:(tt + 1) * NE],
                                        b["yh"].t[:, dc, tt * 128:(tt + 1) * 128],
                                        gwh.t[:, dc, :], start=(dc == 0), stop=False)
                    for dc in range(8):
                        last = e.matmul(b["pL"].t[:, tt * NE:(tt + 1) * NE],
                                        b["yl"].t[:, dc, tt * 128:(tt + 1) * 128],
                                        gwh.t[:, dc, :], start=False, stop=False)
                    for dc in range(8):
                        last = e.matmul(b["pL"].t[:, tt * NE:(tt + 1) * NE],
                                        b["yh"].t[:, dc, tt * 128:(tt + 1) * 128],
                                        gwl.t[:, dc, :], start=False, stop=(dc == 7))
                    return last
                ctx.step("tensor", mml, reads=[b["yh"], b["yl"], gwh, gwl],
                         writes=[b["pL"]], tag=f"{p}rl{st}.{tt}")
                ctx.step("vector", lambda e, tt=tt: e.tensor_copy(
                    b["lgst"].t[:, tt, :], b["pL"].t[:, tt * NE:(tt + 1) * NE]),
                    reads=[b["pL"]], writes=[b["lgst"]])
                def mmtr(e, tt=tt):
                    last = None
                    for dc in range(8):
                        last = e.transpose(b["pT"].t[:, dc * 128:(dc + 1) * 128],
                                           b["yh"].t[:, dc, tt * 128:(tt + 1) * 128],
                                           consts["ident128"].t[:, :])
                    return last
                ctx.step("tensor", mmtr, reads=[b["yh"], consts["ident128"]], writes=[b["pT"]],
                         tag=f"{p}rtr{ti}")
                ctx.step("vector", lambda e: e.tensor_copy(b["ytm"].t[:], b["pT"].t[:]),
                         reads=[b["pT"]], writes=[b["ytm"]])
                ctx.dma("gpsimd", dio[p + "ytm_d"].t.ap()[ti * 128:(ti + 1) * 128, :],
                        b["ytm"].t[:], reads=[b["ytm"]], writes=[dio[p + "ytm_d"]],
                        tag=f"{p}sytm{ti}")
            ctx.dma("sync", dio[p + "logits_d"].t.ap()[:, st * NT:(st + 1) * NT, :],
                    b["lgst"].t[:], reads=[b["lgst"]], writes=[dio[p + "logits_d"]],
                    tag=f"{p}slg{st}")
    ctx.barrier_all()
    emit_p4_pfx(ctx, consts, long, dio, p, NB)


def emit_p4_pfx(ctx, consts, long, dio, p, NB):
    sub = dict(dio)
    sub["logits_d"] = dio[p + "logits_d"]
    sub["w_out"] = dio[p + "w_out"]
    sub["pos_out"] = dio[p + "pos_out"]
    sub["cnt_out"] = dio[p + "cnt_out"]
    emit_p4(ctx, consts, long, sub)

"""attn_b block emitter: standard softmax-attention transformer block with the
reference's odd softmax-over-query-axis, plus the surrounding residual/LN.

Sequence per launch-4:
  in: x1_fm [8,128,TT] f32, yplanes [8, TT, 1024] bf16 (MoE entry planes),
      w_r [128, NTI, 8] f32, in_proj_t [1024, 3072] f32 (host-transposed),
      in_proj_b [1, 3072], out_w_t [1024, 1024], out_b [1,1024],
      lin1_t [1024, 4096], lin1_b [1,4096], lin2_t [4096, 1024], lin2_b [1,1024],
      ln1_g/b, ln2_g/b [1,1024], gw_t [1024, 64]
  out: x3_fm, s3_out, w3/pos3/cnt3 (via emit_routing), y3tm_d
  internal: x2_d f32, y2_d f32, post_d...

Phases:
  B0: delta = sum_r w_r*plane_r -> X2 = x1 + delta (FM via transposes) -> x2_d
      + stats -> y2 = LN(X2) -> y2_d (f32) + y2bf strips -> q/k/v projections
  B1: attention per (b, h) -> post (FM, per b) -> out-proj + out_b -> attn_out
      t = y2 + attn_out -> LN1(g,b) -> xh -> FFN -> +xh -> LN2 -> ret
      X3 = x2 + ret -> x3_fm
"""
import numpy as np
from contextlib import ExitStack

H = 16
DH = 64
D = 1024
T = 2048
STRIP = 512
NT = 4
SC = 1.0 / np.sqrt(DH)
ISQ = 1.0 / np.sqrt(2048.0)


def emit_attnb(ctx, consts, long, dio, NB, upto=99):
    TT = NB * T
    emit_b0(ctx, consts, long, dio, NB)
    if upto < 1:
        return
    ctx.barrier_all()
    emit_b1(ctx, consts, long, dio, NB)
    if upto < 2:
        return
    ctx.barrier_all()
    emit_b2(ctx, consts, long, dio, NB, upto)


def emit_b0(ctx, consts, long, dio, NB):
    with ExitStack() as ph:
        sb = lambda n, s, d: Buf(n, ph.enter_context(ctx.nc.sbuf_tensor(n, list(s), d)))
        ps = lambda n, s, d: Buf(n, ph.enter_context(ctx.nc.psum_tensor(n, list(s), d)))
        b = {}
        b["wr"] = sb("b0_wr", (128, NB * 16, 8), F32)
        b["pl"] = [sb(f"b0_pl{i}", (128, 8, 1024), BF16) for i in range(2)]
        b["acc"] = sb("b0_acc", (128, 1024), F32)
        b["dtb"] = [sb(f"b0_dtb{i}", (128, 1024), BF16) for i in range(2)]
        b["xfm"] = [sb(f"b0_xfm{i}", (128, 8, STRIP), BF16) for i in range(2)]
        b["x"] = [sb(f"b0_x{i}", (128, 8, STRIP), F32) for i in range(2)]
        b["xb"] = [sb(f"b0_xb{i}", (128, 8, STRIP), BF16) for i in range(2)]
        b["sq"] = [sb(f"b0_sq{i}", (128, 8, STRIP), BF16) for i in range(2)]
        b["srow"] = [sb(f"b0_srow{i}", (1, 2 * STRIP), F32) for i in range(2)]
        b["murow"] = [sb(f"b0_mur{i}", (1, STRIP), F32) for i in range(2)]
        b["rstd"] = [sb(f"b0_rstd{i}", (1, STRIP), F32) for i in range(2)]
        b["pT"] = [ps(f"b0_pT{i}", (128, 1024), BF16) for i in range(2)]
        b["pAB"] = [ps(f"b0_pAB{i}", (128, 1024), F32) for i in range(2)]
        b["pR"] = [ps(f"b0_pR{i}", (1, 512), F32) for i in range(2)]
        ctx.dma("sync", b["wr"].t[:], dio["w_r"].t.ap()[:, :, :],
                reads=[dio["w_r"]], writes=[b["wr"]])
        for st in range(NB * 4):
            par = st % 2
            col0 = st * STRIP
            x, xfm = b["x"][par], b["xfm"][par]
            for hc in range(2):
                ctx.dma("gpsimd" if hc else "sync", x.t[:, hc * 4:(hc + 1) * 4, :],
                        dio["x1_fm"].t.ap()[hc * 4:(hc + 1) * 4, :, col0:col0 + STRIP]
                        .rearrange("c p k -> p c k"),
                        reads=[dio["x1_fm"]], writes=[x], tag=f"b0x{st}.{hc}")
            for ti4 in range(4):
                ti = st * 4 + ti4
                tp = ti % 2
                pl, dtb, pT = b["pl"][tp], b["dtb"][tp], b["pT"][tp]
                for hr in range(2):
                    ctx.dma("gpsimd" if hr else "sync", pl.t[:, hr * 4:(hr + 1) * 4, :],
                            dio["yplanes"].t.ap()[hr * 4:(hr + 1) * 4,
                                                  ti * 128:(ti + 1) * 128, :]
                            .rearrange("r t d -> t r d"),
                            reads=[dio["yplanes"]], writes=[pl], tag=f"b0pl{ti}.{hr}")
                ctx.step("vector", lambda e, ti=ti, pl=pl: e.tensor_scalar(
                    b["acc"].t[:], pl.t[:, 0, :], b["wr"].t[:, ti, 0:1], None, op0=ALU.mult),
                    reads=[pl, b["wr"]], writes=[b["acc"]])
                for r in range(1, 8):
                    ctx.step("vector", lambda e, ti=ti, r=r, pl=pl: e.scalar_tensor_tensor(
                        b["acc"].t[:], pl.t[:, r, :], b["wr"].t[:, ti, r:r + 1],
                        b["acc"].t[:], op0=ALU.mult, op1=ALU.add),
                        reads=[pl, b["wr"], b["acc"]], writes=[b["acc"]])
                ctx.step("vector", lambda e, dtb=dtb: e.tensor_copy(dtb.t[:], b["acc"].t[:]),
                         reads=[b["acc"]], writes=[dtb])
                def mmtr(e, dtb=dtb, pT=pT):
                    last = None
                    for dc in range(8):
                        last = e.transpose(pT.t[:, dc * 128:(dc + 1) * 128],
                                           dtb.t[:, dc * 128:(dc + 1) * 128],
                                           consts["ident128"].t[:, :])
                    return last
                ctx.step("tensor", mmtr, reads=[dtb, consts["ident128"]], writes=[pT],
                         tag=f"b0tr{ti}")
                ctx.step("vector", lambda e, ti4=ti4, pT=pT, xfm=xfm: e.tensor_copy(
                    xfm.t[:, :, ti4 * 128:(ti4 + 1) * 128],
                    pT.t[:].rearrange("p (c k) -> p c k", c=8)),
                    reads=[pT], writes=[xfm], tag=f"b0ev{ti}")
            xb, sq = b["xb"][par], b["sq"][par]
            for dc in range(8):
                ctx.step("vector", lambda e, dc=dc, x=x, xfm=xfm: e.tensor_tensor(
                    x.t[:, dc, :], x.t[:, dc, :], xfm.t[:, dc, :], op=ALU.add),
                    reads=[x, xfm], writes=[x])
            ctx.dma("scalar", dio["x2_d"].t.ap()[:, :, col0:col0 + STRIP]
                    .rearrange("c p k -> p c k"), x.t[:, :, :],
                    reads=[x], writes=[dio["x2_d"]], tag=f"b0sx2{st}")
            ctx.step("vector", lambda e, x=x, xb=xb: e.tensor_copy(xb.t[:], x.t[:]),
                     reads=[x], writes=[xb])
            ctx.step("scalar", lambda e, xb=xb, sq=sq: e.activation(
                out=sq.t[:], in_=xb.t[:], func=AF.Square),
                reads=[xb], writes=[sq])
            pR, srow = b["pR"][par], b["srow"][par]
            for which, src2 in (("x", xb), ("x2", sq)):
                def mmst(e, src2=src2, pR=pR):
                    last = None
                    for dc in range(8):
                        last = e.matmul(pR.t[0:1, 0:STRIP], long["ones_col"].t[:, 0:1],
                                        src2.t[:, dc, :], start=(dc == 0), stop=(dc == 7))
                    return last
                ctx.step("tensor", mmst, reads=[src2, long["ones_col"]], writes=[pR],
                         tag=f"b0st{st}.{which}")
                off = 0 if which == "x" else STRIP
                ctx.step("vector", lambda e, off=off, pR=pR, srow=srow: e.tensor_copy(
                    srow.t[0:1, off:off + STRIP], pR.t[0:1, 0:STRIP]),
                    reads=[pR], writes=[srow])
            murow, rstd = b["murow"][par], b["rstd"][par]
            rowmath_ln(ctx, (srow, srow.t[0:1, 0:STRIP]),
                             (srow, srow.t[0:1, STRIP:2 * STRIP]),
                             (murow, murow.t[:]), (rstd, rstd.t[:]))
            pAB = b["pAB"][par]
            def mmab(e, pAB=pAB, rstd=rstd, murow=murow):
                e.matmul(pAB.t[:, 0:512], long["ones_rowf"].t[0:1, :],
                         rstd.t[0:1, :], start=True, stop=True)
                return e.matmul(pAB.t[:, 512:1024], long["ones_rowf"].t[0:1, :],
                                murow.t[0:1, :], start=True, stop=True)
            ctx.step("tensor", mmab, reads=[rstd, murow, long["ones_rowf"]],
                     writes=[pAB], tag=f"b0ab{st}")
            for dc in range(8):
                ctx.step("vector", lambda e, dc=dc, x=x, pAB=pAB: e.tensor_tensor(
                    x.t[:, dc, :], x.t[:, dc, :], pAB.t[:, 512:1024], op=ALU.add),
                    reads=[x, pAB], writes=[x])
                ctx.step("vector", lambda e, dc=dc, x=x, pAB=pAB: e.tensor_tensor(
                    x.t[:, dc, :], x.t[:, dc, :], pAB.t[:, 0:512], op=ALU.mult),
                    reads=[x, pAB], writes=[x])
            ctx.dma("scalar", dio["y2_d"].t.ap()[:, :, col0:col0 + STRIP]
                    .rearrange("c p k -> p c k"), x.t[:, :, :],
                    reads=[x], writes=[dio["y2_d"]], tag=f"b0sy2{st}")


def emit_b1(ctx, consts, long, dio, NB):
    with ExitStack() as ph:
        sb = lambda n, s, d: Buf(n, ph.enter_context(ctx.nc.sbuf_tensor(n, list(s), d)))
        ps = lambda n, s, d: Buf(n, ph.enter_context(ctx.nc.psum_tensor(n, list(s), d)))
        Wq = sb("b1_Wq", (128, 8, 1024), BF16)
        Wk = sb("b1_Wk", (128, 8, 1024), BF16)
        Wv = sb("b1_Wv", (128, 8, 1024), BF16)
        for dc in range(8):
            ctx.dma("gpsimd", Wq.t[:, dc, :], dio["in_proj_t"].t.ap()[dc * 128:(dc + 1) * 128, 0:1024],
                    reads=[dio["in_proj_t"]], writes=[Wq], tag=f"b1wq{dc}")
            ctx.dma("gpsimd", Wk.t[:, dc, :], dio["in_proj_t"].t.ap()[dc * 128:(dc + 1) * 128, 1024:2048],
                    reads=[dio["in_proj_t"]], writes=[Wk], tag=f"b1wk{dc}")
            ctx.dma("gpsimd", Wv.t[:, dc, :], dio["in_proj_t"].t.ap()[dc * 128:(dc + 1) * 128, 2048:3072],
                    reads=[dio["in_proj_t"]], writes=[Wv], tag=f"b1wv{dc}")
        bq = sb("b1_bq", (128, 8), F32)
        bk = sb("b1_bk", (128, 8), F32)
        bvr = sb("b1_bvr", (1, 1024), F32)
        btmp = sb("b1_btmp", (1, 1024), F32)
        pC = ps("b1_pS0", (128, 1024), F32)
        for (name, off, dst) in (("bq", 0, bq), ("bk", 1024, bk)):
            ctx.dma("sync", btmp.t[:], dio["in_proj_b"].t.ap()[0:1, off:off + 1024],
                    reads=[dio["in_proj_b"]], writes=[btmp], tag=f"b1{name}")
            def mmc(e):
                last = None
                for dc in range(8):
                    last = e.matmul(pC.t[:, dc:dc + 1], btmp.t[0:1, dc * 128:(dc + 1) * 128],
                                    long["ones_rowf"].t[:, 0:1], start=True, stop=True)
                return last
            ctx.step("tensor", mmc, reads=[btmp, long["ones_rowf"]], writes=[pC], tag=f"b1c{name}")
            ctx.step("vector", lambda e, dst=dst: e.tensor_copy(dst.t[:], pC.t[:, 0:8]),
                     reads=[pC], writes=[dst])
        ctx.dma("sync", bvr.t[:], dio["in_proj_b"].t.ap()[0:1, 2048:3072],
                reads=[dio["in_proj_b"]], writes=[bvr], tag="b1bv")
        bvb = sb("b1_bvb", (1, 1024), BF16)
        ctx.step("vector", lambda e: e.tensor_copy(bvb.t[:], bvr.t[:]), reads=[bvr], writes=[bvb])

        b = {}
        b["y2b"] = sb("b1_y2b", (128, 8, STRIP), BF16)
        b["qf"] = sb("b1_qf", (128, 8, T), BF16)
        b["kf"] = sb("b1_kf", (128, 8, T), BF16)
        b["vt"] = sb("b1_vt", (128, 16, 1024), BF16)
        b["vs"] = [[sb(f"b1_vs{i}_{j}", (128, DH), BF16) for j in range(2)] for i in range(2)]
        b["A"] = [[sb(f"b1_A{i}_{j}", (128, T), BF16) for j in range(2)] for i in range(2)]
        b["z"] = [[sb(f"b1_z{i}_{j}", (128, 4), F32) for j in range(2)] for i in range(2)]
        b["zr"] = [[sb(f"b1_zr{i}_{j}", (128, 1), F32) for j in range(2)] for i in range(2)]
        b["postf"] = [sb(f"b1_postf{i}", (128, T), BF16) for i in range(2)]
        b["pS"] = [pC, ps("b1_pS1", (128, 1024), F32)]
        b["pP"] = ps("b1_pP", (128, 2048), F32)
        for bi in range(NB):
            emit_b1_b(ctx, consts, long, dio, b, Wq, Wk, Wv, bq, bk, bvb, bi)


def emit_b1_b(ctx, consts, long, dio, b, Wq, Wk, Wv, bq, bk, bvb, bi):
    for si in range(NT):
        col0 = bi * T + si * STRIP
        for hc in range(2):
            ctx.dma("gpsimd", b["y2b"].t[:, hc * 4:(hc + 1) * 4, :],
                    dio["y2_d"].t.ap()[hc * 4:(hc + 1) * 4, :, col0:col0 + STRIP]
                    .rearrange("c p k -> p c k"),
                    reads=[dio["y2_d"]], writes=[b["y2b"]], tag=f"b1y{bi}.{si}.{hc}")
        for fc in range(8):
            for wj, (which, W, bias) in enumerate((("q", Wq, bq), ("k", Wk, bk))):
                pS = b["pS"][(fc * 2 + wj) % 2]
                def mmqk(e, fc=fc, W=W, si=si, pS=pS):
                    last = None
                    for dc in range(8):
                        last = e.matmul(pS.t[:, 0:STRIP],
                                        W.t[:, dc, fc * 128:(fc + 1) * 128],
                                        b["y2b"].t[:, dc, :], start=(dc == 0), stop=(dc == 7))
                    return last
                ctx.step("tensor", mmqk, reads=[W, b["y2b"]], writes=[pS],
                         tag=f"b1{which}{bi}.{si}.{fc}")
                dst = b["qf"] if which == "q" else b["kf"]
                bb = bq if which == "q" else bk
                ctx.step("vector", lambda e, fc=fc, si=si, dst=dst, bb=bb, pS=pS: e.tensor_scalar(
                    dst.t[:, fc, si * STRIP:(si + 1) * STRIP], pS.t[:, 0:STRIP],
                    bb.t[:, fc:fc + 1], None, op0=ALU.add),
                    reads=[pS, bb], writes=[dst])
        for tt in range(NT):
            ti = si * NT + tt
            pS = b["pS"][tt % 2]
            def mmv(e, tt=tt, pS=pS):
                last = None
                for dc in range(8):
                    lhsT = b["y2b"].t[:, dc, tt * 128:(tt + 1) * 128]
                    for half in range(2):
                        last = e.matmul(pS.t[:, half * 512:(half + 1) * 512], lhsT,
                                        Wv.t[:, dc, half * 512:(half + 1) * 512],
                                        start=(dc == 0), stop=False)
                for half in range(2):
                    last = e.matmul(pS.t[:, half * 512:(half + 1) * 512],
                                    long["ones_row"].t[0:1, :],
                                    bvb.t[0:1, half * 512:(half + 1) * 512],
                                    start=False, stop=True)
                return last
            ctx.step("tensor", mmv, reads=[b["y2b"], Wv, long["ones_row"], bvb],
                     writes=[pS], tag=f"b1v{bi}.{si}.{tt}")
            ctx.step("vector", lambda e, ti=ti, pS=pS: e.tensor_copy(
                b["vt"].t[:, ti, :], pS.t[:, 0:1024]),
                reads=[pS], writes=[b["vt"]])
    for hp in range(H // 2):
        fc = hp
        for sblk in range(17):
            par = sblk % 2

            def emit_th(th, fc=fc, sblk=sblk, par=par):
                def mms2(e, fc=fc, sblk=sblk, th=th):
                    last = None
                    for tq in range(2):
                        for hi in range(2):
                            hb = hi * 64
                            last = e.matmul(
                                b["pS"][hi].t[:, tq * 512:(tq + 1) * 512],
                                b["kf"].t[hb:hb + 64, fc, sblk * 128:(sblk + 1) * 128],
                                b["qf"].t[hb:hb + 64, fc, th * 1024 + tq * 512:th * 1024 + (tq + 1) * 512],
                                start=True, stop=True, tile_position=(hb, 0))
                    return last
                ctx.step("tensor", mms2, reads=[b["kf"], b["qf"]],
                         writes=[b["pS"][0], b["pS"][1]], tag=f"b1s{bi}.{hp}.{sblk}.{th}")
                for hi in range(2):
                    A, z = b["A"][hi][par], b["z"][hi][par]
                    ctx.step("scalar", lambda e, th=th, hi=hi, A=A, z=z: e.activation(
                        out=A.t[:, th * 1024:(th + 1) * 1024],
                        in_=b["pS"][hi].t[:, 0:1024],
                        func=AF.Exp, scale=SC,
                        accum_out=z.t[:, th * 2:th * 2 + 1]),
                        reads=[b["pS"][hi]], writes=[A, z],
                        tag=f"b1e{bi}.{hp}.{sblk}.{th}.{hi}")

            def emit_mmp_half(half, sblk=sblk):
                pb = sblk - 1
                pp = pb % 2
                def mmp(e, pb=pb, pp=pp, half=half):
                    last = None
                    for tq in (2 * half, 2 * half + 1):
                        for hi in range(2):
                            hb = hi * 64
                            last = e.matmul(
                                b["pP"].t[hb:hb + 64, tq * 512:(tq + 1) * 512],
                                b["vs"][hi][pp].t[:, :],
                                b["A"][hi][pp].t[:, tq * 512:(tq + 1) * 512],
                                start=(pb == 0), stop=(pb == 15),
                                tile_position=(0, hb))
                    return last
                ctx.step("tensor", mmp,
                         reads=[b["vs"][0][pp], b["vs"][1][pp], b["A"][0][pp], b["A"][1][pp]],
                         writes=[b["pP"]], tag=f"b1p{bi}.{hp}.{sblk - 1}.{half}")

            if sblk < 16:
                emit_th(0)
            if sblk >= 1:
                emit_mmp_half(0)
            if sblk < 16:
                emit_th(1)
                for hi in range(2):
                    z, zr, vs = b["z"][hi][par], b["zr"][hi][par], b["vs"][hi][par]
                    h = 2 * hp + hi
                    ctx.step("vector", lambda e, z=z, zr=zr: e.tensor_tensor(
                        zr.t[:], z.t[:, 0:1], z.t[:, 2:3], op=ALU.add),
                        reads=[z], writes=[zr])
                    ctx.step("vector", lambda e, zr=zr: e.reciprocal(zr.t[:], zr.t[:]),
                             reads=[zr], writes=[zr])
                    ctx.step("vector", lambda e, h=h, sblk=sblk, zr=zr, vs=vs: e.tensor_scalar(
                        vs.t[:], b["vt"].t[:, sblk, h * 64:(h + 1) * 64],
                        zr.t[:, 0:1], None, op0=ALU.mult),
                        reads=[b["vt"], zr], writes=[vs])
            if sblk >= 1:
                emit_mmp_half(1)
        pf = b["postf"][hp % 2]
        for hi in range(2):
            hb = hi * 64
            ctx.step("scalar", lambda e, hb=hb, pf=pf: e.activation(
                out=pf.t[hb:hb + 64, :], in_=b["pP"].t[hb:hb + 64, 0:T],
                func=AF.Gelu),
                reads=[b["pP"]], writes=[pf], tag=f"b1g{bi}.{hp}.{hi}")
        ctx.dma("sync", dio["post_d"].t.ap()[fc, :, bi * T:(bi + 1) * T],
                pf.t[:, :], reads=[pf], writes=[dio["post_d"]], tag=f"b1sp{bi}.{fc}")


def emit_b2(ctx, consts, long, dio, NB, upto=99):
    emit_b2a(ctx, consts, long, dio, NB)
    if upto < 3:
        return
    ctx.barrier_all()
    emit_b2b(ctx, consts, long, dio, NB)


def emit_b2a(ctx, consts, long, dio, NB):
    with ExitStack() as ph:
        sb = lambda n, s, d: Buf(n, ph.enter_context(ctx.nc.sbuf_tensor(n, list(s), d)))
        ps = lambda n, s, d: Buf(n, ph.enter_context(ctx.nc.psum_tensor(n, list(s), d)))
        Wo = sb("b2_Wo", (128, 8, 1024), BF16)
        for dc in range(8):
            ctx.dma("gpsimd", Wo.t[:, dc, :], dio["out_w_t"].t.ap()[dc * 128:(dc + 1) * 128, :],
                    reads=[dio["out_w_t"]], writes=[Wo], tag=f"b2wo{dc}")
        ctx.step("vector", lambda e: e.tensor_scalar(
            Wo.t[:], Wo.t[:], ISQ, None, op0=ALU.mult), reads=[Wo], writes=[Wo])
        cols = {}
        pC = ps("b2_pC", (128, 512), F32)
        for n in ("ob", "g1", "bb1"):
            cols[n] = sb("b2_c" + n, (128, 8), F32)
        with ExitStack() as rph:
            rows = {}
            for n, dn, width in (("ob", "out_b", 1024), ("g1", "ln1_g", 1024),
                                 ("bb1", "ln1_b", 1024)):
                rows[n] = Buf("b2_r" + n, rph.enter_context(
                    ctx.nc.sbuf_tensor("b2_r" + n, [1, width], F32)))
                ctx.dma("sync", rows[n].t[:], dio[dn].t.ap()[0:1, :], reads=[dio[dn]],
                        writes=[rows[n]], tag=f"b2r{n}")
            for n in ("ob", "g1", "bb1"):
                def mmc(e, n=n):
                    last = None
                    for c in range(8):
                        last = e.matmul(pC.t[:, c:c + 1], rows[n].t[0:1, c * 128:(c + 1) * 128],
                                        long["ones_rowf"].t[:, 0:1], start=True, stop=True)
                    return last
                ctx.step("tensor", mmc, reads=[rows[n], long["ones_rowf"]], writes=[pC], tag=f"b2c{n}")
                ctx.step("vector", lambda e, n=n: e.tensor_copy(cols[n].t[:], pC.t[:, 0:8]),
                         reads=[pC], writes=[cols[n]])
            ctx.barrier_all()
        b = {}
        b["post"] = [sb(f"b2_post{i}", (128, 8, STRIP), BF16) for i in range(2)]
        b["y2"] = [sb(f"b2_y2{i}", (128, 8, STRIP), F32) for i in range(2)]
        b["t"] = [sb(f"b2_t{i}", (128, 8, STRIP), F32) for i in range(2)]
        b["tb"] = [sb(f"b2_tb{i}", (128, 8, STRIP), BF16) for i in range(2)]
        b["srow"] = [sb(f"b2_srow{i}", (1, 2 * STRIP), F32) for i in range(2)]
        b["murow"] = [sb(f"b2_mur{i}", (1, STRIP), F32) for i in range(2)]
        b["rstd"] = [sb(f"b2_rstd{i}", (1, STRIP), F32) for i in range(2)]
        b["pF"] = [ps(f"b2_pF{i}", (128, STRIP), F32) for i in range(2)]
        b["pAB"] = [ps(f"b2_pAB{i}", (128, 1024), F32) for i in range(2)]
        b["pR"] = ps("b2_pR", (1, 512), F32)
        for st in range(NB * 4):
            par = st % 2
            col0 = st * STRIP
            post, y2, t, tb = b["post"][par], b["y2"][par], b["t"][par], b["tb"][par]
            srow, murow, rstd = b["srow"][par], b["murow"][par], b["rstd"][par]
            pAB = b["pAB"][par]
            for hc in range(2):
                ctx.dma("gpsimd" if hc else "sync", post.t[:, hc * 4:(hc + 1) * 4, :],
                        dio["post_d"].t.ap()[hc * 4:(hc + 1) * 4, :, col0:col0 + STRIP]
                        .rearrange("c p k -> p c k"),
                        reads=[dio["post_d"]], writes=[post], tag=f"b2p{st}.{hc}")
                ctx.dma("gpsimd" if hc else "sync", y2.t[:, hc * 4:(hc + 1) * 4, :],
                        dio["y2_d"].t.ap()[hc * 4:(hc + 1) * 4, :, col0:col0 + STRIP]
                        .rearrange("c p k -> p c k"),
                        reads=[dio["y2_d"]], writes=[y2], tag=f"b2y{st}.{hc}")
            for dc in range(8):
                pF = b["pF"][dc % 2]
                def mmo(e, dc=dc, pF=pF, post=post):
                    last = None
                    for vc in range(8):
                        last = e.matmul(pF.t[:, 0:STRIP], Wo.t[:, vc, dc * 128:(dc + 1) * 128],
                                        post.t[:, vc, :], start=(vc == 0), stop=(vc == 7))
                    return last
                ctx.step("tensor", mmo, reads=[Wo, post], writes=[pF], tag=f"b2o{st}.{dc}")
                ctx.step("vector", lambda e, dc=dc, pF=pF, t=t: e.tensor_scalar(
                    t.t[:, dc, :], pF.t[:, 0:STRIP], cols["ob"].t[:, dc:dc + 1], None,
                    op0=ALU.add),
                    reads=[pF, cols["ob"]], writes=[t])
                ctx.step("vector", lambda e, dc=dc, t=t, y2=y2: e.tensor_tensor(
                    t.t[:, dc, :], t.t[:, dc, :], y2.t[:, dc, :], op=ALU.add),
                    reads=[t, y2], writes=[t])
            ctx.step("vector", lambda e, t=t, tb=tb: e.tensor_copy(tb.t[:], t.t[:]),
                     reads=[t], writes=[tb])
            def mms1(e, tb=tb):
                last = None
                for dc in range(8):
                    last = e.matmul(b["pR"].t[0:1, 0:STRIP], long["ones_col"].t[:, 0:1],
                                    tb.t[:, dc, :], start=(dc == 0), stop=(dc == 7))
                return last
            ctx.step("tensor", mms1, reads=[tb, long["ones_col"]], writes=[b["pR"]],
                     tag=f"b2s1{st}")
            ctx.step("vector", lambda e, srow=srow: e.tensor_copy(
                srow.t[0:1, 0:STRIP], b["pR"].t[0:1, 0:STRIP]),
                reads=[b["pR"]], writes=[srow])
            ctx.step("scalar", lambda e, tb=tb: e.activation(
                out=tb.t[:], in_=tb.t[:], func=AF.Square),
                reads=[tb], writes=[tb])
            def mms2(e, tb=tb):
                last = None
                for dc in range(8):
                    last = e.matmul(b["pR"].t[0:1, 0:STRIP], long["ones_col"].t[:, 0:1],
                                    tb.t[:, dc, :], start=(dc == 0), stop=(dc == 7))
                return last
            ctx.step("tensor", mms2, reads=[tb, long["ones_col"]], writes=[b["pR"]],
                     tag=f"b2s2{st}")
            ctx.step("vector", lambda e, srow=srow: e.tensor_copy(
                srow.t[0:1, STRIP:2 * STRIP], b["pR"].t[0:1, 0:STRIP]),
                reads=[b["pR"]], writes=[srow])
            rowmath_ln(ctx, (srow, srow.t[0:1, 0:STRIP]),
                             (srow, srow.t[0:1, STRIP:2 * STRIP]),
                             (murow, murow.t[:]), (rstd, rstd.t[:]))
            def mmab(e, pAB=pAB, rstd=rstd, murow=murow):
                e.matmul(pAB.t[:, 0:512], long["ones_rowf"].t[0:1, :], rstd.t[0:1, :],
                         start=True, stop=True)
                return e.matmul(pAB.t[:, 512:1024], long["ones_rowf"].t[0:1, :],
                                murow.t[0:1, :], start=True, stop=True)
            ctx.step("tensor", mmab, reads=[rstd, murow, long["ones_rowf"]],
                     writes=[pAB], tag=f"b2ab{st}")
            for dc in range(8):
                ctx.step("vector", lambda e, dc=dc, y2=y2, t=t, pAB=pAB: e.tensor_tensor(
                    y2.t[:, dc, :], t.t[:, dc, :], pAB.t[:, 512:1024], op=ALU.add),
                    reads=[t, pAB], writes=[y2])
                ctx.step("vector", lambda e, dc=dc, y2=y2, pAB=pAB: e.tensor_tensor(
                    y2.t[:, dc, :], y2.t[:, dc, :], pAB.t[:, 0:512], op=ALU.mult),
                    reads=[y2, pAB], writes=[y2])
                ctx.step("vector", lambda e, dc=dc, y2=y2: e.tensor_scalar(
                    y2.t[:, dc, :], y2.t[:, dc, :], cols["g1"].t[:, dc:dc + 1],
                    None, op0=ALU.mult),
                    reads=[y2, cols["g1"]], writes=[y2])
                ctx.step("vector", lambda e, dc=dc, y2=y2: e.tensor_scalar(
                    y2.t[:, dc, :], y2.t[:, dc, :], cols["bb1"].t[:, dc:dc + 1],
                    None, op0=ALU.add),
                    reads=[y2, cols["bb1"]], writes=[y2])
            ctx.dma("scalar", dio["xh_d"].t.ap()[:, :, col0:col0 + STRIP]
                    .rearrange("c p k -> p c k"), y2.t[:, :, :],
                    reads=[y2], writes=[dio["xh_d"]], tag=f"b2sxh{st}")


def emit_b2b(ctx, consts, long, dio, NB):
    with ExitStack() as ph:
        sb = lambda n, s, d: Buf(n, ph.enter_context(ctx.nc.sbuf_tensor(n, list(s), d)))
        ps = lambda n, s, d: Buf(n, ph.enter_context(ctx.nc.psum_tensor(n, list(s), d)))
        W1 = sb("b2_W1", (128, 8, 4096), BF16)
        W2 = sb("b2_W2", (128, 32, 1024), BF16)
        for dc in range(8):
            ctx.dma("gpsimd", W1.t[:, dc, :], dio["lin1_t"].t.ap()[dc * 128:(dc + 1) * 128, :],
                    reads=[dio["lin1_t"]], writes=[W1], tag=f"b2w1{dc}")
        for fc in range(32):
            ctx.dma("gpsimd", W2.t[:, fc, :], dio["lin2_t"].t.ap()[fc * 128:(fc + 1) * 128, :],
                    reads=[dio["lin2_t"]], writes=[W2], tag=f"b2w2{fc}")
        cols = {}
        pC = ps("b2b_pC", (128, 512), F32)
        for n, width in (("b1", 4096), ("b2", 1024), ("g2", 1024), ("bb2", 1024)):
            cols[n] = sb("b2b_c" + n, (128, width // 128), F32)
        with ExitStack() as rph:
            rows = {}
            for n, dn, width in (("b1", "lin1_b", 4096), ("b2", "lin2_b", 1024),
                                 ("g2", "ln2_g", 1024), ("bb2", "ln2_b", 1024)):
                rows[n] = Buf("b2b_r" + n, rph.enter_context(
                    ctx.nc.sbuf_tensor("b2b_r" + n, [1, width], F32)))
                ctx.dma("sync", rows[n].t[:], dio[dn].t.ap()[0:1, :], reads=[dio[dn]],
                        writes=[rows[n]], tag=f"b2br{n}")
            for n, width in (("b1", 4096), ("b2", 1024), ("g2", 1024), ("bb2", 1024)):
                nch = width // 128
                def mmc2(e, n=n, nch=nch):
                    last = None
                    for c in range(nch):
                        last = e.matmul(pC.t[:, c:c + 1],
                                        rows[n].t[0:1, c * 128:(c + 1) * 128],
                                        long["ones_rowf"].t[:, 0:1], start=True, stop=True)
                    return last
                ctx.step("tensor", mmc2, reads=[rows[n], long["ones_rowf"]], writes=[pC], tag=f"b2bc{n}")
                ctx.step("vector", lambda e, n=n, nch=nch: e.tensor_copy(cols[n].t[:], pC.t[:, 0:nch]),
                         reads=[pC], writes=[cols[n]])
            ctx.barrier_all()
        b = {}
        b["xhf"] = sb("b2b_xhf", (128, 8, STRIP), F32)
        b["x2f"] = sb("b2b_x2f", (128, 8, STRIP), F32)
        b["xhb"] = sb("b2b_xhb", (128, 8, STRIP), BF16)
        b["ff"] = sb("b2b_ff", (128, 16, STRIP), BF16)
        b["ub"] = b["xhb"]
        b["srow"] = sb("b2b_srow", (1, 2 * STRIP), F32)
        b["murow"] = sb("b2b_mur", (1, STRIP), F32)
        b["rstd"] = sb("b2b_rstd", (1, STRIP), F32)
        b["pF"] = [ps(f"b2b_pF{i}", (128, STRIP), F32) for i in range(2)]
        b["pW"] = [ps(f"b2b_pW{i}", (128, STRIP), F32) for i in range(2)]
        b["pAB"] = ps("b2b_pAB", (128, 1024), F32)
        b["pR"] = ps("b2b_pR", (1, 512), F32)
        for st in range(NB * 4):
            emit_b2b_strip(ctx, long, dio, b, W1, W2, cols, st)


def emit_b2b_strip(ctx, long, dio, b, W1, W2, cols, st):
    col0 = st * STRIP
    for hc in range(2):
        ctx.dma("gpsimd" if hc else "sync", b["xhf"].t[:, hc * 4:(hc + 1) * 4, :],
                dio["xh_d"].t.ap()[hc * 4:(hc + 1) * 4, :, col0:col0 + STRIP]
                .rearrange("c p k -> p c k"),
                reads=[dio["xh_d"]], writes=[b["xhf"]], tag=f"b2bxh{st}.{hc}")
    ctx.dma("scalar", b["x2f"].t[:, :, :],
            dio["x2_d"].t.ap()[:, :, col0:col0 + STRIP].rearrange("c p k -> p c k"),
            reads=[dio["x2_d"]], writes=[b["x2f"]], tag=f"b2bx2{st}")
    ctx.step("vector", lambda e: e.tensor_copy(b["xhb"].t[:], b["xhf"].t[:]),
             reads=[b["xhf"]], writes=[b["xhb"]])
    for half in range(2):
        for fc2 in range(16):
            fc = half * 16 + fc2
            pF = b["pF"][fc2 % 2]
            def mmf(e, fc=fc, pF=pF):
                last = None
                for dc in range(8):
                    last = e.matmul(pF.t[:, 0:STRIP], W1.t[:, dc, fc * 128:(fc + 1) * 128],
                                    b["xhb"].t[:, dc, :], start=(dc == 0), stop=(dc == 7))
                return last
            ctx.step("tensor", mmf, reads=[W1, b["xhb"]], writes=[pF], tag=f"b2f{st}.{fc}")
            ctx.step("scalar", lambda e, fc=fc, fc2=fc2, pF=pF: e.activation(
                out=b["ff"].t[:, fc2, :], in_=pF.t[:, 0:STRIP], func=AF.Gelu,
                bias=cols["b1"].t[:, fc:fc + 1]),
                reads=[pF, cols["b1"]], writes=[b["ff"]])
        for dc in range(8):
            pW = b["pW"][dc % 2]
            def mmw2(e, dc=dc, half=half, pW=pW):
                last = None
                for fc2 in range(16):
                    last = e.matmul(pW.t[:, 0:STRIP],
                                    W2.t[:, half * 16 + fc2, dc * 128:(dc + 1) * 128],
                                    b["ff"].t[:, fc2, :], start=(fc2 == 0), stop=(fc2 == 15))
                return last
            ctx.step("tensor", mmw2, reads=[W2, b["ff"]], writes=[pW], tag=f"b2l2{st}.{half}.{dc}")
            ctx.step("vector", lambda e, dc=dc, pW=pW: e.tensor_tensor(
                b["xhf"].t[:, dc, :], b["xhf"].t[:, dc, :], pW.t[:, 0:STRIP], op=ALU.add),
                reads=[b["xhf"], pW], writes=[b["xhf"]])
    for dc in range(8):
        ctx.step("vector", lambda e, dc=dc: e.tensor_scalar(
            b["xhf"].t[:, dc, :], b["xhf"].t[:, dc, :], cols["b2"].t[:, dc:dc + 1], None,
            op0=ALU.add),
            reads=[b["xhf"], cols["b2"]], writes=[b["xhf"]])
    ctx.step("vector", lambda e: e.tensor_copy(b["ub"].t[:], b["xhf"].t[:]),
             reads=[b["xhf"]], writes=[b["ub"]])
    def mms1(e):
        last = None
        for dc in range(8):
            last = e.matmul(b["pR"].t[0:1, 0:STRIP], long["ones_col"].t[:, 0:1],
                            b["ub"].t[:, dc, :], start=(dc == 0), stop=(dc == 7))
        return last
    ctx.step("tensor", mms1, reads=[b["ub"], long["ones_col"]], writes=[b["pR"]], tag=f"b2bs1{st}")
    ctx.step("vector", lambda e: e.tensor_copy(b["srow"].t[0:1, 0:STRIP], b["pR"].t[0:1, 0:STRIP]),
             reads=[b["pR"]], writes=[b["srow"]])
    ctx.step("scalar", lambda e: e.activation(
        out=b["ub"].t[:], in_=b["ub"].t[:], func=AF.Square),
        reads=[b["ub"]], writes=[b["ub"]])
    def mms2(e):
        last = None
        for dc in range(8):
            last = e.matmul(b["pR"].t[0:1, 0:STRIP], long["ones_col"].t[:, 0:1],
                            b["ub"].t[:, dc, :], start=(dc == 0), stop=(dc == 7))
        return last
    ctx.step("tensor", mms2, reads=[b["ub"], long["ones_col"]], writes=[b["pR"]], tag=f"b2bs2{st}")
    ctx.step("vector", lambda e: e.tensor_copy(b["srow"].t[0:1, STRIP:2 * STRIP], b["pR"].t[0:1, 0:STRIP]),
             reads=[b["pR"]], writes=[b["srow"]])
    rowmath_ln(ctx, (b["srow"], b["srow"].t[0:1, 0:STRIP]),
                     (b["srow"], b["srow"].t[0:1, STRIP:2 * STRIP]),
                     (b["murow"], b["murow"].t[:]), (b["rstd"], b["rstd"].t[:]))
    def mmab2(e):
        e.matmul(b["pAB"].t[:, 0:512], long["ones_rowf"].t[0:1, :], b["rstd"].t[0:1, :],
                 start=True, stop=True)
        return e.matmul(b["pAB"].t[:, 512:1024], long["ones_rowf"].t[0:1, :],
                        b["murow"].t[0:1, :], start=True, stop=True)
    ctx.step("tensor", mmab2, reads=[b["rstd"], b["murow"], long["ones_rowf"]],
             writes=[b["pAB"]], tag=f"b2ba{st}")
    for dc in range(8):
        ctx.step("vector", lambda e, dc=dc: e.tensor_tensor(
            b["xhf"].t[:, dc, :], b["xhf"].t[:, dc, :], b["pAB"].t[:, 512:1024], op=ALU.add),
            reads=[b["xhf"], b["pAB"]], writes=[b["xhf"]])
        ctx.step("vector", lambda e, dc=dc: e.tensor_tensor(
            b["xhf"].t[:, dc, :], b["xhf"].t[:, dc, :], b["pAB"].t[:, 0:512], op=ALU.mult),
            reads=[b["xhf"], b["pAB"]], writes=[b["xhf"]])
        ctx.step("vector", lambda e, dc=dc: e.tensor_scalar(
            b["xhf"].t[:, dc, :], b["xhf"].t[:, dc, :], cols["g2"].t[:, dc:dc + 1],
            None, op0=ALU.mult),
            reads=[b["xhf"], cols["g2"]], writes=[b["xhf"]])
        ctx.step("vector", lambda e, dc=dc: e.tensor_scalar(
            b["xhf"].t[:, dc, :], b["xhf"].t[:, dc, :], cols["bb2"].t[:, dc:dc + 1],
            None, op0=ALU.add),
            reads=[b["xhf"], cols["bb2"]], writes=[b["xhf"]])
        ctx.step("vector", lambda e, dc=dc: e.tensor_tensor(
            b["x2f"].t[:, dc, :], b["x2f"].t[:, dc, :], b["xhf"].t[:, dc, :], op=ALU.add),
            reads=[b["x2f"], b["xhf"]], writes=[b["x2f"]])
    ctx.dma("scalar", dio["x3_fm"].t.ap()[:, :, col0:col0 + STRIP]
            .rearrange("c p k -> p c k"), b["x2f"].t[:, :, :],
            reads=[b["x2f"]], writes=[dio["x3_fm"]], tag=f"b2bsx3{st}")

"""Full kernel assembly: 7-launch pipeline for nn_mk_hybrid_d1.

kernel(**inputs) -> np.ndarray (32, 2048, 1024) f32.
"""
import numpy as np
import ml_dtypes
from contextlib import ExitStack
import concourse.bass as bass
from concourse.bass_utils import run_bass_kernel_spmd

NCORES = 8
NB = 4
T = 2048
TT = NB * T
NTI = NB * 16
NE = 64
CAP = 256
NSLOT = NE * CAP
BF = ml_dtypes.bfloat16

_cache = {}


def _dram(nc, dio, name, shape, dtype, kind):
    dio[name] = Buf(name, nc.dram_tensor(name, list(shape), dtype, kind=kind))


def build_l1():
    nc = bass.Bass("TRN2", target_bir_lowering=False)
    dio = {}
    _dram(nc, dio, "x_fm", [8, 128, TT], F32, "ExternalInput")
    for n, shp in (("Wq", [1024, 1024]), ("Wk", [1024, 1024]), ("Wv", [1024, 1024]),
                   ("Wg", [1024, 1024]), ("Wo", [1024, 1024]),
                   ("w1", [1024, 2736]), ("w2", [1024, 2736]), ("w3", [2736, 1024]),
                   ("g_attn", [1, 1024]), ("g_ffn", [1, 1024]), ("gw_t", [1024, 64])):
        _dram(nc, dio, n, shp, F32, "ExternalInput")
    _dram(nc, dio, "x1_fm", [8, 128, TT], F32, "ExternalOutput")
    _dram(nc, dio, "r1w_out", [128, NTI, NE], F32, "ExternalOutput")
    _dram(nc, dio, "r1pos_out", [128, NTI, NE], F32, "ExternalOutput")
    _dram(nc, dio, "r1cnt_out", [1, NE], F32, "ExternalOutput")
    _dram(nc, dio, "r1ytm_d", [TT, 1024], BF16, "ExternalOutput")
    _dram(nc, dio, "ofm_d", [8, 128, TT], BF16, "Internal")
    _dram(nc, dio, "xpl_d", [8, 128, TT], F32, "Internal")
    _dram(nc, dio, "xlg2_d", [8, 128, TT], BF16, "Internal")
    _dram(nc, dio, "rows_d", [2, TT], F32, "Internal")
    _dram(nc, dio, "r1logits_d", [128, NTI, NE], F32, "Internal")
    with ExitStack() as stack:
        ctx = Ctx(nc, stack)
        consts = make_gla_consts(ctx)
        long = make_long(ctx)
        emit_l1(ctx, dio, long=long, consts=consts)
        ctx.barrier_all()
        dio["r1x_fm"] = dio["x1_fm"]
        emit_route(ctx, consts, long, dio, "r1", NB)
        ctx.finalize(final_waits=ctx.dma_final_waits())
    return nc


def build_l2():
    nc = bass.Bass("TRN2", target_bir_lowering=False)
    dio = {}
    _dram(nc, dio, "w_own", [128, NTI, NE], F32, "ExternalInput")
    _dram(nc, dio, "pos_own", [128, NTI, NE], F32, "ExternalInput")
    _dram(nc, dio, "cnt_all", [8, NE], F32, "ExternalInput")
    _dram(nc, dio, "coremask", [1, 8], F32, "ExternalInput")
    _dram(nc, dio, "addr_r", [128, NTI, 8], I32, "ExternalOutput")
    _dram(nc, dio, "w_r", [128, NTI, 8], F32, "ExternalOutput")
    with ExitStack() as stack:
        ctx = Ctx(nc, stack)
        long = make_long(ctx)
        emit_l2(ctx, long, dio, NB)
        ctx.finalize(final_waits=ctx.dma_final_waits())
    return nc


def build_l3():
    nc = bass.Bass("TRN2", target_bir_lowering=False)
    dio = {}
    _dram(nc, dio, "disp", [2048, 1024], BF16, "ExternalInput")
    _dram(nc, dio, "wg_t", [8, 1024, 768], F32, "ExternalInput")
    _dram(nc, dio, "wu_t", [8, 1024, 768], F32, "ExternalInput")
    _dram(nc, dio, "wd_t", [8, 768, 1024], F32, "ExternalInput")
    _dram(nc, dio, "y", [2048, 1024], BF16, "ExternalOutput")
    with ExitStack() as stack:
        ctx = Ctx(nc, stack)
        consts = make_gla_consts(ctx)
        long = make_long(ctx)
        emit_l3(ctx, consts, long, dio)
        ctx.finalize(final_waits=ctx.dma_final_waits())
    return nc


def build_l4():
    nc = bass.Bass("TRN2", target_bir_lowering=False)
    dio = {}
    _dram(nc, dio, "x1_fm", [8, 128, TT], F32, "ExternalInput")
    _dram(nc, dio, "yplanes", [8, TT, 1024], BF16, "ExternalInput")
    _dram(nc, dio, "w_r", [128, NTI, 8], F32, "ExternalInput")
    _dram(nc, dio, "in_proj_t", [1024, 3072], F32, "ExternalInput")
    _dram(nc, dio, "in_proj_b", [1, 3072], F32, "ExternalInput")
    _dram(nc, dio, "out_w_t", [1024, 1024], F32, "ExternalInput")
    _dram(nc, dio, "out_b", [1, 1024], F32, "ExternalInput")
    _dram(nc, dio, "lin1_t", [1024, 4096], F32, "ExternalInput")
    _dram(nc, dio, "lin1_b", [1, 4096], F32, "ExternalInput")
    _dram(nc, dio, "lin2_t", [4096, 1024], F32, "ExternalInput")
    _dram(nc, dio, "lin2_b", [1, 1024], F32, "ExternalInput")
    for n in ("ln1_g", "ln1_b", "ln2_g", "ln2_b"):
        _dram(nc, dio, n, [1, 1024], F32, "ExternalInput")
    _dram(nc, dio, "gw_t", [1024, 64], F32, "ExternalInput")
    _dram(nc, dio, "x3_fm", [8, 128, TT], F32, "ExternalOutput")
    _dram(nc, dio, "r3w_out", [128, NTI, NE], F32, "ExternalOutput")
    _dram(nc, dio, "r3pos_out", [128, NTI, NE], F32, "ExternalOutput")
    _dram(nc, dio, "r3cnt_out", [1, NE], F32, "ExternalOutput")
    _dram(nc, dio, "r3ytm_d", [TT, 1024], BF16, "ExternalOutput")
    _dram(nc, dio, "delta_d", [8, 128, TT], BF16, "Internal")
    _dram(nc, dio, "x2_d", [8, 128, TT], F32, "Internal")
    _dram(nc, dio, "y2_d", [8, 128, TT], F32, "Internal")
    _dram(nc, dio, "post_d", [8, 128, TT], BF16, "Internal")
    _dram(nc, dio, "xh_d", [8, 128, TT], F32, "Internal")
    _dram(nc, dio, "r3logits_d", [128, NTI, NE], F32, "Internal")
    with ExitStack() as stack:
        ctx = Ctx(nc, stack)
        consts = make_gla_consts(ctx)
        long = make_long(ctx)
        emit_attnb(ctx, consts, long, dio, NB)
        ctx.barrier_all()
        dio["r3x_fm"] = dio["x3_fm"]
        emit_route(ctx, consts, long, dio, "r3", NB)
        ctx.finalize(final_waits=ctx.dma_final_waits())
    return nc


def build_l7():
    nc = bass.Bass("TRN2", target_bir_lowering=False)
    dio = {}
    _dram(nc, dio, "x3_fm", [8, 128, TT], F32, "ExternalInput")
    _dram(nc, dio, "yplanes", [8, TT, 1024], BF16, "ExternalInput")
    _dram(nc, dio, "w_r", [128, NTI, 8], F32, "ExternalInput")
    _dram(nc, dio, "x4_fm", [8, 128, TT], F32, "ExternalOutput")
    with ExitStack() as stack:
        ctx = Ctx(nc, stack)
        consts = make_gla_consts(ctx)
        with ExitStack() as ph:
            sb = lambda n, s, d: Buf(n, ph.enter_context(nc.sbuf_tensor(n, list(s), d)))
            ps = lambda n, s, d: Buf(n, ph.enter_context(nc.psum_tensor(n, list(s), d)))
            b = {}
            b["wr"] = sb("l7wr", (128, NTI, 8), F32)
            b["pl"] = [sb(f"l7pl{i}", (128, 8, 1024), BF16) for i in range(2)]
            b["acc"] = sb("l7acc", (128, 1024), F32)
            b["dtb"] = [sb(f"l7dtb{i}", (128, 1024), BF16) for i in range(2)]
            b["xc"] = [sb(f"l7xc{i}", (128, 8, 128), F32) for i in range(2)]
            b["pT"] = [ps(f"l7pT{i}", (128, 1024), BF16) for i in range(2)]
            ctx.dma("sync", b["wr"].t[:], dio["w_r"].t.ap()[:, :, :],
                    reads=[dio["w_r"]], writes=[b["wr"]])
            for ti in range(NTI):
                tp = ti % 2
                pl, dtb, pT, xc = b["pl"][tp], b["dtb"][tp], b["pT"][tp], b["xc"][tp]
                for hr in range(2):
                    ctx.dma("gpsimd" if hr else "sync", pl.t[:, hr * 4:(hr + 1) * 4, :],
                            dio["yplanes"].t.ap()[hr * 4:(hr + 1) * 4,
                                                  ti * 128:(ti + 1) * 128, :]
                            .rearrange("r t d -> t r d"),
                            reads=[dio["yplanes"]], writes=[pl], tag=f"l7p{ti}.{hr}")
                ctx.dma("gpsimd" if tp else "sync", xc.t[:, :, :],
                        dio["x3_fm"].t.ap()[:, :, ti * 128:(ti + 1) * 128]
                        .rearrange("c p k -> p c k"),
                        reads=[dio["x3_fm"]], writes=[xc], tag=f"l7x{ti}")
                ctx.step("vector", lambda e, ti=ti, pl=pl: e.tensor_scalar(
                    b["acc"].t[:], pl.t[:, 0, :], b["wr"].t[:, ti, 0:1], None, op0=ALU.mult),
                    reads=[pl, b["wr"]], writes=[b["acc"]])
                for r in range(1, 7):
                    ctx.step("vector", lambda e, ti=ti, r=r, pl=pl: e.scalar_tensor_tensor(
                        b["acc"].t[:], pl.t[:, r, :], b["wr"].t[:, ti, r:r + 1],
                        b["acc"].t[:], op0=ALU.mult, op1=ALU.add),
                        reads=[pl, b["wr"], b["acc"]], writes=[b["acc"]])
                ctx.step("vector", lambda e, ti=ti, pl=pl, dtb=dtb: e.scalar_tensor_tensor(
                    dtb.t[:], pl.t[:, 7, :], b["wr"].t[:, ti, 7:8],
                    b["acc"].t[:], op0=ALU.mult, op1=ALU.add),
                    reads=[pl, b["wr"], b["acc"]], writes=[dtb])
                def mmtr(e, dtb=dtb, pT=pT):
                    last = None
                    for dc in range(8):
                        last = e.transpose(pT.t[:, dc * 128:(dc + 1) * 128],
                                           dtb.t[:, dc * 128:(dc + 1) * 128],
                                           consts["ident128"].t[:, :])
                    return last
                ctx.step("tensor", mmtr, reads=[dtb, consts["ident128"]], writes=[pT],
                         tag=f"l7tr{ti}")
                ctx.step("vector", lambda e, xc=xc, pT=pT: e.tensor_tensor(
                    xc.t[:, :, :], xc.t[:, :, :],
                    pT.t[:].rearrange("p (c k) -> p c k", c=8), op=ALU.add),
                    reads=[xc, pT], writes=[xc], tag=f"l7add{ti}")
                ctx.dma("scalar", dio["x4_fm"].t.ap()[:, :, ti * 128:(ti + 1) * 128]
                        .rearrange("c p k -> p c k"), xc.t[:, :, :],
                        reads=[xc], writes=[dio["x4_fm"]], tag=f"l7s{ti}")
        ctx.finalize(final_waits=ctx.dma_final_waits())
    return nc


def get_programs():
    if "l1" not in _cache:
        _cache["l1"] = build_l1()
        _cache["l2"] = build_l2()
        _cache["l3"] = build_l3()
        _cache["l4"] = build_l4()
        _cache["l7"] = build_l7()
    return _cache


def fm_tile(x):
    nb = x.shape[0]
    return np.ascontiguousarray(
        x.transpose(2, 0, 1).reshape(8, 128, nb * T))


def fm_untile(a, nb):
    return np.ascontiguousarray(a.reshape(1024, nb, T).transpose(1, 2, 0))


TRACE = False
TLOG = []


def _run(nc, in_maps, **kw):
    import time as _t
    t0 = _t.time()
    r = run_bass_kernel_spmd(nc, in_maps, core_ids=list(range(NCORES)),
                             trace=TRACE, **kw)
    if TRACE:
        TLOG.append((r.exec_time_ns, r.mean_exec_time_ns, round(_t.time() - t0, 2)))
    return r


def moe_host_scatter(res_w, res_pos, res_cnt, res_ytm, progs, moe_w):
    cnt_all = np.concatenate([res_cnt[c] for c in range(NCORES)], axis=0)
    l2_maps = []
    for c in range(NCORES):
        cm = np.zeros((1, 8), np.float32)
        cm[0, :c] = 1.0
        l2_maps.append({"w_own": res_w[c], "pos_own": res_pos[c],
                        "cnt_all": cnt_all, "coremask": cm})
    r2 = _run(progs["l2"], l2_maps)
    addr = np.concatenate(
        [r2.results[c]["addr_r"].transpose(1, 0, 2).reshape(TT, 8) for c in range(NCORES)],
        axis=0)
    w_r = [np.ascontiguousarray(r2.results[c]["w_r"]) for c in range(NCORES)]
    y_all = np.concatenate([res_ytm[c] for c in range(NCORES)], axis=0)
    disp = np.zeros((NSLOT + 1, 1024), BF)
    for r in range(8):
        disp[addr[:, r]] = y_all
    l3_maps = []
    for c in range(NCORES):
        l3_maps.append({"disp": np.ascontiguousarray(disp[c * 2048:(c + 1) * 2048]),
                        "wg_t": moe_w["wg_t"][c], "wu_t": moe_w["wu_t"][c],
                        "wd_t": moe_w["wd_t"][c]})
    r3 = _run(progs["l3"], l3_maps)
    y_big = np.zeros((NSLOT + 1, 1024), BF)
    y_big[:NSLOT] = np.concatenate([r3.results[c]["y"] for c in range(NCORES)], axis=0)
    yplanes = []
    for c in range(NCORES):
        pl = np.stack([y_big[addr[c * TT:(c + 1) * TT, r]] for r in range(8)], axis=0)
        yplanes.append(np.ascontiguousarray(pl))
    return yplanes, w_r


def kernel(**inputs):
    progs = get_programs()
    x = np.asarray(inputs["L0_attn_a_x"])
    wg_t = [np.ascontiguousarray(np.asarray(inputs["L0_ffn_w_gate"])[c * 8:(c + 1) * 8].transpose(0, 2, 1))
            for c in range(NCORES)]
    wu_t = [np.ascontiguousarray(np.asarray(inputs["L0_ffn_w_up"])[c * 8:(c + 1) * 8].transpose(0, 2, 1))
            for c in range(NCORES)]
    wd_t = [np.ascontiguousarray(np.asarray(inputs["L0_ffn_w_down"])[c * 8:(c + 1) * 8].transpose(0, 2, 1))
            for c in range(NCORES)]
    moe_w = {"wg_t": wg_t, "wu_t": wu_t, "wd_t": wd_t}
    gw_t = np.ascontiguousarray(np.asarray(inputs["L0_ffn_gate_w"]).T)

    a = inputs
    l1_common = {
        "Wq": np.asarray(a["L0_attn_a_Wq"]), "Wk": np.asarray(a["L0_attn_a_Wk"]),
        "Wv": np.asarray(a["L0_attn_a_Wv"]), "Wg": np.asarray(a["L0_attn_a_Wg"]),
        "Wo": np.asarray(a["L0_attn_a_Wo"]),
        "w1": np.asarray(a["L0_attn_a_w1"]), "w2": np.asarray(a["L0_attn_a_w2"]),
        "w3": np.asarray(a["L0_attn_a_w3"]),
        "g_attn": np.asarray(a["L0_attn_a_g_attn"]).reshape(1, 1024),
        "g_ffn": np.asarray(a["L0_attn_a_g_ffn"]).reshape(1, 1024),
        "gw_t": gw_t,
    }
    l1_maps = []
    for c in range(NCORES):
        m = dict(l1_common)
        m["x_fm"] = fm_tile(x[c * NB:(c + 1) * NB])
        l1_maps.append(m)
    r1 = _run(progs["l1"], l1_maps)
    res_w = [r1.results[c]["r1w_out"] for c in range(NCORES)]
    res_pos = [r1.results[c]["r1pos_out"] for c in range(NCORES)]
    res_cnt = [r1.results[c]["r1cnt_out"] for c in range(NCORES)]
    res_ytm = [r1.results[c]["r1ytm_d"] for c in range(NCORES)]
    x1 = [r1.results[c]["x1_fm"] for c in range(NCORES)]

    yplanes, w_r = moe_host_scatter(res_w, res_pos, res_cnt, res_ytm, progs, moe_w)

    l4_common = {
        "in_proj_t": np.ascontiguousarray(np.asarray(a["L0_attn_b_in_proj_w"]).T),
        "in_proj_b": np.asarray(a["L0_attn_b_in_proj_b"]).reshape(1, 3072),
        "out_w_t": np.ascontiguousarray(np.asarray(a["L0_attn_b_out_w"]).T),
        "out_b": np.asarray(a["L0_attn_b_out_b"]).reshape(1, 1024),
        "lin1_t": np.ascontiguousarray(np.asarray(a["L0_attn_b_lin1_w"]).T),
        "lin1_b": np.asarray(a["L0_attn_b_lin1_b"]).reshape(1, 4096),
        "lin2_t": np.ascontiguousarray(np.asarray(a["L0_attn_b_lin2_w"]).T),
        "lin2_b": np.asarray(a["L0_attn_b_lin2_b"]).reshape(1, 1024),
        "ln1_g": np.asarray(a["L0_attn_b_ln1_g"]).reshape(1, 1024),
        "ln1_b": np.asarray(a["L0_attn_b_ln1_b"]).reshape(1, 1024),
        "ln2_g": np.asarray(a["L0_attn_b_ln2_g"]).reshape(1, 1024),
        "ln2_b": np.asarray(a["L0_attn_b_ln2_b"]).reshape(1, 1024),
        "gw_t": gw_t,
    }
    l4_maps = []
    for c in range(NCORES):
        m = dict(l4_common)
        m["x1_fm"] = x1[c]
        m["yplanes"] = yplanes[c]
        m["w_r"] = w_r[c]
        l4_maps.append(m)
    r4 = _run(progs["l4"], l4_maps)
    res_w3 = [r4.results[c]["r3w_out"] for c in range(NCORES)]
    res_pos3 = [r4.results[c]["r3pos_out"] for c in range(NCORES)]
    res_cnt3 = [r4.results[c]["r3cnt_out"] for c in range(NCORES)]
    res_ytm3 = [r4.results[c]["r3ytm_d"] for c in range(NCORES)]
    x3 = [r4.results[c]["x3_fm"] for c in range(NCORES)]

    yplanes2, w_r2 = moe_host_scatter(res_w3, res_pos3, res_cnt3, res_ytm3, progs, moe_w)

    l7_maps = []
    for c in range(NCORES):
        l7_maps.append({"x3_fm": x3[c], "yplanes": yplanes2[c], "w_r": w_r2[c]})
    r7 = _run(progs["l7"], l7_maps)
    out = np.concatenate([fm_untile(r7.results[c]["x4_fm"], NB) for c in range(NCORES)], axis=0)
    return out
